# version 01
# baseline (speedup 1.0000x reference)
import numpy as np
from contextlib import ExitStack

import concourse.bass as bass
import concourse.mybir as mybir
from concourse.bass_utils import run_bass_kernel_spmd

F32 = mybir.dt.float32
BF16 = mybir.dt.bfloat16
AX = mybir.AxisListType
ALU = mybir.AluOpType
ACT = mybir.ActivationFunctionType

B, L, DM = 4, 4096, 2048
DI = 4096
NH, HD, NST = 32, 128, 128
DFF = 8192
PROJ = 2 * DI + 2 * NH * NST + NH
NCORES = 8
TOK = 2048
PFX = 128
TOKA = TOK + PFX
CS = 128
NCH = TOK // CS
KD = DM // 128
KI = DFF // 128
EPSLN = 1e-5
MAGIC = float(np.float32(3 * (2 ** 22)))
NEG_BIG = -1.0e30

ENG = ("tensor", "vector", "scalar", "gpsimd", "sync")
EK = {"tensor": "pe", "vector": "dve", "scalar": "act", "gpsimd": "pool", "sync": "sp"}
QK = {"sync": "qsp", "scalar": "qact", "gpsimd": "qpool"}


class Buf:
    __slots__ = ("w", "r")

    def __init__(self):
        self.w = None
        self.r = []


class Prog:

    NLANES = 8

    def __init__(self, nc, ctx):
        self.nc = nc
        self.q = {e: [] for e in ENG}
        self.sem = {}
        self.cnt = {}
        self.waited = {e: {} for e in ENG}
        keys = ["pe", "dve", "act", "pool", "sp"]
        for qk in ("qsp", "qact", "qpool"):
            for i in range(self.NLANES):
                keys.append(f"{qk}{i}")
        for k in keys:
            self.sem[k] = ctx.enter_context(nc.semaphore("sb_" + k))
            self.cnt[k] = 0
        self.dma_i = {"sync": 0, "scalar": 0, "gpsimd": 0}

    def fence(self):
        return [self.tok(k) for k in self.cnt]

    def set_fence(self, toks):
        self._fence = list(toks)

    def emit(self, eng, fn, reads=(), writes=(), deps=(), key="auto", n=1):
        if key == "auto":
            key = EK[eng]
        toks = list(deps)
        fence = getattr(self, "_fence", None)
        if fence is not None:
            for b in writes:
                if b.w is None and not b.r:
                    toks.extend(fence)
                    break
        for b in reads:
            toks.append(b.w)
        for b in writes:
            toks.append(b.w)
            toks.extend(b.r)
        w = self.waited[eng]
        need = []
        for t in toks:
            if t is None:
                continue
            k, v = t
            if v > 0 and w.get(k, 0) < v:
                need.append((self.sem[k], v))
                w[k] = v
        tok = None
        if key is not None:
            self.cnt[key] += n
            tok = (key, self.cnt[key])
            sem, val = self.sem[key], n
        else:
            sem, val = None, 0
        self.q[eng].append((need, fn, sem, val))
        for b in reads:
            b.r.append(tok)
        for b in writes:
            b.w = tok
            b.r = []
        return tok

    def dma(self, eng, out, in_, reads=(), writes=(), deps=(), slow=False):
        lane = self.dma_i[eng] % self.NLANES
        self.dma_i[eng] += 1
        key = QK[eng] + str(lane)
        deps = list(deps) + [(key, self.cnt[key])]
        return self.emit(eng, lambda e, o=out, i=in_, s=slow: e.dma_start(
            out=o, in_=i, allow_slow_non_contiguous=s),
            reads=reads, writes=writes, deps=deps, key=key, n=16)

    def tok(self, key):
        return (key, self.cnt[key])

    def finalize(self, block):
        for eng in ENG:
            queue = self.q[eng]
            if not queue:
                continue

            def body(e, queue=queue):
                for need, fn, sem, val in queue:
                    for s, v in need:
                        e.wait_ge(s, v)
                    ins = fn(e)
                    if sem is not None:
                        ins.then_inc(sem, val)
            getattr(block, eng)(body)


def build(debug_taps=(), local_cc=False, stop_after=None):
    import ml_dtypes
    nc = bass.Bass("TRN2", target_bir_lowering=False)

    d_x = nc.dram_tensor("x", [TOKA, DM], F32, kind="ExternalInput").ap()
    d_WinT = nc.dram_tensor("WinT", [DM, PROJ], F32, kind="ExternalInput").ap()
    d_convw = nc.dram_tensor("convw", [128, 32 * 4], F32, kind="ExternalInput").ap()
    d_convb = nc.dram_tensor("convb", [128, 32], F32, kind="ExternalInput").ap()
    d_Alog = nc.dram_tensor("Alog", [NH, 1], F32, kind="ExternalInput").ap()
    d_D = nc.dram_tensor("Dparam", [128, 32], F32, kind="ExternalInput").ap()
    d_dtb = nc.dram_tensor("dtb", [NH, 1], F32, kind="ExternalInput").ap()
    d_WoutT = nc.dram_tensor("WoutT", [DI, DM], F32, kind="ExternalInput").ap()
    d_w1T = nc.dram_tensor("w1T", [DM, DFF], F32, kind="ExternalInput").ap()
    d_w1 = nc.dram_tensor("w1n", [DFF, DM], F32, kind="ExternalInput").ap()
    d_w2T = nc.dram_tensor("w2T", [DFF, DM], F32, kind="ExternalInput").ap()
    d_w2 = nc.dram_tensor("w2n", [DM, DFF], F32, kind="ExternalInput").ap()
    d_b1 = nc.dram_tensor("b1", [128, KI], F32, kind="ExternalInput").ap()
    d_b2 = nc.dram_tensor("b2", [1, DM], F32, kind="ExternalInput").ap()
    d_isodd = nc.dram_tensor("isodd", [1, 1], F32, kind="ExternalInput").ap()

    d_out = nc.dram_tensor("out", [TOK, DM], F32, kind="ExternalOutput").ap()

    d_val = nc.dram_tensor("sp_val", [DI, TOK], BF16, kind="Internal").ap()
    d_B = nc.dram_tensor("sp_B", [DI, TOK], BF16, kind="Internal").ap()
    d_C = nc.dram_tensor("sp_C", [DI, TOK], BF16, kind="Internal").ap()
    d_gate = nc.dram_tensor("sp_gate", [DI, TOK], BF16, kind="Internal").ap()
    d_yg = nc.dram_tensor("sp_yg", [DI, TOK], BF16, kind="Internal").ap()
    d_h = nc.dram_tensor("sp_h", [DFF, TOK], F32, kind="Internal").ap()
    d_s = nc.dram_tensor("sp_s", [TOK, DM], BF16, kind="Internal").ap()
    d_gp = nc.dram_tensor("sp_gp", [TOK, DM], F32, kind="Internal").ap()
    d_row = nc.dram_tensor("sp_row", [1, DM], F32, kind="Internal").ap()
    d_mx1i = nc.dram_tensor("cc_mx1i", [1, 128], F32, kind="Internal").ap()
    d_mx1o = nc.dram_tensor("cc_mx1o", [1, 128], F32, kind="Internal").ap()
    d_mx2i = nc.dram_tensor("cc_mx2i", [1, 128], F32, kind="Internal").ap()
    d_mx2o = nc.dram_tensor("cc_mx2o", [1, 128], F32, kind="Internal").ap()
    d_agi = nc.dram_tensor("cc_agi", [128, DI], F32, kind="Internal").ap()
    d_ago = nc.dram_tensor("cc_ago", [256, DI], F32, kind="Internal").ap()

    eye_f32 = nc.inline_tensor(np.eye(128, dtype=np.float32), "c_eye_f").ap()
    eye_bf = nc.inline_tensor(
        np.eye(128, dtype=np.float32).astype(ml_dtypes.bfloat16), "c_eye_b").ap()
    mk = np.where(np.arange(128)[None, :] >= np.arange(128)[:, None],
                  np.float32(0.0), np.float32(NEG_BIG)).astype(np.float32)
    d_mask = nc.inline_tensor(mk, "c_mask").ap()

    DBG = {
        "dbg_xnf": ([128, KD * TOKA], BF16),
        "dbg_stats": ([128, 2 * (TOKA // 128)], F32),
        "dbg_scal": ([128, 8], F32),
        "dbg_ld": ([32, TOK], F32),
        "dbg_hend": ([128, DI], F32),
        "dbg_sw": ([128, 80], F32),
    }
    d_dbg = {}
    for nm in debug_taps:
        shape, dt = DBG[nm]
        d_dbg[nm] = nc.dram_tensor(nm, shape, dt, kind="ExternalOutput").ap()

    with ExitStack() as ctx:
        P = Prog(nc, ctx)

        def sbuf(name, shape, dt=F32):
            return ctx.enter_context(nc.sbuf_tensor(name, shape, dt))

        t_eye_f = sbuf("eye_f", [128, 128])
        t_eye_b = sbuf("eye_b", [128, 128], BF16)
        t_mask = sbuf("mask", [128, 128])
        t_ones_r = sbuf("ones_r", [1, 128])
        t_cst = sbuf("cst", [128, 8])
        t_stats = sbuf("stats", [128, 2 * (TOKA // 128)])
        t_amax = sbuf("amax", [128, 2])
        t_scal = sbuf("scal", [128, 8])
        t_r1row = sbuf("r1row", [1, 128])
        t_misc = sbuf("misc", [128, 16])

        sA = ctx.enter_context(ExitStack())

        def sbufA(name, shape, dt=F32):
            return sA.enter_context(nc.sbuf_tensor(name, shape, dt))

        t_ones_q = sbufA("ones_q", [32, 128])
        t_ld = sbufA("ld", [32, TOK])
        t_cumf = sbufA("cumf", [32, TOK])
        t_lrow = sbufA("lrow", [1, NH * NCH])
        t_lpre = sbufA("lpre", [1, NH * NCH])
        t_lrun = sbufA("lrun", [1, NH])
        t_h = sbufA("h_state", [128, DI])
        t_hbf = sbufA("h_bf", [128, DI], BF16)
        t_hinit = sbufA("h_init", [128, DI])
        t_Dp = sbufA("Dp", [128, 32])
        t_cw = sbufA("t_convw", [128, 32 * 4])
        t_cb = sbufA("t_convb", [128, 32])
        t_negA = sbufA("negA", [32, 1])
        t_dtb = sbufA("dtbc", [32, 1])
        t_cvh = sbufA("cvh", [128, 32 * 4])

        pb = [ctx.enter_context(nc.psum_tensor(f"pb{i}", [128, 512], F32))
              for i in range(6)]
        pb += [ctx.enter_context(nc.psum_tensor(f"pb{i}", [128, 1024], BF16))
               for i in (6, 7)]
        bB = [Buf() for _ in range(8)]

        bufs = {}
        def gb(name):
            if name not in bufs:
                bufs[name] = Buf()
            return bufs[name]

        P.dma("sync", t_eye_f[:], eye_f32[:], writes=[gb("eye_f")])
        P.dma("sync", t_eye_b[:], eye_bf[:], writes=[gb("eye_b")])
        P.dma("sync", t_mask[:], d_mask[:], writes=[gb("mask")])
        P.dma("sync", t_cw[:], d_convw[:], writes=[gb("cw")])
        P.dma("sync", t_cb[:], d_convb[:], writes=[gb("cb")])
        P.dma("sync", t_Dp[:], d_D[:], writes=[gb("Dp")])
        P.dma("sync", t_negA[:], d_Alog[:], writes=[gb("negA")])
        P.dma("sync", t_dtb[:], d_dtb[:], writes=[gb("dtb")])
        P.emit("vector", lambda e: e.memset(t_scal[:], 0.0), writes=[gb("scal")])
        P.dma("sync", t_scal[0:1, 4:5], d_isodd[:], writes=[gb("scal")])

        P.emit("vector", lambda e: e.memset(t_ones_r[:], 1.0), writes=[gb("ones_r")])
        P.emit("vector", lambda e: e.memset(t_ones_q[:], 1.0), writes=[gb("ones_q")])
        P.emit("vector", lambda e: e.memset(t_cst[:], 0.0), writes=[gb("cst")])
        P.emit("vector", lambda e: e.memset(t_cst[:, 0:1], -1.0),
               writes=[gb("cst")])
        P.emit("vector", lambda e: e.memset(t_cst[:, 1:2], 1.0), writes=[gb("cst")])
        P.emit("vector", lambda e: e.memset(t_cst[:, 2:3], EPSLN), writes=[gb("cst")])
        P.emit("vector", lambda e: e.memset(t_amax[:], 0.0), writes=[gb("amax")])
        P.emit("vector", lambda e: e.memset(t_misc[:], 0.0), writes=[gb("m0"), gb("m1"), gb("m2"), gb("m3"), gb("m4")])
        P.emit("vector", lambda e: e.memset(t_lrun[:], 0.0), writes=[gb("lrun")])
        P.emit("vector", lambda e: e.memset(t_h[:], 0.0), writes=[gb("h")])
        P.emit("scalar", lambda e: e.activation(t_negA[:], t_negA[:], ACT.Exp),
               reads=[], writes=[gb("negA")])
        P.emit("vector", lambda e: e.tensor_scalar_mul(t_negA[:], t_negA[:], -1.0),
               writes=[gb("negA")])

        s1 = ctx.enter_context(ExitStack())

        def sbuf1(name, shape, dt=F32):
            return s1.enter_context(nc.sbuf_tensor(name, shape, dt))

        NLC = TOKA // 128
        t_xc = [sbuf1(f"ln_x{i}", [128, DM]) for i in range(2)]
        t_bn6 = sbuf1("ln_bn6", [128, 24])
        t_mv = sbuf1("ln_mv", [128, 2])
        t_xn = sbuf1("ln_xn", [128, DM])
        t_xnb = sbuf1("ln_xnb", [128, DM], BF16)
        t_cmax = sbuf1("ln_cmax", [128, 1])
        t_xnf = sbuf1("xnf", [128, KD * TOKA], BF16)

        bxc = [Buf(), Buf()]
        bbn6, bmv, bxn, bxnb, bcmax = Buf(), Buf(), Buf(), Buf(), Buf()
        bstats, bamax = gb("stats"), gb("amax")
        bxnf = [Buf() for _ in range(NLC)]

        for i in range(NLC):
            xb, bx = t_xc[i % 2], bxc[i % 2]
            P.dma("sync", xb[:], d_x[i * 128:(i + 1) * 128, :], writes=[bx])
            for s4 in range(4):
                P.emit("vector", lambda e, s4=s4, xb=xb: e.bn_stats(
                    t_bn6[:, s4 * 6:(s4 + 1) * 6], xb[:, s4 * 512:(s4 + 1) * 512]),
                    reads=[bx], writes=[bbn6])
            P.emit("vector", lambda e: e.bn_aggr(t_mv[:], t_bn6[:]),
                   reads=[bbn6], writes=[bmv])
            P.emit("vector", lambda e: e.tensor_scalar_add(
                t_misc[:, 0:1], t_mv[:, 1:2], t_cst[:, 2:3]),
                reads=[bmv, gb("cst")], writes=[gb("m0")])
            P.emit("scalar", lambda e: e.sqrt(t_misc[:, 1:2], t_misc[:, 0:1]),
                   reads=[gb("m0")], writes=[gb("m1")])
            P.emit("vector", lambda e, i=i: e.reciprocal(
                t_stats[:, 2 * i + 1:2 * i + 2], t_misc[:, 1:2]),
                reads=[gb("m1")], writes=[bstats])
            P.emit("vector", lambda e, i=i: e.tensor_scalar(
                t_stats[:, 2 * i:2 * i + 1], t_mv[:, 0:1],
                t_stats[:, 2 * i + 1:2 * i + 2], -1.0, ALU.mult, ALU.mult),
                reads=[bmv, bstats], writes=[bstats])
            P.emit("scalar", lambda e, i=i, xb=xb: e.activation(
                t_xn[:], xb[:], ACT.Identity,
                bias=t_stats[:, 2 * i:2 * i + 1],
                scale=t_stats[:, 2 * i + 1:2 * i + 2]),
                reads=[bx, bstats], writes=[bxn])
            P.emit("vector", lambda e: e.reduce_max(
                t_cmax[:], t_xn[:], AX.X, apply_absolute_value=True),
                reads=[bxn], writes=[bcmax])
            P.emit("vector", lambda e: e.tensor_max(
                t_amax[:, 0:1], t_amax[:, 0:1], t_cmax[:]),
                reads=[bcmax], writes=[bamax])
            P.emit("scalar", lambda e: e.copy(out=t_xnb[:], in_=t_xn[:]),
                   reads=[bxn], writes=[bxnb])
            for kc in range(KD):
                bank = 6 + kc % 2
                P.emit("tensor", lambda e, kc=kc, bank=bank: e.transpose(
                    pb[bank][:, 0:128], t_xnb[:, kc * 128:(kc + 1) * 128], t_eye_b[:]),
                    reads=[bxnb, gb("eye_b")], writes=[bB[bank]])
                if kc % 2 == 0:
                    P.emit("vector", lambda e, kc=kc, i=i, bank=bank: e.tensor_copy(
                        out=t_xnf[:, kc * TOKA + i * 128: kc * TOKA + (i + 1) * 128],
                        in_=pb[bank][:, 0:128]),
                        reads=[bB[bank]], writes=[bxnf[i]])
                else:
                    P.emit("scalar", lambda e, kc=kc, i=i, bank=bank: e.copy(
                        out=t_xnf[:, kc * TOKA + i * 128: kc * TOKA + (i + 1) * 128],
                        in_=pb[bank][:, 0:128]),
                        reads=[bB[bank]], writes=[bxnf[i]])

        P.dma("gpsimd", d_mx1i[:], t_amax[:, 0:1], reads=[bamax], writes=[gb("dmx1i")])
        if local_cc:
            P.dma("gpsimd", d_mx1o[:], d_mx1i[:], reads=[gb("dmx1i")],
                  writes=[gb("dmx1o")])
        else:
            P.emit("gpsimd", lambda e: e.collective_compute(
                "AllReduce", ALU.max, replica_groups=[list(range(NCORES))],
                ins=[d_mx1i], outs=[d_mx1o]),
                reads=[gb("dmx1i")], writes=[gb("dmx1o")])
        P.dma("sync", t_r1row[:], d_mx1o[:], reads=[gb("dmx1o")], writes=[gb("r1row")])
        P.emit("vector", lambda e: e.reduce_max(
            t_misc[0:1, 2:3], t_r1row[:], AX.X), reads=[gb("r1row")], writes=[gb("m2")])
        P.emit("vector", lambda e: e.tensor_scalar(
            t_misc[0:1, 3:4], t_misc[0:1, 2:3], 1.0 / 127.0, 1e-8, ALU.mult, ALU.max),
            reads=[gb("m2")], writes=[gb("m3")])
        P.emit("vector", lambda e: e.reciprocal(
            t_misc[0:1, 4:5], t_misc[0:1, 3:4]), reads=[gb("m3")], writes=[gb("m4")])
        P.emit("tensor", lambda e: e.matmul(
            pb[5][:, 0:2], t_ones_r[:, :], t_misc[0:1, 3:5], start=True, stop=True),
            reads=[gb("m3"), gb("m4"), gb("ones_r")], writes=[bB[5]])
        bscal = gb("scal")
        P.emit("vector", lambda e: e.tensor_copy(out=t_scal[:, 1:2], in_=pb[5][:, 0:1]),
               reads=[bB[5]], writes=[bscal])
        P.emit("vector", lambda e: e.tensor_copy(out=t_scal[:, 0:1], in_=pb[5][:, 1:2]),
               reads=[bB[5]], writes=[bscal])
        P.emit("tensor", lambda e: e.matmul(
            pb[5][:, 4:5], t_ones_r[:, :], t_scal[0:1, 4:5], start=True, stop=True),
            reads=[gb("scal"), gb("ones_r")], writes=[bB[5]])
        P.emit("vector", lambda e: e.tensor_copy(out=t_scal[:, 4:5], in_=pb[5][:, 4:5]),
               reads=[bB[5]], writes=[bscal])

        if "dbg_stats" in d_dbg:
            P.dma("sync", d_dbg["dbg_stats"][:], t_stats[:], reads=[bstats])
        if "dbg_xnf" in d_dbg:
            P.dma("sync", d_dbg["dbg_xnf"][:], t_xnf[:], reads=bxnf)
        if "dbg_scal" in d_dbg:
            P.dma("sync", d_dbg["dbg_scal"][:], t_scal[:], reads=[bscal])
        if stop_after == "ln":
            _final_wait(P)
            P.finalize(ctx.enter_context(nc.Block()))
            return nc, d_dbg

        t_wp = [sbuf1(f"ip_wp{i}", [128, KD * 512], BF16) for i in range(2)]
        t_cv = sbuf1("ip_cv", [128, 512])
        t_sp = [sbuf1(f"ip_sp{i}", [128, 512], BF16) for i in range(4)]
        t_dt1 = sbuf1("ip_dt1", [32, 512])
        t_dt2 = sbuf1("ip_dt2", [32, 512])

        bwp = [Buf(), Buf()]
        bcv, bdt1, bdt2 = Buf(), Buf(), Buf()
        bsp = [Buf() for _ in range(4)]
        bcvh = [Buf() for _ in range(32)]
        bld = gb("ld")
        bspill = {"val": Buf(), "B": Buf(), "C": Buf(), "gate": Buf()}

        NPAN = PROJ // 512 + 1
        for fp in range(NPAN):
            f0 = fp * 512
            W = 512 if fp < 32 else 32
            pbuf, bwpc = t_wp[fp % 2], bwp[fp % 2]
            P.dma("gpsimd", pbuf[:, 0:KD * W].rearrange("p (k f) -> p k f", k=KD),
                  d_WinT[:, f0:f0 + W].rearrange("(k p) f -> p k f", p=128),
                  writes=[bwpc])
            is_val = f0 < DI
            is_gate = DI <= f0 < 2 * DI
            is_dt = fp == 32
            groups = ([(-1, 0, 128)] if is_val else []) + \
                     [(g, PFX + g * 512, 512) for g in range(4)]
            for (g, c0, Wt) in groups:
                nfc = 4 if not is_dt else 1
                for fc in range(nfc):
                    fg = f0 + fc * 128
                    bank = 2 + ((fp * 24 + (g + 1) * 4 + fc) % 2)
                    Mt = 128 if not is_dt else 32
                    lnreads = [bxnf[ci] for ci in
                               range(c0 // 128, (c0 + Wt + 127) // 128)]
                    for k in range(KD):
                        P.emit("tensor",
                               lambda e, bank=bank, Mt=Mt, Wt=Wt, k=k, fc=fc, W=W,
                               c0=c0, pbuf=pbuf: e.matmul(
                                   pb[bank][0:Mt, 0:Wt],
                                   pbuf[:, k * W + fc * 128: k * W + fc * 128 + Mt],
                                   t_xnf[:, k * TOKA + c0: k * TOKA + c0 + Wt],
                                   start=(k == 0), stop=(k == KD - 1)),
                               reads=([bwpc] + lnreads) if k == 0
                               else ([bwpc] if k == KD - 1 else ()),
                               writes=[bB[bank]] if k == KD - 1 else (),
                               deps=[bB[bank].w] + [t for t in bB[bank].r]
                               if k == 0 else (),
                               key="pe" if k == KD - 1 else None)
                    if is_dt:
                        P.emit("scalar", lambda e, bank=bank, Wt=Wt: e.activation(
                            t_dt1[:, 0:Wt], pb[bank][0:32, 0:Wt], ACT.Abs,
                            bias=t_dtb[:, 0:1]),
                            reads=[bB[bank], gb("dtb")], writes=[bdt1])
                        P.emit("scalar", lambda e, Wt=Wt: e.activation(
                            t_dt1[:, 0:Wt], t_dt1[:, 0:Wt], ACT.Exp,
                            scale=t_cst[0:32, 0:1]), reads=[gb("cst")], writes=[bdt1])
                        P.emit("scalar", lambda e, Wt=Wt: e.activation(
                            t_dt1[:, 0:Wt], t_dt1[:, 0:Wt], ACT.Ln,
                            bias=t_cst[0:32, 1:2]), reads=[gb("cst")], writes=[bdt1])
                        P.emit("scalar", lambda e, bank=bank, Wt=Wt: e.activation(
                            t_dt2[:, 0:Wt], pb[bank][0:32, 0:Wt], ACT.Relu,
                            bias=t_dtb[:, 0:1]),
                            reads=[bB[bank], gb("dtb")], writes=[bdt2])
                        P.emit("vector", lambda e, Wt=Wt: e.tensor_add(
                            t_dt1[:, 0:Wt], t_dt1[:, 0:Wt], t_dt2[:, 0:Wt]),
                            reads=[bdt2, bB[bank]], writes=[bdt1])
                        P.emit("vector", lambda e, c0=c0, Wt=Wt: e.tensor_scalar(
                            t_ld[:, c0 - PFX: c0 - PFX + Wt], t_dt1[:, 0:Wt],
                            t_negA[:, 0:1], None, ALU.mult),
                            reads=[bdt1, gb("negA")], writes=[bld])
                        continue
                    if is_val:
                        cic = fg // 128
                        if g == -1:
                            P.emit("vector", lambda e, bank=bank, cic=cic: e.tensor_copy(
                                out=t_cvh[:, cic * 4: cic * 4 + 3],
                                in_=pb[bank][:, 125:128]),
                                reads=[bB[bank]], writes=[bcvh[cic]])
                            continue
                        w4 = [t_cw[:, cic * 4 + j: cic * 4 + j + 1] for j in range(4)]
                        hist = t_cvh[:, cic * 4: cic * 4 + 3]
                        P.emit("vector", lambda e, bank=bank, w=w4[0]: e.tensor_scalar(
                            t_cv[:, 3:512], pb[bank][:, 0:509], w, None, ALU.mult),
                            reads=[bB[bank], gb("cw")], writes=[bcv])
                        for j, sl in ((1, (1, 510)), (2, (2, 511)), (3, (3, 512))):
                            P.emit("vector",
                                   lambda e, bank=bank, w=w4[j], sl=sl:
                                   e.scalar_tensor_tensor(
                                       t_cv[:, 3:512], pb[bank][:, sl[0]:sl[1]], w,
                                       t_cv[:, 3:512], ALU.mult, ALU.add),
                                   reads=[bB[bank]], writes=[bcv])
                        P.emit("vector", lambda e, hist=hist, w=w4[0]: e.tensor_scalar(
                            t_cv[:, 0:3], hist, w, None, ALU.mult),
                            reads=[bcvh[cic]], writes=[bcv])
                        P.emit("vector", lambda e, hist=hist, w=w4[1]:
                               e.scalar_tensor_tensor(
                                   t_cv[:, 0:2], hist[:, 1:3], w, t_cv[:, 0:2],
                                   ALU.mult, ALU.add), writes=[bcv])
                        P.emit("vector", lambda e, bank=bank, w=w4[1]:
                               e.scalar_tensor_tensor(
                                   t_cv[:, 2:3], pb[bank][:, 0:1], w, t_cv[:, 2:3],
                                   ALU.mult, ALU.add), reads=[bB[bank]], writes=[bcv])
                        P.emit("vector", lambda e, hist=hist, w=w4[2]:
                               e.scalar_tensor_tensor(
                                   t_cv[:, 0:1], hist[:, 2:3], w, t_cv[:, 0:1],
                                   ALU.mult, ALU.add), writes=[bcv])
                        P.emit("vector", lambda e, bank=bank, w=w4[2]:
                               e.scalar_tensor_tensor(
                                   t_cv[:, 1:3], pb[bank][:, 0:2], w, t_cv[:, 1:3],
                                   ALU.mult, ALU.add), reads=[bB[bank]], writes=[bcv])
                        P.emit("vector", lambda e, bank=bank, w=w4[3]:
                               e.scalar_tensor_tensor(
                                   t_cv[:, 0:3], pb[bank][:, 0:3], w, t_cv[:, 0:3],
                                   ALU.mult, ALU.add), reads=[bB[bank]], writes=[bcv])
                        P.emit("vector", lambda e, bank=bank, cic=cic: e.tensor_copy(
                            out=t_cvh[:, cic * 4: cic * 4 + 3],
                            in_=pb[bank][:, 509:512]),
                            reads=[bB[bank]], writes=[bcvh[cic]])
                        sps = (fp * 20 + g * 4 + fc) % 4
                        P.emit("scalar", lambda e, sps=sps, cic=cic: e.activation(
                            t_sp[sps][:], t_cv[:], ACT.Silu,
                            bias=t_cb[:, cic: cic + 1]),
                            reads=[bcv, gb("cb")], writes=[bsp[sps]])
                        P.dma("sync",
                              d_val.rearrange("(c p) t -> p c t", p=128)[
                                  :, cic, c0 - PFX: c0 - PFX + 512],
                              t_sp[sps][:], reads=[bsp[sps]], writes=[bspill["val"]])
                    else:
                        sps = (fp * 20 + g * 4 + fc) % 4
                        if is_gate:
                            dst, cic, nmk = d_gate, (fg - DI) // 128, "gate"
                            P.emit("scalar", lambda e, sps=sps, bank=bank: e.activation(
                                t_sp[sps][:], pb[bank][:], ACT.Silu),
                                reads=[bB[bank]], writes=[bsp[sps]])
                        else:
                            isB = fg < 3 * DI
                            dst = d_B if isB else d_C
                            cic = (fg - 2 * DI) // 128 if isB else (fg - 3 * DI) // 128
                            nmk = "B" if isB else "C"
                            P.emit("scalar", lambda e, sps=sps, bank=bank: e.copy(
                                out=t_sp[sps][:], in_=pb[bank][:]),
                                reads=[bB[bank]], writes=[bsp[sps]])
                        P.dma("sync",
                              dst.rearrange("(c p) t -> p c t", p=128)[
                                  :, cic, c0 - PFX: c0 - PFX + 512],
                              t_sp[sps][:], reads=[bsp[sps]], writes=[bspill[nmk]])

        if "dbg_ld" in d_dbg:
            P.dma("sync", d_dbg["dbg_ld"][:], t_ld[:], reads=[bld])
        if stop_after == "inproj":
            _final_wait(P)
            P.finalize(ctx.enter_context(nc.Block()))
            return nc, d_dbg

        s1.close()
        P.set_fence(P.fence())
        s2 = ctx.enter_context(ExitStack())

        def sbuf2(name, shape, dt=F32):
            return s2.enter_context(nc.sbuf_tensor(name, shape, dt))

        t_sv = sbuf2("sc_v", [128, 32 * 256], BF16)
        t_sB = sbuf2("sc_B", [128, 32 * 256], BF16)
        t_sC = sbuf2("sc_C", [128, 32 * 256], BF16)
        t_sG = sbuf2("sc_G", [128, 32 * 256], BF16)
        bsv, bsB, bsC, bsG = Buf(), Buf(), Buf(), Buf()
        t_yg = sbuf2("sc_yg", [128, DI], BF16)
        byg = Buf()
        t_cumt = sbuf2("sc_cumt", [128, 32])
        t_cflat = sbuf2("sc_cflat", [1, DI])
        bcflat = Buf()
        t_wtm = sbuf2("sc_wtm", [128, 32])
        t_wfm = sbuf2("sc_wfm", [32, 128])
        t_Ebc = sbuf2("sc_Ebc", [128, 32])
        t_lr0 = sbuf2("sc_lr0", [1, 32])
        t_er0 = sbuf2("sc_er0", [1, 32])
        t_d2 = sbuf2("sc_d2", [128, 128])
        t_ed2 = sbuf2("sc_ed2", [128, 128])
        t_eec = sbuf2("sc_eec", [128, 128])
        t_mt = sbuf2("sc_mt", [128, 128], BF16)
        t_ce = sbuf2("sc_ce", [128, 128], BF16)
        t_vtm = sbuf2("sc_vtm", [128, 128], BF16)
        t_btm = sbuf2("sc_btm", [128, 128], BF16)
        t_tmp = sbuf2("sc_tmp", [128, 128])
        bcumt, bwtm, bwfm, bEbc = Buf(), Buf(), Buf(), Buf()
        blr0, ber0 = Buf(), Buf()
        bd2, bed2, beec, bmt, bce, bvtm, bbtm, btmp = (Buf() for _ in range(8))
        bcumf, blrow, blpre, blrun = gb("cumf"), gb("lrow"), gb("lpre"), gb("lrun")
        bh, bhbf = gb("h"), gb("hbf")
        bygsp = Buf()

        def chunk_cols(c):
            off = (c % 2) * 128
            return off

        for c in range(NCH):
            if c % 2 == 0:
                cp0 = c * CS
                for (dst, bufv, src) in ((t_sv, bsv, d_val), (t_sB, bsB, d_B),
                                         (t_sC, bsC, d_C), (t_sG, bsG, d_gate)):
                    P.dma("sync", dst[:],
                          src.rearrange("(ci p) t -> p ci t", p=128)[
                              :, :, cp0:cp0 + 256],
                          reads=[bspill["val" if src is d_val else
                                        "B" if src is d_B else
                                        "C" if src is d_C else "gate"]],
                          writes=[bufv])
            co = chunk_cols(c)

            P.emit("vector", lambda e, c=c: e.tensor_tensor_scan(
                t_cumf[:, c * CS:(c + 1) * CS], t_ones_q[:],
                t_ld[:, c * CS:(c + 1) * CS], 0.0, ALU.mult, ALU.add),
                reads=[bld, gb("ones_q")], writes=[bcumf])
            P.dma("sync", t_cflat[:], t_cumf[:, c * CS:(c + 1) * CS],
                  reads=[bcumf], writes=[bcflat])
            P.emit("tensor", lambda e, c=c: e.transpose(
                pb[4][:, 0:32], t_cumf[:, c * CS:(c + 1) * CS],
                t_eye_f[0:32, 0:32]), reads=[bcumf, gb("eye_f")], writes=[bB[4]])
            P.emit("vector", lambda e: e.tensor_copy(out=t_cumt[:], in_=pb[4][:, 0:32]),
                   reads=[bB[4]], writes=[bcumt])
            P.emit("vector", lambda e: e.tensor_copy(
                out=t_lr0[:].rearrange("one (a u) -> one a u", u=1),
                in_=t_cflat[:].rearrange("one (a b) -> one a b", b=128)[:, :, 127:128]),
                reads=[bcflat], writes=[blr0])
            P.emit("vector", lambda e, c=c: e.tensor_copy(
                out=t_lrow[:, c * NH:(c + 1) * NH], in_=t_lr0[:]),
                reads=[blr0], writes=[blrow])
            P.emit("vector", lambda e, c=c: e.tensor_copy(
                out=t_lpre[:, c * NH:(c + 1) * NH], in_=t_lrun[:]),
                reads=[blrun], writes=[blpre])
            P.emit("vector", lambda e: e.tensor_add(t_lrun[:], t_lrun[:], t_lr0[:]),
                   reads=[blr0], writes=[blrun])
            P.emit("scalar", lambda e, c=c: e.activation(
                t_wfm[:], t_cumf[:, c * CS:(c + 1) * CS], ACT.Exp,
                bias=t_cumf[:, c * CS + CS - 1: c * CS + CS],
                scale=t_cst[0:32, 0:1]),
                reads=[bcumf, gb("cst")], writes=[bwfm])
            P.emit("tensor", lambda e: e.transpose(
                pb[5][:, 0:32], t_wfm[:], t_eye_f[0:32, 0:32]),
                reads=[bwfm, gb("eye_f")], writes=[bB[5]])
            P.emit("vector", lambda e: e.tensor_copy(out=t_wtm[:], in_=pb[5][:, 0:32]),
                   reads=[bB[5]], writes=[bwtm])
            P.emit("scalar", lambda e: e.activation(t_er0[:], t_lr0[:], ACT.Exp),
                   reads=[blr0], writes=[ber0])
            P.emit("tensor", lambda e: e.matmul(
                pb[4][:, 64:96], t_ones_r[:, :], t_er0[:], start=True, stop=True),
                reads=[ber0, gb("ones_r")], writes=[bB[4]])
            P.emit("vector", lambda e: e.tensor_copy(out=t_Ebc[:], in_=pb[4][:, 64:96]),
                   reads=[bB[4]], writes=[bEbc])
            P.emit("scalar", lambda e: e.copy(out=t_hbf[:], in_=t_h[:]),
                   reads=[bh], writes=[bhbf])

            for hh in range(NH):
                a_v = t_sv[:, hh * 256 + co: hh * 256 + co + 128]
                a_B = t_sB[:, hh * 256 + co: hh * 256 + co + 128]
                a_C = t_sC[:, hh * 256 + co: hh * 256 + co + 128]
                a_G = t_sG[:, hh * 256 + co: hh * 256 + co + 128]
                bkY, bkS, bkCB, bkE, bkT1, bkT2 = 0, 1, 2, 3, 6, 7
                P.emit("tensor", lambda e, hh=hh: e.matmul(
                    pb[bkE][:, 0:128], t_ones_r[:, :],
                    t_cflat[0:1, hh * 128:(hh + 1) * 128], start=True, stop=True),
                    reads=[bcflat, gb("ones_r")], writes=[bB[bkE]])
                P.emit("tensor", lambda e, a_B=a_B, a_C=a_C: e.matmul(
                    pb[bkCB][:, 0:128], a_B, a_C, start=True, stop=True),
                    reads=[bsB, bsC], writes=[bB[bkCB]])
                P.emit("tensor", lambda e, a_v=a_v: e.transpose(
                    pb[bkT1][:, 0:128], a_v, t_eye_b[:]),
                    reads=[bsv, gb("eye_b")], writes=[bB[bkT1]])
                P.emit("tensor", lambda e, a_B=a_B: e.transpose(
                    pb[bkT2][:, 0:128], a_B, t_eye_b[:]),
                    reads=[bsB, gb("eye_b")], writes=[bB[bkT2]])
                P.emit("vector", lambda e, hh=hh: e.scalar_tensor_tensor(
                    t_d2[:], pb[bkE][:, 0:128], t_cumt[:, hh:hh + 1], t_mask[:],
                    ALU.subtract, ALU.add),
                    reads=[bB[bkE], bcumt, gb("mask")], writes=[bd2])
                P.emit("scalar", lambda e: e.activation(t_ed2[:], t_d2[:], ACT.Exp),
                       reads=[bd2], writes=[bed2])
                P.emit("scalar", lambda e: e.activation(
                    t_eec[:], pb[bkE][:, 0:128], ACT.Exp),
                    reads=[bB[bkE]], writes=[beec])
                P.emit("vector", lambda e: e.tensor_mul(
                    t_mt[:], pb[bkCB][:, 0:128], t_ed2[:]),
                    reads=[bB[bkCB], bed2], writes=[bmt])
                P.emit("vector", lambda e, a_C=a_C: e.tensor_mul(
                    t_ce[:], a_C, t_eec[:]), reads=[bsC, beec], writes=[bce])
                P.emit("scalar", lambda e: e.copy(out=t_vtm[:], in_=pb[bkT1][:, 0:128]),
                       reads=[bB[bkT1]], writes=[bvtm])
                P.emit("vector", lambda e, hh=hh: e.tensor_scalar(
                    t_btm[:], pb[bkT2][:, 0:128], t_wtm[:, hh:hh + 1], None, ALU.mult),
                    reads=[bB[bkT2], bwtm], writes=[bbtm])
                P.emit("tensor", lambda e, hh=hh: e.matmul(
                    pb[bkY][:, 0:128], t_hbf[:, hh * 128:(hh + 1) * 128], t_ce[:],
                    start=True, stop=False),
                    reads=[bhbf, bce], writes=[], deps=[bB[bkY].w] + bB[bkY].r,
                    key=None)
                P.emit("tensor", lambda e: e.matmul(
                    pb[bkY][:, 0:128], t_vtm[:], t_mt[:], start=False, stop=True),
                    reads=[bvtm, bmt], writes=[bB[bkY]])
                P.emit("tensor", lambda e: e.matmul(
                    pb[bkS][:, 0:128], t_btm[:], t_vtm[:], start=True, stop=True),
                    reads=[bbtm, bvtm], writes=[bB[bkS]])
                P.emit("vector", lambda e, hh=hh, a_v=a_v: e.scalar_tensor_tensor(
                    t_tmp[:], a_v, t_Dp[:, hh:hh + 1], pb[bkY][:, 0:128],
                    ALU.mult, ALU.add),
                    reads=[bsv, gb("Dp"), bB[bkY]], writes=[btmp])
                P.emit("vector", lambda e, hh=hh, a_G=a_G: e.tensor_mul(
                    t_yg[:, hh * 128:(hh + 1) * 128], t_tmp[:], a_G),
                    reads=[btmp, bsG], writes=[byg])
                P.emit("vector", lambda e, hh=hh: e.scalar_tensor_tensor(
                    t_h[:, hh * 128:(hh + 1) * 128],
                    t_h[:, hh * 128:(hh + 1) * 128], t_Ebc[:, hh:hh + 1],
                    pb[bkS][:, 0:128], ALU.mult, ALU.add),
                    reads=[bEbc, bB[bkS]], writes=[bh])
            P.dma("sync", d_yg.rearrange("(ci p) t -> p ci t", p=128)[
                :, :, c * CS:(c + 1) * CS], t_yg[:], reads=[byg], writes=[bygsp])

        P.dma("gpsimd", d_agi[:], t_h[:], reads=[bh], writes=[gb("dagi")])
        if local_cc:
            P.dma("gpsimd", d_ago[0:128, :], d_agi[:], reads=[gb("dagi")],
                  writes=[gb("dago")])
        else:
            P.emit("gpsimd", lambda e: e.collective_compute(
                "AllGather", ALU.bypass,
                replica_groups=[[0, 1], [2, 3], [4, 5], [6, 7]],
                ins=[d_agi], outs=[d_ago]),
                reads=[gb("dagi")], writes=[gb("dago")])

        if "dbg_hend" in d_dbg:
            P.dma("sync", d_dbg["dbg_hend"][:], t_h[:], reads=[bh])
        if stop_after == "scan":
            _final_wait(P)
            P.finalize(ctx.enter_context(nc.Block()))
            return nc, d_dbg

        s2.close()
        P.set_fence(P.fence())
        s3 = ctx.enter_context(ExitStack())

        def sbuf3(name, shape, dt=F32):
            return s3.enter_context(nc.sbuf_tensor(name, shape, dt))

        t_kC = sbuf3("co_C", [128, DI], BF16)
        t_kG = sbuf3("co_G", [128, DI], BF16)
        t_kY = sbuf3("co_Y", [128, DI], BF16)
        t_cflat2 = sbuf3("co_cflat", [1, DI])
        t_pr0 = sbuf3("co_pr0", [1, 32])
        t_Pb = sbuf3("co_Pb", [128, 32])
        t_eec2 = sbuf3("co_eec", [128, 128])
        t_ce2 = sbuf3("co_ce", [128, 128], BF16)
        t_hP = sbuf3("co_hP", [128, 128], BF16)
        t_ygc = sbuf3("co_ygc", [128, DI], BF16)
        bkC, bkG, bkY2 = Buf(), Buf(), Buf()
        bcflat2, bpr0, bPb = Buf(), Buf(), Buf()
        beec2, bce2, bhP, bygc = Buf(), Buf(), Buf(), Buf()
        bhinit = gb("hinit")
        bygsp2 = Buf()

        P.dma("sync", t_hinit[:], d_ago[0:128, :], reads=[gb("dago")],
              writes=[bhinit])
        P.emit("vector", lambda e: e.tensor_scalar(
            t_hinit[:], t_hinit[:], t_scal[:, 4:5], None, ALU.mult),
            reads=[gb("scal")], writes=[bhinit])

        for c in range(NCH):
            ccols = (c * CS, (c + 1) * CS)
            P.dma("sync", t_kC[:],
                  d_C.rearrange("(ci p) t -> p ci t", p=128)[:, :, ccols[0]:ccols[1]],
                  reads=[bspill["C"]], writes=[bkC])
            P.dma("scalar", t_kG[:],
                  d_gate.rearrange("(ci p) t -> p ci t", p=128)[:, :, ccols[0]:ccols[1]],
                  reads=[bspill["gate"]], writes=[bkG])
            P.dma("sync", t_kY[:],
                  d_yg.rearrange("(ci p) t -> p ci t", p=128)[:, :, ccols[0]:ccols[1]],
                  reads=[bygsp], writes=[bkY2])
            P.dma("scalar", t_cflat2[:], t_cumf[:, ccols[0]:ccols[1]],
                  reads=[bcumf], writes=[bcflat2])
            P.emit("scalar", lambda e, c=c: e.activation(
                t_pr0[:], t_lpre[:, c * NH:(c + 1) * NH], ACT.Exp),
                reads=[blpre], writes=[bpr0])
            P.emit("tensor", lambda e: e.matmul(
                pb[5][:, 0:32], t_ones_r[:, :], t_pr0[:], start=True, stop=True),
                reads=[bpr0, gb("ones_r")], writes=[bB[5]])
            P.emit("vector", lambda e: e.tensor_copy(out=t_Pb[:], in_=pb[5][:, 0:32]),
                   reads=[bB[5]], writes=[bPb])
            for hh in range(NH):
                P.emit("tensor", lambda e, hh=hh: e.matmul(
                    pb[3][:, 0:128], t_ones_r[:, :],
                    t_cflat2[0:1, hh * 128:(hh + 1) * 128], start=True, stop=True),
                    reads=[bcflat2, gb("ones_r")], writes=[bB[3]])
                P.emit("scalar", lambda e: e.activation(
                    t_eec2[:], pb[3][:, 0:128], ACT.Exp),
                    reads=[bB[3]], writes=[beec2])
                P.emit("vector", lambda e, hh=hh: e.tensor_mul(
                    t_ce2[:], t_kC[:, hh * 128:(hh + 1) * 128], t_eec2[:]),
                    reads=[bkC, beec2], writes=[bce2])
                P.emit("vector", lambda e, hh=hh: e.tensor_scalar(
                    t_hP[:], t_hinit[:, hh * 128:(hh + 1) * 128],
                    t_Pb[:, hh:hh + 1], None, ALU.mult),
                    reads=[bhinit, bPb], writes=[bhP])
                P.emit("tensor", lambda e: e.matmul(
                    pb[0][:, 0:128], t_hP[:], t_ce2[:], start=True, stop=True),
                    reads=[bhP, bce2], writes=[bB[0]])
                P.emit("vector", lambda e, hh=hh: e.tensor_mul(
                    t_ygc[:, hh * 128:(hh + 1) * 128], pb[0][:, 0:128],
                    t_kG[:, hh * 128:(hh + 1) * 128]),
                    reads=[bB[0], bkG], writes=[bygc])
            P.emit("vector", lambda e: e.tensor_add(t_kY[:], t_kY[:], t_ygc[:]),
                   reads=[bygc], writes=[bkY2])
            P.dma("sync", d_yg.rearrange("(ci p) t -> p ci t", p=128)[
                :, :, ccols[0]:ccols[1]], t_kY[:], reads=[bkY2], writes=[bygsp2])

        if stop_after == "corr":
            _final_wait(P)
            P.finalize(ctx.enter_context(nc.Block()))
            return nc, d_dbg

        s3.close()
        sA.close()
        P.set_fence(P.fence())
        s4 = ctx.enter_context(ExitStack())

        def sbuf4(name, shape, dt=F32):
            return s4.enter_context(nc.sbuf_tensor(name, shape, dt))

        t_wo = sbuf4("op_wo", [128, 32 * 2048], BF16)
        t_ygi = [sbuf4(f"op_yg{i}", [128, DI], BF16) for i in range(2)]
        t_so = [sbuf4(f"op_so{i}", [128, DM], BF16) for i in range(2)]
        bwo = Buf()
        bygi = [Buf(), Buf()]
        bso = [Buf(), Buf()]
        bssp = Buf()

        for dc in range(8):
            P.dma("gpsimd",
                  t_wo[:].rearrange("p (c m) -> p c m", m=2048)[:, dc * 4:(dc + 1) * 4, :],
                  d_WoutT.rearrange("(c p) m -> p c m", p=128)[:, dc * 4:(dc + 1) * 4, :],
                  writes=[bwo])
        for tc in range(NCH):
            gi, bgi = t_ygi[tc % 2], bygi[tc % 2]
            P.dma("sync", gi[:],
                  d_yg.rearrange("(ci p) t -> p ci t", p=128)[
                      :, :, tc * 128:(tc + 1) * 128],
                  reads=[bygsp2], writes=[bgi])
            for di in range(32):
                for q in range(4):
                    P.emit("tensor", lambda e, gi=gi, di=di, q=q: e.matmul(
                        pb[q][:, 0:512],
                        gi[:, di * 128:(di + 1) * 128],
                        t_wo[:, di * 2048 + q * 512: di * 2048 + (q + 1) * 512],
                        start=(di == 0), stop=(di == 31)),
                        reads=[bgi, bwo] if di == 0 else
                        ([bgi, bwo] if di == 31 else ()),
                        writes=[bB[q]] if di == 31 else (),
                        deps=([bB[q].w] + bB[q].r) if di == 0 else (),
                        key="pe" if di == 31 else None)
            so, bso_ = t_so[tc % 2], bso[tc % 2]
            for q in range(4):
                eng = "vector" if q % 2 == 0 else "scalar"
                if eng == "vector":
                    P.emit("vector", lambda e, so=so, q=q: e.tensor_copy(
                        out=so[:, q * 512:(q + 1) * 512], in_=pb[q][:, 0:512]),
                        reads=[bB[q]], writes=[bso_])
                else:
                    P.emit("scalar", lambda e, so=so, q=q: e.copy(
                        out=so[:, q * 512:(q + 1) * 512], in_=pb[q][:, 0:512]),
                        reads=[bB[q]], writes=[bso_])
            P.dma("sync", d_s[tc * 128:(tc + 1) * 128, :], so[:],
                  reads=[bso_], writes=[bssp])

        if stop_after == "outproj":
            _final_wait(P)
            P.finalize(ctx.enter_context(nc.Block()))
            return nc, d_dbg

        s4.close()
        P.set_fence(P.fence())
        s5 = ctx.enter_context(ExitStack())

        def sbuf5(name, shape, dt=F32):
            return s5.enter_context(nc.sbuf_tensor(name, shape, dt))

        t_qxf = sbuf5("qxf", [128, KD * TOK], BF16)
        t_xq = [sbuf5(f"qx_x{i}", [128, DM]) for i in range(2)]
        t_xnq = sbuf5("qx_xn", [128, DM])
        t_qxb = sbuf5("qx_qb", [128, DM], BF16)
        bqxf = [Buf() for _ in range(NCH)]
        bxq = [Buf(), Buf()]
        bxnq, bqxb = Buf(), Buf()

        for i in range(NCH):
            xb, bx = t_xq[i % 2], bxq[i % 2]
            P.dma("sync", xb[:], d_x[PFX + i * 128: PFX + (i + 1) * 128, :],
                  writes=[bx])
            st2 = 2 * (i + 1)
            P.emit("scalar", lambda e, xb=xb, st2=st2: e.activation(
                t_xnq[:], xb[:], ACT.Identity,
                bias=t_stats[:, st2:st2 + 1], scale=t_stats[:, st2 + 1:st2 + 2]),
                reads=[bx, gb("stats")], writes=[bxnq])
            P.emit("vector", lambda e: e.tensor_scalar(
                t_xnq[:], t_xnq[:], t_scal[:, 0:1], MAGIC, ALU.mult, ALU.add),
                reads=[gb("scal")], writes=[bxnq])
            P.emit("vector", lambda e: e.tensor_scalar(
                t_qxb[:], t_xnq[:], MAGIC, None, ALU.subtract),
                reads=[bxnq], writes=[bqxb])
            for kc in range(KD):
                bank = 6 + kc % 2
                P.emit("tensor", lambda e, kc=kc, bank=bank: e.transpose(
                    pb[bank][:, 0:128], t_qxb[:, kc * 128:(kc + 1) * 128], t_eye_b[:]),
                    reads=[bqxb, gb("eye_b")], writes=[bB[bank]])
                if kc % 2 == 0:
                    P.emit("vector", lambda e, kc=kc, i=i, bank=bank: e.tensor_copy(
                        out=t_qxf[:, kc * TOK + i * 128: kc * TOK + (i + 1) * 128],
                        in_=pb[bank][:, 0:128]),
                        reads=[bB[bank]], writes=[bqxf[i]])
                else:
                    P.emit("scalar", lambda e, kc=kc, i=i, bank=bank: e.copy(
                        out=t_qxf[:, kc * TOK + i * 128: kc * TOK + (i + 1) * 128],
                        in_=pb[bank][:, 0:128]),
                        reads=[bB[bank]], writes=[bqxf[i]])

        t_b1 = sbuf5("t_b1c", [128, KI])
        t_sw1 = sbuf5("sw1", [128, KI])
        t_rw1 = sbuf5("rw1", [128, KI])
        t_sxsw1 = sbuf5("sxsw1", [128, KI])
        t_onesq2 = sbuf5("onesq2", [128, 128])
        t_w1m = [sbuf5(f"w1m{i}", [128, DM]) for i in range(2)]
        t_wq = [sbuf5(f"f1_wq{i}", [128, KD * 512]) for i in range(2)]
        t_qw = sbuf5("f1_qw", [128, KD * 512], BF16)
        t_rbc = sbuf5("f1_rbc", [128, 512])
        t_qtmp = sbuf5("f1_qtmp", [128, 512])
        t_hsp = [sbuf5(f"f1_h{i}", [128, 512]) for i in range(4)]
        t_hcol = sbuf5("f1_hc", [128, 1])
        bb1, bsw1, brw1, bsxsw1 = Buf(), Buf(), Buf(), Buf()
        bonesq2 = Buf()
        bw1m = [Buf(), Buf()]
        bwq = [Buf(), Buf()]
        bqw, brbc, bqtmp = Buf(), Buf(), Buf()
        bhsp = [Buf() for _ in range(4)]
        bhcol = Buf()
        bhspill = Buf()

        P.dma("sync", t_b1[:], d_b1[:], writes=[bb1])
        P.emit("vector", lambda e: e.memset(t_onesq2[:], 1.0), writes=[bonesq2])
        for r in range(KI):
            wb, bwb = t_w1m[r % 2], bw1m[r % 2]
            P.dma("scalar", wb[:], d_w1[r * 128:(r + 1) * 128, :], writes=[bwb])
            P.emit("vector", lambda e, wb=wb, r=r: e.reduce_max(
                t_sw1[:, r:r + 1], wb[:], AX.X, apply_absolute_value=True),
                reads=[bwb], writes=[bsw1])
        P.emit("vector", lambda e: e.tensor_scalar(
            t_sw1[:], t_sw1[:], 1.0 / 127.0, 1e-8, ALU.mult, ALU.max),
            writes=[bsw1])
        P.emit("vector", lambda e: e.reciprocal(t_rw1[:], t_sw1[:]),
               reads=[bsw1], writes=[brw1])
        P.emit("vector", lambda e: e.tensor_scalar(
            t_sxsw1[:], t_sw1[:], t_scal[:, 1:2], None, ALU.mult),
            reads=[bsw1, gb("scal")], writes=[bsxsw1])

        for Ip in range(KI // 4):
            f0 = Ip * 512
            wqb, bwqb = t_wq[Ip % 2], bwq[Ip % 2]
            P.dma("gpsimd", wqb[:].rearrange("p (k f) -> p k f", k=KD),
                  d_w1T[:, f0:f0 + 512].rearrange("(k p) f -> p k f", p=128),
                  writes=[bwqb])
            for c4 in range(4):
                P.emit("vector", lambda e, Ip=Ip, c4=c4: e.tensor_scalar(
                    t_qtmp[:, 0:128], t_onesq2[:, 0:128],
                    t_rw1[:, Ip * 4 + c4: Ip * 4 + c4 + 1], None, ALU.mult),
                    reads=[bonesq2, brw1], writes=[bqtmp])
                P.emit("tensor", lambda e: e.transpose(
                    pb[4][:, 0:128], t_qtmp[:, 0:128], t_eye_f[:]),
                    reads=[bqtmp, gb("eye_f")], writes=[bB[4]])
                P.emit("vector", lambda e, c4=c4: e.tensor_copy(
                    out=t_rbc[:, c4 * 128:(c4 + 1) * 128], in_=pb[4][:, 0:128]),
                    reads=[bB[4]], writes=[brbc])
            for k in range(KD):
                P.emit("vector", lambda e, k=k, wqb=wqb: e.tensor_mul(
                    t_qtmp[:], wqb[:, k * 512:(k + 1) * 512], t_rbc[:]),
                    reads=[bwqb, brbc], writes=[bqtmp])
                P.emit("vector", lambda e, k=k: e.tensor_scalar(
                    t_qw[:, k * 512:(k + 1) * 512], t_qtmp[:], MAGIC, MAGIC,
                    ALU.add, ALU.subtract),
                    reads=[bqtmp], writes=[bqw])
            for tg in range(4):
                for Ic in range(4):
                    bank = (tg * 4 + Ic) % 4
                    for k in range(KD):
                        P.emit("tensor", lambda e, bank=bank, k=k, Ic=Ic, tg=tg: e.matmul(
                            pb[bank][:, 0:512],
                            t_qw[:, k * 512 + Ic * 128: k * 512 + Ic * 128 + 128],
                            t_qxf[:, k * TOK + tg * 512: k * TOK + (tg + 1) * 512],
                            start=(k == 0), stop=(k == KD - 1)),
                            reads=([bqw] + bqxf) if k == 0 else
                            ([bqw] if k == KD - 1 else ()),
                            writes=[bB[bank]] if k == KD - 1 else (),
                            deps=([bB[bank].w] + bB[bank].r) if k == 0 else (),
                            key="pe" if k == KD - 1 else None)
                    ci = Ip * 4 + Ic
                    sps = (tg * 4 + Ic) % 4
                    P.emit("scalar", lambda e, bank=bank, sps=sps, ci=ci: e.activation(
                        t_hsp[sps][:], pb[bank][:, 0:512], ACT.Gelu,
                        bias=t_b1[:, ci:ci + 1], scale=t_sxsw1[:, ci:ci + 1]),
                        reads=[bB[bank], bb1, bsxsw1], writes=[bhsp[sps]])
                    P.emit("vector", lambda e, sps=sps: e.reduce_max(
                        t_hcol[:], t_hsp[sps][:], AX.X, apply_absolute_value=True),
                        reads=[bhsp[sps]], writes=[bhcol])
                    P.emit("vector", lambda e: e.tensor_max(
                        t_amax[:, 1:2], t_amax[:, 1:2], t_hcol[:]),
                        reads=[bhcol], writes=[gb("amax")])
                    P.dma("sync",
                          d_h.rearrange("(ci p) t -> p ci t", p=128)[
                              :, ci, tg * 512:(tg + 1) * 512],
                          t_hsp[sps][:], reads=[bhsp[sps]], writes=[bhspill])

        P.dma("gpsimd", d_mx2i[:], t_amax[:, 1:2], reads=[gb("amax")],
              writes=[gb("dmx2i")])
        if local_cc:
            P.dma("gpsimd", d_mx2o[:], d_mx2i[:], reads=[gb("dmx2i")],
                  writes=[gb("dmx2o")])
        else:
            P.emit("gpsimd", lambda e: e.collective_compute(
                "AllReduce", ALU.max, replica_groups=[list(range(NCORES))],
                ins=[d_mx2i], outs=[d_mx2o]),
                reads=[gb("dmx2i")], writes=[gb("dmx2o")])
        P.dma("sync", t_r1row[:], d_mx2o[:], reads=[gb("dmx2o")], writes=[gb("r1row")])
        P.emit("vector", lambda e: e.reduce_max(
            t_misc[0:1, 8:9], t_r1row[:], AX.X), reads=[gb("r1row")], writes=[gb("m8")])
        P.emit("vector", lambda e: e.tensor_scalar(
            t_misc[0:1, 9:10], t_misc[0:1, 8:9], 1.0 / 127.0, 1e-8, ALU.mult, ALU.max),
            reads=[gb("m8")], writes=[gb("m9")])
        P.emit("vector", lambda e: e.reciprocal(
            t_misc[0:1, 10:11], t_misc[0:1, 9:10]), reads=[gb("m9")], writes=[gb("m10")])
        P.emit("tensor", lambda e: e.matmul(
            pb[5][:, 8:10], t_ones_r[:, :], t_misc[0:1, 9:11], start=True, stop=True),
            reads=[gb("m9"), gb("m10"), gb("ones_r")], writes=[bB[5]])
        P.emit("vector", lambda e: e.tensor_copy(out=t_scal[:, 3:4], in_=pb[5][:, 8:9]),
               reads=[bB[5]], writes=[gb("scal")])
        P.emit("vector", lambda e: e.tensor_copy(out=t_scal[:, 2:3], in_=pb[5][:, 9:10]),
               reads=[bB[5]], writes=[gb("scal")])

        if stop_after == "ffn1":
            _final_wait(P)
            P.finalize(ctx.enter_context(nc.Block()))
            return nc, d_dbg

        s5.close()
        P.set_fence(P.fence())
        s6 = ctx.enter_context(ExitStack())

        def sbuf6(name, shape, dt=F32):
            return s6.enter_context(nc.sbuf_tensor(name, shape, dt))

        t_qw2 = sbuf6("f2_qw2", [128, 16 * DM], BF16)
        t_w2s = [sbuf6(f"f2_w2s{i}", [128, DM]) for i in range(2)]
        t_hq = [sbuf6(f"f2_hq{i}", [128, DM]) for i in range(2)]
        t_qh = [sbuf6(f"f2_qh{i}", [128, DM], BF16) for i in range(2)]
        t_gp = [sbuf6("f2_gp0", [128, DM])] * 2
        t_xf = [sbuf6("f2_x0", [128, DM])] * 2
        t_sf = [sbuf6(f"f2_s{i}", [128, DM], BF16) for i in range(2)]
        t_of = [sbuf6("f2_o0", [128, DM])] * 2
        t_gtmp = sbuf6("f2_gt", [128, DM])
        t_rb2 = sbuf6("f2_rb2", [128, DM])
        t_sc2 = sbuf6("f2_sc2", [128, DM])
        t_bb2 = sbuf6("f2_bb2", [128, DM])
        t_b2r = sbuf6("f2_b2r", [1, DM])
        t_sw2c = sbuf6("sw2c", [128, KD])
        t_wcol = sbuf6("f2_wcol", [128, 1])
        t_swrow = sbuf6("swrow", [1, DM])
        t_rwrow = sbuf6("rwrow", [1, DM])
        t_scrow = sbuf6("scrow", [1, DM])
        bqw2 = Buf()
        bw2s = [Buf(), Buf()]
        bhq = [Buf(), Buf()]
        bqh = [Buf(), Buf()]
        _bgp = Buf(); bgp = [_bgp, _bgp]
        _bxf = Buf(); bxf = [_bxf, _bxf]
        bsf = [Buf(), Buf()]
        _bof = Buf(); bof = [_bof, _bof]
        bgtmp, brb2, bsc2, bbb2, bb2r = Buf(), Buf(), Buf(), Buf(), Buf()
        bgpsp = Buf()
        bout = Buf()

        bsw2c, bwcol = Buf(), Buf()
        bswrow, brwrow, bscrow = Buf(), Buf(), Buf()

        P.emit("vector", lambda e: e.memset(t_sw2c[:], 0.0), writes=[bsw2c])
        for r in range(KD):
            for part in range(4):
                wb, bwb = t_hq[part % 2], bhq[part % 2]
                P.dma("scalar", wb[:],
                      d_w2[r * 128:(r + 1) * 128, part * 2048:(part + 1) * 2048],
                      writes=[bwb])
                P.emit("vector", lambda e, wb=wb: e.reduce_max(
                    t_wcol[:], wb[:], AX.X, apply_absolute_value=True),
                    reads=[bwb], writes=[bwcol])
                P.emit("vector", lambda e, r=r: e.tensor_max(
                    t_sw2c[:, r:r + 1], t_sw2c[:, r:r + 1], t_wcol[:]),
                    reads=[bwcol], writes=[bsw2c])
        P.emit("vector", lambda e: e.tensor_scalar(
            t_sw2c[:], t_sw2c[:], 1.0 / 127.0, 1e-8, ALU.mult, ALU.max),
            writes=[bsw2c])
        P.dma("sync", d_row.rearrange("one (c p) -> p (one c)", p=128), t_sw2c[:],
              reads=[bsw2c], writes=[gb("drow")], slow=True)
        P.dma("sync", t_swrow[:], d_row[:], reads=[gb("drow")], writes=[bswrow])
        P.emit("vector", lambda e: e.reciprocal(t_rwrow[:], t_swrow[:]),
               reads=[bswrow], writes=[brwrow])
        P.emit("vector", lambda e: e.tensor_scalar(
            t_scrow[:], t_swrow[:], t_misc[0:1, 9:10], None, ALU.mult),
            reads=[bswrow, gb("m9")], writes=[bscrow])

        P.dma("sync", t_b2r[:], d_b2[:], writes=[bb2r])
        for q in range(4):
            P.emit("tensor", lambda e, q=q: e.matmul(
                pb[4][:, 0:512], t_ones_r[:, :],
                t_rwrow[0:1, q * 512:(q + 1) * 512], start=True, stop=True),
                reads=[brwrow, gb("ones_r")], writes=[bB[4]],
                deps=[bB[4].w] + bB[4].r)
            P.emit("vector", lambda e, q=q: e.tensor_copy(
                out=t_rb2[:, q * 512:(q + 1) * 512], in_=pb[4][:, 0:512]),
                reads=[bB[4]], writes=[brb2])
            P.emit("tensor", lambda e, q=q: e.matmul(
                pb[5][:, 0:512], t_ones_r[:, :],
                t_scrow[0:1, q * 512:(q + 1) * 512], start=True, stop=True),
                reads=[bscrow, gb("ones_r")], writes=[bB[5]],
                deps=[bB[5].w] + bB[5].r)
            P.emit("vector", lambda e, q=q: e.tensor_copy(
                out=t_sc2[:, q * 512:(q + 1) * 512], in_=pb[5][:, 0:512]),
                reads=[bB[5]], writes=[bsc2])
            P.emit("tensor", lambda e, q=q: e.matmul(
                pb[4][:, 512 - 512:512], t_ones_r[:, :],
                t_b2r[0:1, q * 512:(q + 1) * 512], start=True, stop=True),
                reads=[bb2r, gb("ones_r")], writes=[bB[4]],
                deps=[bB[4].w] + bB[4].r)
            P.emit("vector", lambda e, q=q: e.tensor_copy(
                out=t_bb2[:, q * 512:(q + 1) * 512], in_=pb[4][:, 0:512]),
                reads=[bB[4]], writes=[bbb2])

        for Iq in range(4):
            for k in range(16):
                wb, bwb = t_w2s[k % 2], bw2s[k % 2]
                P.dma("gpsimd", wb[:],
                      d_w2T[Iq * 2048 + k * 128: Iq * 2048 + (k + 1) * 128, :],
                      writes=[bwb])
                P.emit("vector", lambda e, wb=wb: e.tensor_mul(
                    t_gtmp[:], wb[:], t_rb2[:]),
                    reads=[bwb, brb2], writes=[bgtmp])
                P.emit("vector", lambda e, k=k: e.tensor_scalar(
                    t_qw2[:, k * DM:(k + 1) * DM], t_gtmp[:], MAGIC, MAGIC,
                    ALU.add, ALU.subtract),
                    reads=[bgtmp], writes=[bqw2])
            for tc in range(NCH):
                par = tc % 2
                hq, bhq_ = t_hq[par], bhq[par]
                qh, bqh_ = t_qh[par], bqh[par]
                P.dma("sync", hq[:].rearrange("p (c t) -> p c t", t=128),
                      d_h.rearrange("(c p) t -> p c t", p=128)[
                          :, Iq * 16:(Iq + 1) * 16, tc * 128:(tc + 1) * 128],
                      reads=[bhspill], writes=[bhq_])
                P.emit("vector", lambda e, hq=hq: e.tensor_scalar(
                    t_gtmp[:], hq[:], t_scal[:, 2:3], MAGIC, ALU.mult, ALU.add),
                    reads=[bhq_, gb("scal")], writes=[bgtmp])
                P.emit("vector", lambda e, qh=qh: e.tensor_scalar(
                    qh[:], t_gtmp[:], MAGIC, None, ALU.subtract),
                    reads=[bgtmp], writes=[bqh_])
                for k in range(16):
                    for q in range(4):
                        P.emit("tensor", lambda e, qh=qh, k=k, q=q: e.matmul(
                            pb[q][:, 0:512],
                            qh[:, k * 128:(k + 1) * 128],
                            t_qw2[:, k * DM + q * 512: k * DM + (q + 1) * 512],
                            start=(k == 0), stop=(k == 15)),
                            reads=[bqh_, bqw2] if k in (0, 15) else (),
                            writes=[bB[q]] if k == 15 else (),
                            deps=([bB[q].w] + bB[q].r) if k == 0 else (),
                            key="pe" if k == 15 else None)
                gp, bgp_ = t_gp[par], bgp[par]
                if Iq == 0:
                    for q in range(4):
                        if q % 2 == 0:
                            P.emit("vector", lambda e, gp=gp, q=q: e.tensor_copy(
                                out=gp[:, q * 512:(q + 1) * 512], in_=pb[q][:, 0:512]),
                                reads=[bB[q]], writes=[bgp_])
                        else:
                            P.emit("scalar", lambda e, gp=gp, q=q: e.copy(
                                out=gp[:, q * 512:(q + 1) * 512], in_=pb[q][:, 0:512]),
                                reads=[bB[q]], writes=[bgp_])
                    P.dma("sync", d_gp[tc * 128:(tc + 1) * 128, :], gp[:],
                          reads=[bgp_], writes=[bgpsp])
                elif Iq < 3:
                    P.dma("sync", gp[:], d_gp[tc * 128:(tc + 1) * 128, :],
                          reads=[bgpsp], writes=[bgp_])
                    for q in range(4):
                        P.emit("vector", lambda e, gp=gp, q=q: e.tensor_tensor(
                            gp[:, q * 512:(q + 1) * 512],
                            gp[:, q * 512:(q + 1) * 512], pb[q][:, 0:512], ALU.add),
                            reads=[bB[q]], writes=[bgp_])
                    P.dma("sync", d_gp[tc * 128:(tc + 1) * 128, :], gp[:],
                          reads=[bgp_], writes=[bgpsp])
                else:
                    xf, bxf_ = t_xf[par], bxf[par]
                    sf, bsf_ = t_sf[par], bsf[par]
                    of, bof_ = t_of[par], bof[par]
                    P.dma("sync", gp[:], d_gp[tc * 128:(tc + 1) * 128, :],
                          reads=[bgpsp], writes=[bgp_])
                    P.dma("scalar", xf[:], d_x[PFX + tc * 128: PFX + (tc + 1) * 128, :],
                          writes=[bxf_])
                    P.dma("scalar", sf[:], d_s[tc * 128:(tc + 1) * 128, :],
                          reads=[bssp], writes=[bsf_])
                    for q in range(4):
                        P.emit("vector", lambda e, gp=gp, q=q: e.tensor_tensor(
                            gp[:, q * 512:(q + 1) * 512],
                            gp[:, q * 512:(q + 1) * 512], pb[q][:, 0:512], ALU.add),
                            reads=[bB[q]], writes=[bgp_])
                    P.emit("vector", lambda e, gp=gp: e.tensor_mul(
                        t_gtmp[:], gp[:], t_sc2[:]),
                        reads=[bgp_, bsc2], writes=[bgtmp])
                    P.emit("vector", lambda e: e.tensor_add(
                        t_gtmp[:], t_gtmp[:], t_bb2[:]),
                        reads=[bbb2], writes=[bgtmp])
                    P.emit("scalar", lambda e: e.activation(
                        t_gtmp[:], t_gtmp[:], ACT.Silu),
                        writes=[bgtmp])
                    P.emit("vector", lambda e, sf=sf: e.tensor_mul(
                        t_gtmp[:], t_gtmp[:], sf[:]),
                        reads=[bsf_], writes=[bgtmp])
                    P.emit("vector", lambda e, xf=xf, of=of: e.tensor_add(
                        of[:], t_gtmp[:], xf[:]),
                        reads=[bgtmp, bxf_], writes=[bof_])
                    P.dma("sync", d_out[tc * 128:(tc + 1) * 128, :], of[:],
                          reads=[bof_], writes=[bout])

        s6.close()
        _final_wait(P)
        P.finalize(ctx.enter_context(nc.Block()))

    return nc, d_dbg


def _final_wait(P):
    deps = [P.tok(k) for k in ("pe", "dve", "act", "pool")]
    for qk in ("qsp", "qact", "qpool"):
        for i in range(P.NLANES):
            deps.append(P.tok(f"{qk}{i}"))
    P.emit("sync", lambda e: e.nop(), deps=deps, key=None)


def _stage(inputs):
    x = np.ascontiguousarray(np.asarray(inputs["L0_attn_x"], dtype=np.float32)
                             .reshape(B * L, DM))
    W_in = np.asarray(inputs["L0_attn_W_in"], dtype=np.float32)
    conv_w = np.ascontiguousarray(
        np.asarray(inputs["L0_attn_conv_w"], dtype=np.float32)
        .reshape(32, 128, 4).transpose(1, 0, 2).reshape(128, 32 * 4))
    conv_b = np.ascontiguousarray(
        np.asarray(inputs["L0_attn_conv_b"], dtype=np.float32)
        .reshape(32, 128).T)
    A_log = np.ascontiguousarray(
        np.asarray(inputs["L0_attn_A_log"], dtype=np.float32).reshape(NH, 1))
    Dp = np.ascontiguousarray(
        np.asarray(inputs["L0_attn_D_param"], dtype=np.float32).reshape(32, 128).T)
    dtb = np.ascontiguousarray(
        np.asarray(inputs["L0_attn_dt_bias"], dtype=np.float32).reshape(NH, 1))
    W_out = np.asarray(inputs["L0_attn_W_out"], dtype=np.float32)
    w1 = np.ascontiguousarray(np.asarray(inputs["L0_ffn_w1"], dtype=np.float32))
    b1 = np.ascontiguousarray(
        np.asarray(inputs["L0_ffn_b1"], dtype=np.float32).reshape(KI, 128).T)
    w2 = np.ascontiguousarray(np.asarray(inputs["L0_ffn_w2"], dtype=np.float32))
    b2 = np.ascontiguousarray(
        np.asarray(inputs["L0_ffn_b2"], dtype=np.float32).reshape(1, DM))

    WinT = np.ascontiguousarray(W_in.T)
    WoutT = np.ascontiguousarray(W_out.T)
    w1T = np.ascontiguousarray(w1.T)
    w2T = np.ascontiguousarray(w2.T)

    in_maps = []
    for k in range(NCORES):
        t0 = k * TOK
        xs = np.zeros((TOKA, DM), dtype=np.float32)
        if k % 2 == 1:
            xs[0:PFX] = x[t0 - PFX:t0]
        xs[PFX:] = x[t0:t0 + TOK]
        in_maps.append({
            "x": np.ascontiguousarray(xs),
            "WinT": WinT, "convw": conv_w, "convb": conv_b, "Alog": A_log,
            "Dparam": Dp, "dtb": dtb, "WoutT": WoutT,
            "w1T": w1T, "w1n": w1, "w2T": w2T, "w2n": w2,
            "b1": b1, "b2": b2,
            "isodd": np.array([[1.0 if k % 2 == 1 else 0.0]], dtype=np.float32),
        })
    return in_maps


def kernel(**inputs):
    nc, _ = build()
    in_maps = _stage(inputs)
    res = run_bass_kernel_spmd(nc, in_maps, core_ids=list(range(NCORES)))
    out = np.concatenate([res.results[k]["out"] for k in range(NCORES)], axis=0)
    return out.reshape(B, L, DM)


# version 05
# speedup vs baseline: 1.0284x; 1.0284x over previous
import numpy as np
from contextlib import ExitStack

import concourse.bass as bass
import concourse.mybir as mybir
from concourse.bass_utils import run_bass_kernel_spmd

F32 = mybir.dt.float32
BF16 = mybir.dt.bfloat16
AX = mybir.AxisListType
ALU = mybir.AluOpType
ACT = mybir.ActivationFunctionType

B, L, DM = 4, 4096, 2048
DI = 4096
NH, HD, NST = 32, 128, 128
DFF = 8192
PROJ = 2 * DI + 2 * NH * NST + NH
NCORES = 8
TOK = 2048
PFX = 128
TOKA = TOK + PFX
CS = 128
NCH = TOK // CS
KD = DM // 128
KI = DFF // 128
EPSLN = 1e-5
MAGIC = float(np.float32(3 * (2 ** 22)))
NEG_BIG = -1.0e30

ENG = ("tensor", "vector", "scalar", "gpsimd", "sync")
EK = {"tensor": "pe", "vector": "dve", "scalar": "act", "gpsimd": "pool", "sync": "sp"}
QK = {"sync": "qsp", "scalar": "qact", "gpsimd": "qpool"}


class Buf:
    __slots__ = ("w", "r", "excl")

    def __init__(self, excl=False):
        self.w = None
        self.r = []
        self.excl = excl


class Prog:

    NLANES = 8

    def __init__(self, nc, ctx):
        self.nc = nc
        self.q = {e: [] for e in ENG}
        self.sem = {}
        self.cnt = {}
        self.waited = {e: {} for e in ENG}
        keys = ["pe", "dve", "act", "pool", "sp"]
        for qk in ("qsp", "qact", "qpool"):
            for i in range(self.NLANES):
                keys.append(f"{qk}{i}")
        for k in keys:
            self.sem[k] = ctx.enter_context(nc.semaphore("sb_" + k))
            self.cnt[k] = 0
        self.dma_i = {"sync": 0, "scalar": 0, "gpsimd": 0}

    def fence(self):
        return [self.tok(k) for k in self.cnt]

    def set_fence(self, toks):
        self._fence = list(toks)

    def emit(self, eng, fn, reads=(), writes=(), deps=(), key="auto", n=1):
        if key == "auto":
            key = EK[eng]
        toks = list(deps)
        fence = getattr(self, "_fence", None)
        if fence is not None:
            for b in writes:
                if b.w is None and not b.r:
                    toks.extend(fence)
                    break
        ek = EK[eng]
        for b in reads:
            toks.append(b.w)
            if b.excl:
                for t in b.r:
                    if t is not None and t[0] != ek:
                        toks.append(t)
        for b in writes:
            toks.append(b.w)
            toks.extend(b.r)
        w = self.waited[eng]
        need = []
        for t in toks:
            if t is None:
                continue
            k, v = t
            if v > 0 and w.get(k, 0) < v:
                need.append((self.sem[k], v))
                w[k] = v
        tok = None
        if key is not None:
            self.cnt[key] += n
            tok = (key, self.cnt[key])
            sem, val = self.sem[key], n
        else:
            sem, val = None, 0
        self.q[eng].append((need, fn, sem, val))
        for b in reads:
            b.r.append(tok)
        for b in writes:
            b.w = tok
            b.r = []
        return tok

    def dma(self, eng, out, in_, reads=(), writes=(), deps=(), slow=False,
            accum=None):
        lane = self.dma_i[eng] % self.NLANES
        self.dma_i[eng] += 1
        key = QK[eng] + str(lane)
        deps = list(deps) + [(key, self.cnt[key])]
        if accum is not None:
            fn = lambda e, o=out, i=in_, a=accum: e.dma_start(out=o, in_=i, accum_op=a)
        else:
            fn = lambda e, o=out, i=in_, s=slow: e.dma_start(
                out=o, in_=i, allow_slow_non_contiguous=s)
        return self.emit(eng, fn,
                         reads=reads, writes=writes, deps=deps, key=key, n=16)

    def tok(self, key):
        return (key, self.cnt[key])

    def finalize(self, block):
        for eng in ENG:
            queue = self.q[eng]
            if not queue:
                continue

            def body(e, queue=queue):
                for need, fn, sem, val in queue:
                    for s, v in need:
                        e.wait_ge(s, v)
                    ins = fn(e)
                    if sem is not None:
                        ins.then_inc(sem, val)
            getattr(block, eng)(body)


def build(debug_taps=(), local_cc=False, stop_after=None, use_accum_dma=False):
    import ml_dtypes
    nc = bass.Bass("TRN2", target_bir_lowering=False)

    d_x = nc.dram_tensor("x", [TOKA, DM], F32, kind="ExternalInput").ap()
    d_WinT = nc.dram_tensor("WinT", [DM, PROJ], F32, kind="ExternalInput").ap()
    d_convw = nc.dram_tensor("convw", [128, 32 * 4], F32, kind="ExternalInput").ap()
    d_convb = nc.dram_tensor("convb", [128, 32], F32, kind="ExternalInput").ap()
    d_Alog = nc.dram_tensor("Alog", [NH, 1], F32, kind="ExternalInput").ap()
    d_D = nc.dram_tensor("Dparam", [128, 32], F32, kind="ExternalInput").ap()
    d_dtb = nc.dram_tensor("dtb", [NH, 1], F32, kind="ExternalInput").ap()
    d_WoutT = nc.dram_tensor("WoutT", [DI, DM], F32, kind="ExternalInput").ap()
    d_w1T = nc.dram_tensor("w1T", [DM, DFF], F32, kind="ExternalInput").ap()
    d_w1 = nc.dram_tensor("w1n", [DFF, DM], F32, kind="ExternalInput").ap()
    d_w2T = nc.dram_tensor("w2T", [DFF, DM], F32, kind="ExternalInput").ap()
    d_w2 = nc.dram_tensor("w2n", [DM, DFF], F32, kind="ExternalInput").ap()
    d_b1 = nc.dram_tensor("b1", [128, KI], F32, kind="ExternalInput").ap()
    d_b2 = nc.dram_tensor("b2", [128, KD], F32, kind="ExternalInput").ap()
    d_isodd = nc.dram_tensor("isodd", [1, 1], F32, kind="ExternalInput").ap()

    d_out = nc.dram_tensor("out", [TOK, DM], F32, kind="ExternalOutput").ap()

    d_val = nc.dram_tensor("sp_val", [DI, TOK], BF16, kind="Internal").ap()
    d_valT = nc.dram_tensor("sp_valT", [TOK, DI], BF16, kind="Internal").ap()
    d_BT = nc.dram_tensor("sp_BT", [TOK, DI], BF16, kind="Internal").ap()
    d_B = nc.dram_tensor("sp_B", [DI, TOK], BF16, kind="Internal").ap()
    d_C = nc.dram_tensor("sp_C", [DI, TOK], BF16, kind="Internal").ap()
    d_gate = nc.dram_tensor("sp_gate", [DI, TOK], BF16, kind="Internal").ap()
    d_yg = nc.dram_tensor("sp_yg", [DI, TOK], BF16, kind="Internal").ap()
    d_h = nc.dram_tensor("sp_h", [DFF, TOK], F32, kind="Internal").ap()
    d_s = nc.dram_tensor("sp_s", [TOK, DM], BF16, kind="Internal").ap()
    d_gp = nc.dram_tensor("sp_gp", [TOK, DM], F32, kind="Internal").ap()
    d_row = nc.dram_tensor("sp_row", [1, DM], F32, kind="Internal").ap()
    d_mx1i = nc.dram_tensor("cc_mx1i", [1, 128], F32, kind="Internal").ap()
    d_mx1o = nc.dram_tensor("cc_mx1o", [1, 128], F32, kind="Internal").ap()
    d_mx2i = nc.dram_tensor("cc_mx2i", [1, 128], F32, kind="Internal").ap()
    d_mx2o = nc.dram_tensor("cc_mx2o", [1, 128], F32, kind="Internal").ap()
    d_agi = nc.dram_tensor("cc_agi", [128, DI], F32, kind="Internal").ap()
    d_ago = nc.dram_tensor("cc_ago", [256, DI], F32, kind="Internal").ap()

    eye_f32 = nc.inline_tensor(np.eye(128, dtype=np.float32), "c_eye_f").ap()
    eye_bf = nc.inline_tensor(
        np.eye(128, dtype=np.float32).astype(ml_dtypes.bfloat16), "c_eye_b").ap()
    mk = np.where(np.arange(128)[None, :] >= np.arange(128)[:, None],
                  np.float32(0.0), np.float32(NEG_BIG)).astype(np.float32)
    d_mask = nc.inline_tensor(mk, "c_mask").ap()
    mk01 = np.tile((np.arange(128)[None, :] >= np.arange(128)[:, None]).astype(np.float32), (1, 4))
    d_mask01 = nc.inline_tensor(np.ascontiguousarray(mk01), "c_mask01").ap()

    DBG = {
        "dbg_xnf": ([128, KD * TOKA], BF16),
        "dbg_stats": ([128, 2 * (TOKA // 128)], F32),
        "dbg_scal": ([128, 8], F32),
        "dbg_ld": ([32, TOK], F32),
        "dbg_hend": ([128, DI], F32),
        "dbg_sw": ([128, 80], F32),
        "dbg_g": ([TOK, DM], F32),
        "dbg_acc": ([TOK, DM], F32),
        "dbg_sc2": ([128, DM], F32),
        "dbg_bb2": ([128, DM], F32),
        "dbg_sw2c": ([128, KD], F32),
    }
    d_dbg = {}
    for nm in debug_taps:
        shape, dt = DBG[nm]
        d_dbg[nm] = nc.dram_tensor(nm, shape, dt, kind="ExternalOutput").ap()

    with ExitStack() as ctx:
        P = Prog(nc, ctx)

        def sbuf(name, shape, dt=F32):
            return ctx.enter_context(nc.sbuf_tensor(name, shape, dt))

        t_eye_f = sbuf("eye_f", [128, 128])
        t_eye_b = sbuf("eye_b", [128, 128], BF16)
        t_mask = sbuf("mask", [128, 128])
        t_mask01 = sbuf("mask01", [128, 512])
        t_ones_r = sbuf("ones_r", [1, 128])
        t_cst = sbuf("cst", [128, 8])
        t_stats = sbuf("stats", [128, 2 * (TOKA // 128)])
        t_amax = sbuf("amax", [128, 2])
        t_scal = sbuf("scal", [128, 8])
        t_r1row = sbuf("r1row", [1, 128])
        t_misc = sbuf("misc", [128, 16])

        sA = ctx.enter_context(ExitStack())

        def sbufA(name, shape, dt=F32):
            return sA.enter_context(nc.sbuf_tensor(name, shape, dt))

        t_ones_q = sbufA("ones_q", [32, 128])
        t_ld = sbufA("ld", [32, TOK])
        t_cumf = sbufA("cumf", [32, TOK])
        t_lrow = sbufA("lrow", [1, NH * NCH])
        t_lpre = sbufA("lpre", [1, NH * NCH])
        t_lrun = sbufA("lrun", [1, NH])
        t_h = sbufA("h_state", [128, DI])
        t_Dp = sbufA("Dp", [128, 32])

        pb = [ctx.enter_context(nc.psum_tensor(f"pb{i}", [128, 512], F32))
              for i in range(8)]
        bB = [Buf(excl=True) for _ in range(8)]

        bufs = {}
        def gb(name):
            if name not in bufs:
                bufs[name] = Buf()
            return bufs[name]

        P.dma("sync", t_eye_f[:], eye_f32[:], writes=[gb("eye_f")])
        P.dma("sync", t_eye_b[:], eye_bf[:], writes=[gb("eye_b")])
        P.dma("sync", t_mask[:], d_mask[:], writes=[gb("mask")])
        P.dma("sync", t_mask01[:], d_mask01[:], writes=[gb("mask01")])
        P.dma("sync", t_Dp[:], d_D[:], writes=[gb("Dp")])
        P.emit("vector", lambda e: e.memset(t_scal[:], 0.0), writes=[gb("scal")])
        P.dma("sync", t_scal[0:1, 4:5], d_isodd[:], writes=[gb("scal")])

        P.emit("vector", lambda e: e.memset(t_ones_r[:], 1.0), writes=[gb("ones_r")])
        P.emit("vector", lambda e: e.memset(t_ones_q[:], 1.0), writes=[gb("ones_q")])
        P.emit("vector", lambda e: e.memset(t_cst[:], 0.0), writes=[gb("cst")])
        P.emit("vector", lambda e: e.memset(t_cst[:, 0:1], -1.0),
               writes=[gb("cst")])
        P.emit("vector", lambda e: e.memset(t_cst[:, 1:2], 1.0), writes=[gb("cst")])
        P.emit("vector", lambda e: e.memset(t_cst[:, 2:3], EPSLN), writes=[gb("cst")])
        P.emit("vector", lambda e: e.memset(t_amax[:], 0.0), writes=[gb("amax")])
        P.emit("vector", lambda e: e.memset(t_misc[:], 0.0), writes=[gb("m0"), gb("m1"), gb("m2"), gb("m3"), gb("m4")])
        P.emit("vector", lambda e: e.memset(t_lrun[:], 0.0), writes=[gb("lrun")])
        P.emit("vector", lambda e: e.memset(t_h[:], 0.0), writes=[gb("h")])

        s1 = ctx.enter_context(ExitStack())

        def sbuf1(name, shape, dt=F32):
            return s1.enter_context(nc.sbuf_tensor(name, shape, dt))

        t_cw = sbuf1("t_convw", [128, 32 * 4])
        t_cb = sbuf1("t_convb", [128, 32])
        t_negA = sbuf1("negA", [32, 1])
        t_dtb = sbuf1("dtbc", [32, 1])
        t_cvh = sbuf1("cvh", [128, 32 * 4])
        P.dma("sync", t_cw[:], d_convw[:], writes=[gb("cw")])
        P.dma("sync", t_cb[:], d_convb[:], writes=[gb("cb")])
        P.dma("sync", t_negA[:], d_Alog[:], writes=[gb("negA")])
        P.dma("sync", t_dtb[:], d_dtb[:], writes=[gb("dtb")])
        P.emit("scalar", lambda e: e.activation(t_negA[:], t_negA[:], ACT.Exp),
               reads=[gb("negA")], writes=[gb("negA")])
        P.emit("vector", lambda e: e.tensor_scalar_mul(t_negA[:], t_negA[:], -1.0),
               writes=[gb("negA")])

        NLC = TOKA // 128
        t_xc = [sbuf1(f"ln_x{i}", [128, DM]) for i in range(2)]
        t_bn6 = sbuf1("ln_bn6", [128, 24])
        t_mv = sbuf1("ln_mv", [128, 2])
        t_xn = sbuf1("ln_xn", [128, DM])
        t_xnb = sbuf1("ln_xnb", [128, DM], BF16)
        t_cmax = sbuf1("ln_cmax", [128, 1])
        t_xnf = sbuf1("xnf", [128, KD * TOKA], BF16)

        bxc = [Buf(), Buf()]
        bbn6, bmv, bxn, bxnb, bcmax = Buf(), Buf(), Buf(), Buf(), Buf()
        bstats, bamax = gb("stats"), gb("amax")
        bxnf = [Buf() for _ in range(NLC)]

        for i in range(NLC):
            xb, bx = t_xc[i % 2], bxc[i % 2]
            P.dma("sync", xb[:], d_x[i * 128:(i + 1) * 128, :], writes=[bx])
            for s4 in range(4):
                P.emit("vector", lambda e, s4=s4, xb=xb: e.bn_stats(
                    t_bn6[:, s4 * 6:(s4 + 1) * 6], xb[:, s4 * 512:(s4 + 1) * 512]),
                    reads=[bx], writes=[bbn6])
            P.emit("vector", lambda e: e.bn_aggr(t_mv[:], t_bn6[:]),
                   reads=[bbn6], writes=[bmv])
            P.emit("vector", lambda e: e.tensor_scalar_add(
                t_misc[:, 0:1], t_mv[:, 1:2], t_cst[:, 2:3]),
                reads=[bmv, gb("cst")], writes=[gb("m0")])
            P.emit("scalar", lambda e: e.sqrt(t_misc[:, 1:2], t_misc[:, 0:1]),
                   reads=[gb("m0")], writes=[gb("m1")])
            P.emit("vector", lambda e, i=i: e.reciprocal(
                t_stats[:, 2 * i + 1:2 * i + 2], t_misc[:, 1:2]),
                reads=[gb("m1")], writes=[bstats])
            P.emit("vector", lambda e, i=i: e.tensor_scalar(
                t_stats[:, 2 * i:2 * i + 1], t_mv[:, 0:1],
                t_stats[:, 2 * i + 1:2 * i + 2], -1.0, ALU.mult, ALU.mult),
                reads=[bmv, bstats], writes=[bstats])
            P.emit("scalar", lambda e, i=i, xb=xb: e.activation(
                t_xn[:], xb[:], ACT.Identity,
                bias=t_stats[:, 2 * i:2 * i + 1],
                scale=t_stats[:, 2 * i + 1:2 * i + 2]),
                reads=[bx, bstats], writes=[bxn])
            P.emit("vector", lambda e: e.reduce_max(
                t_cmax[:], t_xn[:], AX.X, apply_absolute_value=True),
                reads=[bxn], writes=[bcmax])
            P.emit("vector", lambda e: e.tensor_max(
                t_amax[:, 0:1], t_amax[:, 0:1], t_cmax[:]),
                reads=[bcmax], writes=[bamax])
            P.emit("scalar", lambda e: e.copy(out=t_xnb[:], in_=t_xn[:]),
                   reads=[bxn], writes=[bxnb])
            for kc in range(KD):
                bank = 6 + kc % 2
                P.emit("tensor", lambda e, kc=kc, bank=bank: e.matmul(
                    pb[bank][:, 0:128], t_xnb[:, kc * 128:(kc + 1) * 128], t_eye_b[:],
                    start=True, stop=True),
                    reads=[bxnb, gb("eye_b")], writes=[bB[bank]])
                if kc % 2 == 0:
                    P.emit("vector", lambda e, kc=kc, i=i, bank=bank: e.tensor_copy(
                        out=t_xnf[:, kc * TOKA + i * 128: kc * TOKA + (i + 1) * 128],
                        in_=pb[bank][:, 0:128]),
                        reads=[bB[bank]], writes=[bxnf[i]])
                else:
                    P.emit("scalar", lambda e, kc=kc, i=i, bank=bank: e.copy(
                        out=t_xnf[:, kc * TOKA + i * 128: kc * TOKA + (i + 1) * 128],
                        in_=pb[bank][:, 0:128]),
                        reads=[bB[bank]], writes=[bxnf[i]])

        P.dma("gpsimd", d_mx1i[:], t_amax[:, 0:1], reads=[bamax], writes=[gb("dmx1i")])
        if local_cc:
            P.dma("gpsimd", d_mx1o[:], d_mx1i[:], reads=[gb("dmx1i")],
                  writes=[gb("dmx1o")])
        else:
            P.emit("gpsimd", lambda e: e.collective_compute(
                "AllReduce", ALU.max, replica_groups=[list(range(NCORES))],
                ins=[d_mx1i], outs=[d_mx1o]),
                reads=[gb("dmx1i")], writes=[gb("dmx1o")])
        P.dma("sync", t_r1row[:], d_mx1o[:], reads=[gb("dmx1o")], writes=[gb("r1row")])
        P.emit("vector", lambda e: e.reduce_max(
            t_misc[0:1, 2:3], t_r1row[:], AX.X), reads=[gb("r1row")], writes=[gb("m2")])
        P.emit("vector", lambda e: e.tensor_scalar(
            t_misc[0:1, 3:4], t_misc[0:1, 2:3], 1.0 / 127.0, 1e-8, ALU.mult, ALU.max),
            reads=[gb("m2")], writes=[gb("m3")])
        P.emit("vector", lambda e: e.reciprocal(
            t_misc[0:1, 4:5], t_misc[0:1, 3:4]), reads=[gb("m3")], writes=[gb("m4")])
        P.emit("tensor", lambda e: e.matmul(
            pb[5][:, 0:2], t_ones_r[:, :], t_misc[0:1, 3:5], start=True, stop=True),
            reads=[gb("m3"), gb("m4"), gb("ones_r")], writes=[bB[5]])
        bscal = gb("scal")
        P.emit("vector", lambda e: e.tensor_copy(out=t_scal[:, 1:2], in_=pb[5][:, 0:1]),
               reads=[bB[5]], writes=[bscal])
        P.emit("vector", lambda e: e.tensor_copy(out=t_scal[:, 0:1], in_=pb[5][:, 1:2]),
               reads=[bB[5]], writes=[bscal])
        P.emit("tensor", lambda e: e.matmul(
            pb[5][:, 4:5], t_ones_r[:, :], t_scal[0:1, 4:5], start=True, stop=True),
            reads=[gb("scal"), gb("ones_r")], writes=[bB[5]])
        P.emit("vector", lambda e: e.tensor_copy(out=t_scal[:, 4:5], in_=pb[5][:, 4:5]),
               reads=[bB[5]], writes=[bscal])

        if "dbg_stats" in d_dbg:
            P.dma("sync", d_dbg["dbg_stats"][:], t_stats[:], reads=[bstats])
        if "dbg_xnf" in d_dbg:
            P.dma("sync", d_dbg["dbg_xnf"][:], t_xnf[:], reads=bxnf)
        if "dbg_scal" in d_dbg:
            P.dma("sync", d_dbg["dbg_scal"][:], t_scal[:], reads=[bscal])
        if stop_after == "ln":
            _final_wait(P)
            P.finalize(ctx.enter_context(nc.Block()))
            return nc, d_dbg

        t_wp = [sbuf1(f"ip_wp{i}", [128, KD * 512], BF16) for i in range(2)]
        t_cv = sbuf1("ip_cv", [128, 512])
        t_sp = [sbuf1(f"ip_sp{i}", [128, 512], BF16) for i in range(4)]
        t_dt1 = sbuf1("ip_dt1", [32, 512])
        t_dt2 = sbuf1("ip_dt2", [32, 512])

        t_tsp = [sbuf1(f"ip_tsp{i}", [128, 512], BF16) for i in range(2)]
        btsp = [Buf(), Buf()]
        _tspi = [0]

        bwp = [Buf(), Buf()]
        bcv, bdt1, bdt2 = Buf(), Buf(), Buf()
        bsp = [Buf() for _ in range(4)]
        bcvh = [Buf() for _ in range(32)]
        bld = gb("ld")
        bspill = {"val": Buf(), "B": Buf(), "C": Buf(), "gate": Buf(),
                  "valT": Buf(), "BT": Buf()}

        def _emit_tspill(sps, cic, c0, dstT, nmk2):
            i2 = _tspi[0] % 2
            _tspi[0] += 1
            for k in range(4):
                bank = 6 + k % 2
                P.emit("tensor", lambda e, sps=sps, k=k, bank=bank: e.matmul(
                    pb[bank][:, 0:128],
                    t_sp[sps][:, k * 128:(k + 1) * 128], t_eye_b[:],
                    start=True, stop=True),
                    reads=[bsp[sps], gb("eye_b")],
                    deps=[bB[bank].w] + bB[bank].r, writes=[bB[bank]])
                if k % 2 == 0:
                    P.emit("vector", lambda e, i2=i2, k=k, bank=bank: e.tensor_copy(
                        out=t_tsp[i2][:, k * 128:(k + 1) * 128],
                        in_=pb[bank][:, 0:128]),
                        reads=[bB[bank]], writes=[btsp[i2]])
                else:
                    P.emit("scalar", lambda e, i2=i2, k=k, bank=bank: e.copy(
                        out=t_tsp[i2][:, k * 128:(k + 1) * 128],
                        in_=pb[bank][:, 0:128]),
                        reads=[bB[bank]], writes=[btsp[i2]])
            P.dma("scalar",
                  dstT[c0 - PFX: c0 - PFX + 512, cic * 128:(cic + 1) * 128]
                  .rearrange("(k p) f -> p k f", p=128),
                  t_tsp[i2][:].rearrange("p (k f) -> p k f", f=128),
                  reads=[btsp[i2]], writes=[bspill[nmk2]])

        NPAN = PROJ // 512 + 1
        for fp in range(NPAN):
            f0 = fp * 512
            W = 512 if fp < 32 else 32
            pbuf, bwpc = t_wp[fp % 2], bwp[fp % 2]
            P.dma("gpsimd", pbuf[:, 0:KD * W].rearrange("p (k f) -> p k f", k=KD),
                  d_WinT[:, f0:f0 + W].rearrange("(k p) f -> p k f", p=128),
                  writes=[bwpc])
            is_val = f0 < DI
            is_gate = DI <= f0 < 2 * DI
            is_dt = fp == 32
            groups = ([(-1, 0, 128)] if is_val else []) + \
                     [(g, PFX + g * 512, 512) for g in range(4)]
            for (g, c0, Wt) in groups:
                nfc = 4 if not is_dt else 1
                for fc in range(nfc):
                    fg = f0 + fc * 128
                    bank = 2 + ((fp * 24 + (g + 1) * 4 + fc) % 2)
                    Mt = 128 if not is_dt else 32
                    lnreads = [bxnf[ci] for ci in
                               range(c0 // 128, (c0 + Wt + 127) // 128)]
                    for k in range(KD):
                        P.emit("tensor",
                               lambda e, bank=bank, Mt=Mt, Wt=Wt, k=k, fc=fc, W=W,
                               c0=c0, pbuf=pbuf: e.matmul(
                                   pb[bank][0:Mt, 0:Wt],
                                   pbuf[:, k * W + fc * 128: k * W + fc * 128 + Mt],
                                   t_xnf[:, k * TOKA + c0: k * TOKA + c0 + Wt],
                                   start=(k == 0), stop=(k == KD - 1)),
                               reads=([bwpc] + lnreads) if k == 0
                               else ([bwpc] if k == KD - 1 else ()),
                               writes=[bB[bank]] if k == KD - 1 else (),
                               deps=[bB[bank].w] + [t for t in bB[bank].r]
                               if k == 0 else (),
                               key="pe" if k == KD - 1 else None)
                    if is_dt:
                        P.emit("scalar", lambda e, bank=bank, Wt=Wt: e.activation(
                            t_dt1[:, 0:Wt], pb[bank][0:32, 0:Wt], ACT.Abs,
                            bias=t_dtb[:, 0:1]),
                            reads=[bB[bank], gb("dtb")], writes=[bdt1])
                        P.emit("scalar", lambda e, Wt=Wt: e.activation(
                            t_dt1[:, 0:Wt], t_dt1[:, 0:Wt], ACT.Exp,
                            scale=t_cst[0:32, 0:1]), reads=[gb("cst")], writes=[bdt1])
                        P.emit("scalar", lambda e, Wt=Wt: e.activation(
                            t_dt1[:, 0:Wt], t_dt1[:, 0:Wt], ACT.Ln,
                            bias=t_cst[0:32, 1:2]), reads=[gb("cst")], writes=[bdt1])
                        P.emit("scalar", lambda e, bank=bank, Wt=Wt: e.activation(
                            t_dt2[:, 0:Wt], pb[bank][0:32, 0:Wt], ACT.Relu,
                            bias=t_dtb[:, 0:1]),
                            reads=[bB[bank], gb("dtb")], writes=[bdt2])
                        P.emit("vector", lambda e, Wt=Wt: e.tensor_add(
                            t_dt1[:, 0:Wt], t_dt1[:, 0:Wt], t_dt2[:, 0:Wt]),
                            reads=[bdt2, bB[bank]], writes=[bdt1])
                        P.emit("vector", lambda e, c0=c0, Wt=Wt: e.tensor_scalar(
                            t_ld[:, c0 - PFX: c0 - PFX + Wt], t_dt1[:, 0:Wt],
                            t_negA[:, 0:1], None, ALU.mult),
                            reads=[bdt1, gb("negA")], writes=[bld])
                        continue
                    if is_val:
                        cic = fg // 128
                        if g == -1:
                            P.emit("vector", lambda e, bank=bank, cic=cic: e.tensor_copy(
                                out=t_cvh[:, cic * 4: cic * 4 + 3],
                                in_=pb[bank][:, 125:128]),
                                reads=[bB[bank]], writes=[bcvh[cic]])
                            continue
                        w4 = [t_cw[:, cic * 4 + j: cic * 4 + j + 1] for j in range(4)]
                        hist = t_cvh[:, cic * 4: cic * 4 + 3]
                        P.emit("vector", lambda e, bank=bank, w=w4[0]: e.tensor_scalar(
                            t_cv[:, 3:512], pb[bank][:, 0:509], w, None, ALU.mult),
                            reads=[bB[bank], gb("cw")], writes=[bcv])
                        for j, sl in ((1, (1, 510)), (2, (2, 511)), (3, (3, 512))):
                            P.emit("vector",
                                   lambda e, bank=bank, w=w4[j], sl=sl:
                                   e.scalar_tensor_tensor(
                                       t_cv[:, 3:512], pb[bank][:, sl[0]:sl[1]], w,
                                       t_cv[:, 3:512], ALU.mult, ALU.add),
                                   reads=[bB[bank]], writes=[bcv])
                        P.emit("vector", lambda e, hist=hist, w=w4[0]: e.tensor_scalar(
                            t_cv[:, 0:3], hist, w, None, ALU.mult),
                            reads=[bcvh[cic]], writes=[bcv])
                        P.emit("vector", lambda e, hist=hist, w=w4[1]:
                               e.scalar_tensor_tensor(
                                   t_cv[:, 0:2], hist[:, 1:3], w, t_cv[:, 0:2],
                                   ALU.mult, ALU.add), writes=[bcv])
                        P.emit("vector", lambda e, bank=bank, w=w4[1]:
                               e.scalar_tensor_tensor(
                                   t_cv[:, 2:3], pb[bank][:, 0:1], w, t_cv[:, 2:3],
                                   ALU.mult, ALU.add), reads=[bB[bank]], writes=[bcv])
                        P.emit("vector", lambda e, hist=hist, w=w4[2]:
                               e.scalar_tensor_tensor(
                                   t_cv[:, 0:1], hist[:, 2:3], w, t_cv[:, 0:1],
                                   ALU.mult, ALU.add), writes=[bcv])
                        P.emit("vector", lambda e, bank=bank, w=w4[2]:
                               e.scalar_tensor_tensor(
                                   t_cv[:, 1:3], pb[bank][:, 0:2], w, t_cv[:, 1:3],
                                   ALU.mult, ALU.add), reads=[bB[bank]], writes=[bcv])
                        P.emit("vector", lambda e, bank=bank, w=w4[3]:
                               e.scalar_tensor_tensor(
                                   t_cv[:, 0:3], pb[bank][:, 0:3], w, t_cv[:, 0:3],
                                   ALU.mult, ALU.add), reads=[bB[bank]], writes=[bcv])
                        P.emit("vector", lambda e, bank=bank, cic=cic: e.tensor_copy(
                            out=t_cvh[:, cic * 4: cic * 4 + 3],
                            in_=pb[bank][:, 509:512]),
                            reads=[bB[bank]], writes=[bcvh[cic]])
                        sps = (fp * 20 + g * 4 + fc) % 4
                        P.emit("scalar", lambda e, sps=sps, cic=cic: e.activation(
                            t_sp[sps][:], t_cv[:], ACT.Silu,
                            bias=t_cb[:, cic: cic + 1]),
                            reads=[bcv, gb("cb")], writes=[bsp[sps]])
                        P.dma("sync",
                              d_val.rearrange("(c p) t -> p c t", p=128)[
                                  :, cic, c0 - PFX: c0 - PFX + 512],
                              t_sp[sps][:], reads=[bsp[sps]], writes=[bspill["val"]])
                        _emit_tspill(sps, cic, c0, d_valT, "valT")
                    else:
                        sps = (fp * 20 + g * 4 + fc) % 4
                        if is_gate:
                            dst, cic, nmk = d_gate, (fg - DI) // 128, "gate"
                            P.emit("scalar", lambda e, sps=sps, bank=bank: e.activation(
                                t_sp[sps][:], pb[bank][:], ACT.Silu),
                                reads=[bB[bank]], writes=[bsp[sps]])
                        else:
                            isB = fg < 3 * DI
                            dst = d_B if isB else d_C
                            cic = (fg - 2 * DI) // 128 if isB else (fg - 3 * DI) // 128
                            nmk = "B" if isB else "C"
                            P.emit("scalar", lambda e, sps=sps, bank=bank: e.copy(
                                out=t_sp[sps][:], in_=pb[bank][:]),
                                reads=[bB[bank]], writes=[bsp[sps]])
                        P.dma("sync",
                              dst.rearrange("(c p) t -> p c t", p=128)[
                                  :, cic, c0 - PFX: c0 - PFX + 512],
                              t_sp[sps][:], reads=[bsp[sps]], writes=[bspill[nmk]])
                        if nmk == "B":
                            _emit_tspill(sps, cic, c0, d_BT, "BT")

        if "dbg_ld" in d_dbg:
            P.dma("sync", d_dbg["dbg_ld"][:], t_ld[:], reads=[bld])
        if stop_after == "inproj":
            _final_wait(P)
            P.finalize(ctx.enter_context(nc.Block()))
            return nc, d_dbg

        s1.close()
        P.set_fence(P.fence())
        s2 = ctx.enter_context(ExitStack())

        def sbuf2(name, shape, dt=F32):
            return s2.enter_context(nc.sbuf_tensor(name, shape, dt))

        t_sv = [sbuf2(f"sc_v{i}", [128, 32 * 256], BF16) for i in range(2)]
        t_sB = [sbuf2(f"sc_B{i}", [128, 32 * 256], BF16) for i in range(2)]
        _tC = sbuf2("sc_C0", [128, 32 * 256], BF16)
        t_sC = [_tC, _tC]
        _tG = sbuf2("sc_G0", [128, 32 * 256], BF16)
        t_sG = [_tG, _tG]
        t_svT = sbuf2("sc_vT", [128, DI], BF16)
        t_sBT = sbuf2("sc_BT", [128, DI], BF16)
        bsv = [Buf(), Buf()]
        bsB = [Buf(), Buf()]
        _bC = Buf()
        bsC = [_bC, _bC]
        _bG = Buf()
        bsG = [_bG, _bG]
        bsvT, bsBT = Buf(), Buf()
        t_yg = sbuf2("sc_yg", [128, DI], BF16)
        byg = Buf()
        t_cumt = sbuf2("sc_cumt", [128, 32])
        t_wtm = sbuf2("sc_wtm", [128, 32])
        t_wfm = sbuf2("sc_wfm", [32, 128])
        t_Ebc = sbuf2("sc_Ebc", [128, 32])
        t_cflat = sbuf2("sc_cflat", [1, DI])
        t_lr0 = sbuf2("sc_lr0", [1, 32])
        t_er0 = sbuf2("sc_er0", [1, 32])
        bcumt, bwtm, bwfm, bEbc = Buf(), Buf(), Buf(), Buf()
        bcflat, blr0, ber0 = Buf(), Buf(), Buf()
        t_d2 = [sbuf2(f"sc_d2{i}", [128, 512]) for i in range(2)]

        t_eec = [sbuf2(f"sc_eec{i}", [128, 512]) for i in range(2)]
        t_mt = [sbuf2(f"sc_mt{i}", [128, 512], BF16) for i in range(2)]
        t_ce = [sbuf2(f"sc_ce{i}", [128, 512], BF16) for i in range(2)]
        t_btm = [sbuf2(f"sc_btm{i}", [128, 512], BF16) for i in range(2)]
        t_tmp = [sbuf2(f"sc_tmp{i}", [128, 512], BF16) for i in range(2)]
        t_hw = [sbuf2(f"sc_hw{i}", [128, 512], BF16) for i in range(2)]
        bd2 = [Buf(), Buf()]
        t_ed2 = t_d2
        bed2 = bd2
        beec = [Buf(), Buf()]
        bmt = [Buf(), Buf()]
        bce = [Buf(), Buf()]
        bbtm = [Buf(), Buf()]
        btmp = [Buf(), Buf()]
        bhw = [Buf(), Buf()]
        bcumf, blrow, blpre, blrun = gb("cumf"), gb("lrow"), gb("lpre"), gb("lrun")
        bh = gb("h")
        bygsp = Buf()


        def scan_waveA(c, w, bsvp, bsBp, bsCp):
            p = w % 2
            co = (c % 2) * 128
            h0 = 4 * w
            P.emit("tensor", lambda e, h0=h0, p=p: e.matmul(
                pb[4 + p][:, 0:512], t_ones_r[:, :],
                t_cflat[0:1, h0 * 128:(h0 + 4) * 128], start=True, stop=True),
                reads=[bcflat, gb("ones_r")], writes=[bB[4 + p]],
                deps=[bB[4 + p].w] + bB[4 + p].r)
            for j in range(4):
                hh = h0 + j
                a_B = t_sB[(c // 2) % 2][:, hh * 256 + co: hh * 256 + co + 128]
                a_C = t_sC[(c // 2) % 2][:, hh * 256 + co: hh * 256 + co + 128]
                P.emit("tensor", lambda e, a_B=a_B, a_C=a_C, j=j, p=p: e.matmul(
                    pb[6 + p][:, j * 128:(j + 1) * 128], a_B, a_C,
                    start=(j == 0), stop=(j == 3), skip_group_check=True),
                    reads=[bsBp, bsCp],
                    writes=[bB[6 + p]] if j == 3 else (),
                    deps=([bB[6 + p].w] + bB[6 + p].r) if j == 0 else (),
                    key="pe" if j == 3 else None)

        def scan_waveB(c, w, bsBp, bsCp):
            p = w % 2
            co = (c % 2) * 128
            h0 = 4 * w
            for j in range(4):
                hh = h0 + j
                P.emit("vector", lambda e, hh=hh, j=j, p=p: e.tensor_scalar(
                    t_d2[p][:, j * 128:(j + 1) * 128],
                    pb[4 + p][:, j * 128:(j + 1) * 128],
                    t_cumt[:, hh:hh + 1], 0.0, ALU.subtract, ALU.min),
                    reads=[bB[4 + p], bcumt],
                    writes=[bd2[p]] if j == 3 else [],
                    deps=([bd2[p].w] + bd2[p].r) if j == 0 else ())
            P.emit("scalar", lambda e, p=p: e.activation(
                t_eec[p][:], pb[4 + p][:, 0:512], ACT.Exp),
                reads=[bB[4 + p]], writes=[beec[p]])
            P.emit("scalar", lambda e, p=p: e.activation(
                t_ed2[p][:], t_d2[p][:], ACT.Exp),
                reads=[bd2[p]], writes=[bed2[p]])
            P.emit("vector", lambda e, p=p: e.tensor_mul(
                t_ed2[p][:], t_ed2[p][:], t_mask01[:]),
                reads=[gb("mask01")], writes=[bed2[p]])
            P.emit("vector", lambda e, p=p: e.tensor_mul(
                t_mt[p][:], pb[6 + p][:, 0:512], t_ed2[p][:]),
                reads=[bB[6 + p], bed2[p]], writes=[bmt[p]])
            for j in range(4):
                hh = h0 + j
                a_C = t_sC[(c // 2) % 2][:, hh * 256 + co: hh * 256 + co + 128]
                P.emit("gpsimd", lambda e, a_C=a_C, j=j, p=p: e.tensor_mul(
                    t_ce[p][:, j * 128:(j + 1) * 128], a_C,
                    t_eec[p][:, j * 128:(j + 1) * 128]),
                    reads=[bsCp, beec[p]],
                    writes=[bce[p]] if j == 3 else [],
                    deps=([bce[p].w] + bce[p].r) if j == 0 else ())
            for j in range(4):
                hh = h0 + j
                a_BT = t_sBT[:, hh * 128:(hh + 1) * 128]
                P.emit("vector", lambda e, a_BT=a_BT, hh=hh, j=j, p=p: e.tensor_scalar(
                    t_btm[p][:, j * 128:(j + 1) * 128], a_BT,
                    t_wtm[:, hh:hh + 1], None, ALU.mult),
                    reads=[bsBT, bwtm],
                    writes=[bbtm[p]] if j == 3 else [],
                    deps=([bbtm[p].w] + bbtm[p].r) if j == 0 else ())
            P.emit("scalar", lambda e, p=p, w=w: e.copy(
                out=t_hw[p][:], in_=t_h[:, w * 512:(w + 1) * 512]),
                reads=[bh], writes=[bhw[p]])

        def scan_waveC(c, w):
            p = w % 2
            for j in range(4):
                hh = 4 * w + j
                P.emit("tensor", lambda e, j=j, p=p: e.matmul(
                    pb[0 + p][:, j * 128:(j + 1) * 128],
                    t_hw[p][:, j * 128:(j + 1) * 128],
                    t_ce[p][:, j * 128:(j + 1) * 128],
                    start=(j == 0), stop=False, skip_group_check=True),
                    reads=[bhw[p], bce[p]], writes=[],
                    deps=([bB[0 + p].w] + bB[0 + p].r) if j == 0 else (),
                    key=None)
                a_vT = t_svT[:, hh * 128:(hh + 1) * 128]
                P.emit("tensor", lambda e, a_vT=a_vT, j=j, p=p: e.matmul(
                    pb[0 + p][:, j * 128:(j + 1) * 128], a_vT,
                    t_mt[p][:, j * 128:(j + 1) * 128],
                    start=False, stop=(j == 3), skip_group_check=True),
                    reads=[bsvT, bmt[p]],
                    writes=[bB[0 + p]] if j == 3 else (),
                    key="pe" if j == 3 else None)
            for j in range(4):
                hh = 4 * w + j
                a_vT = t_svT[:, hh * 128:(hh + 1) * 128]
                P.emit("tensor", lambda e, a_vT=a_vT, j=j, p=p: e.matmul(
                    pb[2 + p][:, j * 128:(j + 1) * 128],
                    t_btm[p][:, j * 128:(j + 1) * 128], a_vT,
                    start=(j == 0), stop=(j == 3), skip_group_check=True),
                    reads=[bbtm[p], bsvT],
                    writes=[bB[2 + p]] if j == 3 else (),
                    deps=([bB[2 + p].w] + bB[2 + p].r) if j == 0 else (),
                    key="pe" if j == 3 else None)

        def scan_waveD(c, w, bsvp, bsGp):
            p = w % 2
            co = (c % 2) * 128
            for j in range(4):
                hh = 4 * w + j
                a_v = t_sv[(c // 2) % 2][:, hh * 256 + co: hh * 256 + co + 128]
                P.emit("vector", lambda e, hh=hh, j=j, p=p, a_v=a_v: e.scalar_tensor_tensor(
                    t_tmp[p][:, j * 128:(j + 1) * 128], a_v, t_Dp[:, hh:hh + 1],
                    pb[0 + p][:, j * 128:(j + 1) * 128], ALU.mult, ALU.add),
                    reads=[bsvp, gb("Dp"), bB[0 + p]],
                    writes=[btmp[p]] if j == 3 else [],
                    deps=([btmp[p].w] + btmp[p].r) if j == 0 else ())
            for j in range(4):
                hh = 4 * w + j
                a_G = t_sG[(c // 2) % 2][:, hh * 256 + co: hh * 256 + co + 128]
                P.emit("gpsimd", lambda e, hh=hh, j=j, p=p, a_G=a_G: e.tensor_mul(
                    t_yg[:, hh * 128:(hh + 1) * 128],
                    t_tmp[p][:, j * 128:(j + 1) * 128], a_G),
                    reads=[btmp[p], bsGp], writes=[byg])
            for j in range(4):
                hh = 4 * w + j
                P.emit("vector", lambda e, hh=hh, j=j, p=p: e.scalar_tensor_tensor(
                    t_h[:, hh * 128:(hh + 1) * 128],
                    t_h[:, hh * 128:(hh + 1) * 128], t_Ebc[:, hh:hh + 1],
                    pb[2 + p][:, j * 128:(j + 1) * 128], ALU.mult, ALU.add),
                    reads=[bEbc, bB[2 + p]], writes=[bh])

        NW = NH // 4
        for c in range(NCH):
            pp = (c // 2) % 2
            if c % 2 == 0:
                cp0 = c * CS
                for (dstl, bufl, srcd, nm) in (
                        (t_sv, bsv, d_val, "val"), (t_sB, bsB, d_B, "B"),
                        (t_sC, bsC, d_C, "C"), (t_sG, bsG, d_gate, "gate")):
                    P.dma("sync", dstl[pp][:],
                          srcd.rearrange("(ci p) t -> p ci t", p=128)[
                              :, :, cp0:cp0 + 256],
                          reads=[bspill[nm]], writes=[bufl[pp]])
            bsvp, bsBp, bsCp, bsGp = bsv[pp], bsB[pp], bsC[pp], bsG[pp]
            P.dma("scalar", t_svT[:],
                  d_valT[c * CS:(c + 1) * CS, :],
                  reads=[bspill["valT"]], writes=[bsvT])
            P.dma("scalar", t_sBT[:],
                  d_BT[c * CS:(c + 1) * CS, :],
                  reads=[bspill["BT"]], writes=[bsBT])

            P.emit("vector", lambda e, c=c: e.tensor_tensor_scan(
                t_cumf[:, c * CS:(c + 1) * CS], t_ones_q[:],
                t_ld[:, c * CS:(c + 1) * CS], 0.0, ALU.mult, ALU.add),
                reads=[bld, gb("ones_q")], writes=[bcumf])
            P.dma("sync", t_cflat[:], t_cumf[:, c * CS:(c + 1) * CS],
                  reads=[bcumf], writes=[bcflat])
            P.emit("tensor", lambda e, c=c: e.matmul(
                pb[0][:, 0:32], t_cumf[:, c * CS:(c + 1) * CS],
                t_eye_f[0:32, 0:32], start=True, stop=True,
                skip_group_check=True), reads=[bcumf, gb("eye_f")],
                deps=[bB[0].w] + bB[0].r, writes=[bB[0]])
            P.emit("vector", lambda e: e.tensor_copy(out=t_cumt[:], in_=pb[0][:, 0:32]),
                   reads=[bB[0]], writes=[bcumt])
            P.emit("vector", lambda e: e.tensor_copy(
                out=t_lr0[:].rearrange("one (a u) -> one a u", u=1),
                in_=t_cflat[:].rearrange("one (a b) -> one a b", b=128)[:, :, 127:128]),
                reads=[bcflat], writes=[blr0])
            P.emit("vector", lambda e, c=c: e.tensor_copy(
                out=t_lrow[:, c * NH:(c + 1) * NH], in_=t_lr0[:]),
                reads=[blr0], writes=[blrow])
            P.emit("vector", lambda e, c=c: e.tensor_copy(
                out=t_lpre[:, c * NH:(c + 1) * NH], in_=t_lrun[:]),
                reads=[blrun], writes=[blpre])
            P.emit("vector", lambda e: e.tensor_add(t_lrun[:], t_lrun[:], t_lr0[:]),
                   reads=[blr0], writes=[blrun])
            P.emit("scalar", lambda e, c=c: e.activation(
                t_wfm[:], t_cumf[:, c * CS:(c + 1) * CS], ACT.Exp,
                bias=t_cumf[:, c * CS + CS - 1: c * CS + CS],
                scale=t_cst[0:32, 0:1]),
                reads=[bcumf, gb("cst")], writes=[bwfm])
            P.emit("tensor", lambda e: e.matmul(
                pb[1][:, 0:32], t_wfm[:], t_eye_f[0:32, 0:32],
                start=True, stop=False, skip_group_check=True),
                reads=[bwfm, gb("eye_f")], deps=[bB[1].w] + bB[1].r, writes=[bB[1]])
            P.emit("vector", lambda e: e.tensor_copy(out=t_wtm[:], in_=pb[1][:, 0:32]),
                   reads=[bB[1]], writes=[bwtm])
            P.emit("scalar", lambda e: e.activation(t_er0[:], t_lr0[:], ACT.Exp),
                   reads=[blr0], writes=[ber0])
            P.emit("tensor", lambda e: e.matmul(
                pb[1][:, 32:64], t_ones_r[:, :], t_er0[:], start=False, stop=True,
                skip_group_check=True),
                reads=[ber0, gb("ones_r")], writes=[bB[1]])
            P.emit("vector", lambda e: e.tensor_copy(out=t_Ebc[:], in_=pb[1][:, 32:64]),
                   reads=[bB[1]], writes=[bEbc])

            scan_waveA(c, 0, bsvp, bsBp, bsCp)
            for w in range(NW):
                if w + 1 < NW:
                    scan_waveA(c, w + 1, bsvp, bsBp, bsCp)
                scan_waveB(c, w, bsBp, bsCp)
                scan_waveC(c, w)
                scan_waveD(c, w, bsvp, bsGp)
            P.dma("sync", d_yg.rearrange("(ci p) t -> p ci t", p=128)[
                :, :, c * CS:(c + 1) * CS], t_yg[:], reads=[byg], writes=[bygsp])

        P.dma("gpsimd", d_agi[:], t_h[:], reads=[bh], writes=[gb("dagi")])
        if local_cc:
            P.dma("gpsimd", d_ago[0:128, :], d_agi[:], reads=[gb("dagi")],
                  writes=[gb("dago")])
        else:
            P.emit("gpsimd", lambda e: e.collective_compute(
                "AllGather", ALU.bypass,
                replica_groups=[[0, 1], [2, 3], [4, 5], [6, 7]],
                ins=[d_agi], outs=[d_ago]),
                reads=[gb("dagi")], writes=[gb("dago")])

        if "dbg_hend" in d_dbg:
            P.dma("sync", d_dbg["dbg_hend"][:], t_h[:], reads=[bh])
        if stop_after == "scan":
            _final_wait(P)
            P.finalize(ctx.enter_context(nc.Block()))
            return nc, d_dbg

        s2.close()
        P.set_fence(P.fence())
        s3 = ctx.enter_context(ExitStack())

        def sbuf3(name, shape, dt=F32):
            return s3.enter_context(nc.sbuf_tensor(name, shape, dt))

        t_hinit = sbuf3("h_init", [128, DI])
        t_kC = sbuf3("co_C", [128, DI], BF16)
        t_kG = sbuf3("co_G", [128, DI], BF16)
        t_kY = sbuf3("co_Y", [128, DI], BF16)
        t_cflat2 = sbuf3("co_cflat", [1, DI])
        t_pr0 = sbuf3("co_pr0", [1, 32])
        t_Pb = sbuf3("co_Pb", [128, 32])
        t_eec2 = [sbuf3(f"co_eec{i}", [128, 512]) for i in range(2)]
        t_ce2 = [sbuf3(f"co_ce{i}", [128, 512], BF16) for i in range(2)]
        t_hP = [sbuf3(f"co_hP{i}", [128, 512], BF16) for i in range(2)]
        t_ygc = sbuf3("co_ygc", [128, DI], BF16)
        bkC, bkG, bkY2 = Buf(), Buf(), Buf()
        bcflat2, bpr0, bPb = Buf(), Buf(), Buf()
        beec2 = [Buf(), Buf()]
        bce2 = [Buf(), Buf()]
        bhP = [Buf(), Buf()]
        bygc = Buf()
        bhinit = gb("hinit")
        bygsp2 = Buf()

        P.dma("sync", t_hinit[:], d_ago[0:128, :], reads=[gb("dago")],
              writes=[bhinit])
        P.emit("vector", lambda e: e.tensor_scalar(
            t_hinit[:], t_hinit[:], t_scal[:, 4:5], None, ALU.mult),
            reads=[gb("scal")], writes=[bhinit])

        def corr_waveA(w):
            p = w % 2
            h0 = 4 * w
            P.emit("tensor", lambda e, h0=h0, p=p: e.matmul(
                pb[4 + p][:, 0:512], t_ones_r[:, :],
                t_cflat2[0:1, h0 * 128:(h0 + 4) * 128], start=True, stop=True),
                reads=[bcflat2, gb("ones_r")], writes=[bB[4 + p]],
                deps=[bB[4 + p].w] + bB[4 + p].r)

        def corr_waveB(w):
            p = w % 2
            P.emit("scalar", lambda e, p=p: e.activation(
                t_eec2[p][:], pb[4 + p][:, 0:512], ACT.Exp),
                reads=[bB[4 + p]], writes=[beec2[p]])
            P.emit("gpsimd", lambda e, p=p, w=w: e.tensor_mul(
                t_ce2[p][:], t_kC[:, w * 512:(w + 1) * 512], t_eec2[p][:]),
                reads=[bkC, beec2[p]], writes=[bce2[p]])
            for j in range(4):
                hh = 4 * w + j
                P.emit("vector", lambda e, hh=hh, j=j, p=p: e.tensor_scalar(
                    t_hP[p][:, j * 128:(j + 1) * 128],
                    t_hinit[:, hh * 128:(hh + 1) * 128],
                    t_Pb[:, hh:hh + 1], None, ALU.mult),
                    reads=[bhinit, bPb],
                    writes=[bhP[p]] if j == 3 else [],
                    deps=([bhP[p].w] + bhP[p].r) if j == 0 else ())

        def corr_waveC(w):
            p = w % 2
            for j in range(4):
                P.emit("tensor", lambda e, j=j, p=p: e.matmul(
                    pb[0 + p][:, j * 128:(j + 1) * 128],
                    t_hP[p][:, j * 128:(j + 1) * 128],
                    t_ce2[p][:, j * 128:(j + 1) * 128],
                    start=(j == 0), stop=(j == 3), skip_group_check=True),
                    reads=[bhP[p], bce2[p]],
                    writes=[bB[0 + p]] if j == 3 else (),
                    deps=([bB[0 + p].w] + bB[0 + p].r) if j == 0 else (),
                    key="pe" if j == 3 else None)

        def corr_waveD(w):
            p = w % 2
            P.emit("vector", lambda e, p=p, w=w: e.tensor_mul(
                t_ygc[:, w * 512:(w + 1) * 512], pb[0 + p][:, 0:512],
                t_kG[:, w * 512:(w + 1) * 512]),
                reads=[bB[0 + p], bkG], writes=[bygc])

        for c in range(NCH):
            ccols = (c * CS, (c + 1) * CS)
            P.dma("sync", t_kC[:],
                  d_C.rearrange("(ci p) t -> p ci t", p=128)[:, :, ccols[0]:ccols[1]],
                  reads=[bspill["C"]], writes=[bkC])
            P.dma("scalar", t_kG[:],
                  d_gate.rearrange("(ci p) t -> p ci t", p=128)[:, :, ccols[0]:ccols[1]],
                  reads=[bspill["gate"]], writes=[bkG])
            P.dma("sync", t_kY[:],
                  d_yg.rearrange("(ci p) t -> p ci t", p=128)[:, :, ccols[0]:ccols[1]],
                  reads=[bygsp], writes=[bkY2])
            P.dma("scalar", t_cflat2[:], t_cumf[:, ccols[0]:ccols[1]],
                  reads=[bcumf], writes=[bcflat2])
            P.emit("scalar", lambda e, c=c: e.activation(
                t_pr0[:], t_lpre[:, c * NH:(c + 1) * NH], ACT.Exp),
                reads=[blpre], writes=[bpr0])
            P.emit("tensor", lambda e: e.matmul(
                pb[2][:, 256:288], t_ones_r[:, :], t_pr0[:], start=True, stop=True,
                skip_group_check=True),
                reads=[bpr0, gb("ones_r")], deps=[bB[2].w] + bB[2].r, writes=[bB[2]])
            P.emit("vector", lambda e: e.tensor_copy(out=t_Pb[:], in_=pb[2][:, 256:288]),
                   reads=[bB[2]], writes=[bPb])
            corr_waveA(0)
            for w in range(NH // 4):
                if w + 1 < NH // 4:
                    corr_waveA(w + 1)
                corr_waveB(w)
                corr_waveC(w)
                corr_waveD(w)
            P.emit("vector", lambda e: e.tensor_add(t_kY[:], t_kY[:], t_ygc[:]),
                   reads=[bygc], writes=[bkY2])
            P.dma("sync", d_yg.rearrange("(ci p) t -> p ci t", p=128)[
                :, :, ccols[0]:ccols[1]], t_kY[:], reads=[bkY2], writes=[bygsp2])

        if stop_after == "corr":
            _final_wait(P)
            P.finalize(ctx.enter_context(nc.Block()))
            return nc, d_dbg

        s3.close()
        sA.close()
        P.set_fence(P.fence())
        s4 = ctx.enter_context(ExitStack())

        def sbuf4(name, shape, dt=F32):
            return s4.enter_context(nc.sbuf_tensor(name, shape, dt))

        t_wo = sbuf4("op_wo", [128, 32 * 2048], BF16)
        t_ygi = [sbuf4(f"op_yg{i}", [128, DI], BF16) for i in range(2)]
        t_so = [sbuf4(f"op_so{i}", [128, DM], BF16) for i in range(2)]
        bwo = Buf()
        bygi = [Buf(), Buf()]
        bso = [Buf(), Buf()]
        bssp = Buf()

        for dc in range(8):
            P.dma("gpsimd",
                  t_wo[:].rearrange("p (c m) -> p c m", m=2048)[:, dc * 4:(dc + 1) * 4, :],
                  d_WoutT.rearrange("(c p) m -> p c m", p=128)[:, dc * 4:(dc + 1) * 4, :],
                  writes=[bwo])
        for tc in range(NCH):
            gi, bgi = t_ygi[tc % 2], bygi[tc % 2]
            P.dma("sync", gi[:],
                  d_yg.rearrange("(ci p) t -> p ci t", p=128)[
                      :, :, tc * 128:(tc + 1) * 128],
                  reads=[bygsp2], writes=[bgi])
            for di in range(32):
                for q in range(4):
                    P.emit("tensor", lambda e, gi=gi, di=di, q=q: e.matmul(
                        pb[q][:, 0:512],
                        gi[:, di * 128:(di + 1) * 128],
                        t_wo[:, di * 2048 + q * 512: di * 2048 + (q + 1) * 512],
                        start=(di == 0), stop=(di == 31)),
                        reads=[bgi, bwo] if di == 0 else
                        ([bgi, bwo] if di == 31 else ()),
                        writes=[bB[q]] if di == 31 else (),
                        deps=([bB[q].w] + bB[q].r) if di == 0 else (),
                        key="pe" if di == 31 else None)
            so, bso_ = t_so[tc % 2], bso[tc % 2]
            for q in range(4):
                eng = "vector" if q % 2 == 0 else "scalar"
                if eng == "vector":
                    P.emit("vector", lambda e, so=so, q=q: e.tensor_copy(
                        out=so[:, q * 512:(q + 1) * 512], in_=pb[q][:, 0:512]),
                        reads=[bB[q]], writes=[bso_])
                else:
                    P.emit("scalar", lambda e, so=so, q=q: e.copy(
                        out=so[:, q * 512:(q + 1) * 512], in_=pb[q][:, 0:512]),
                        reads=[bB[q]], writes=[bso_])
            P.dma("sync", d_s[tc * 128:(tc + 1) * 128, :], so[:],
                  reads=[bso_], writes=[bssp])

        if stop_after == "outproj":
            _final_wait(P)
            P.finalize(ctx.enter_context(nc.Block()))
            return nc, d_dbg

        s4.close()
        P.set_fence(P.fence())
        s5 = ctx.enter_context(ExitStack())

        def sbuf5(name, shape, dt=F32):
            return s5.enter_context(nc.sbuf_tensor(name, shape, dt))

        t_qxf = sbuf5("qxf", [128, KD * TOK], BF16)
        t_xq = [sbuf5(f"qx_x{i}", [128, DM]) for i in range(2)]
        t_xnq = sbuf5("qx_xn", [128, DM])
        t_qxb = sbuf5("qx_qb", [128, DM], BF16)
        bqxf = [Buf() for _ in range(NCH)]
        bxq = [Buf(), Buf()]
        bxnq, bqxb = Buf(), Buf()

        for i in range(NCH):
            xb, bx = t_xq[i % 2], bxq[i % 2]
            P.dma("sync", xb[:], d_x[PFX + i * 128: PFX + (i + 1) * 128, :],
                  writes=[bx])
            st2 = 2 * (i + 1)
            P.emit("scalar", lambda e, xb=xb, st2=st2: e.activation(
                t_xnq[:], xb[:], ACT.Identity,
                bias=t_stats[:, st2:st2 + 1], scale=t_stats[:, st2 + 1:st2 + 2]),
                reads=[bx, gb("stats")], writes=[bxnq])
            P.emit("vector", lambda e: e.tensor_scalar(
                t_xnq[:], t_xnq[:], t_scal[:, 0:1], MAGIC, ALU.mult, ALU.add),
                reads=[gb("scal")], writes=[bxnq])
            P.emit("vector", lambda e: e.tensor_scalar(
                t_qxb[:], t_xnq[:], MAGIC, None, ALU.subtract),
                reads=[bxnq], writes=[bqxb])
            for kc in range(KD):
                bank = 6 + kc % 2
                P.emit("tensor", lambda e, kc=kc, bank=bank: e.matmul(
                    pb[bank][:, 0:128], t_qxb[:, kc * 128:(kc + 1) * 128], t_eye_b[:],
                    start=True, stop=True),
                    reads=[bqxb, gb("eye_b")], writes=[bB[bank]])
                if kc % 2 == 0:
                    P.emit("vector", lambda e, kc=kc, i=i, bank=bank: e.tensor_copy(
                        out=t_qxf[:, kc * TOK + i * 128: kc * TOK + (i + 1) * 128],
                        in_=pb[bank][:, 0:128]),
                        reads=[bB[bank]], writes=[bqxf[i]])
                else:
                    P.emit("scalar", lambda e, kc=kc, i=i, bank=bank: e.copy(
                        out=t_qxf[:, kc * TOK + i * 128: kc * TOK + (i + 1) * 128],
                        in_=pb[bank][:, 0:128]),
                        reads=[bB[bank]], writes=[bqxf[i]])

        t_b1 = sbuf5("t_b1c", [128, KI])
        t_sw1 = sbuf5("sw1", [128, KI])
        t_rw1 = sbuf5("rw1", [128, KI])
        t_sxsw1 = sbuf5("sxsw1", [128, KI])
        t_onesq2 = sbuf5("onesq2", [128, 128])
        t_w1m = [sbuf5(f"w1m{i}", [128, DM]) for i in range(2)]
        t_wq = [sbuf5(f"f1_wq{i}", [128, KD * 512]) for i in range(2)]
        t_qw = sbuf5("f1_qw", [128, KD * 512], BF16)
        t_rbc = sbuf5("f1_rbc", [128, 512])
        t_qtmp = sbuf5("f1_qtmp", [128, 512])
        t_hsp = [sbuf5(f"f1_h{i}", [128, 512]) for i in range(4)]
        t_hcol = sbuf5("f1_hc", [128, 1])
        bb1, bsw1, brw1, bsxsw1 = Buf(), Buf(), Buf(), Buf()
        bonesq2 = Buf()
        bw1m = [Buf(), Buf()]
        bwq = [Buf(), Buf()]
        bqw, brbc, bqtmp = Buf(), Buf(), Buf()
        bhsp = [Buf() for _ in range(4)]
        bhcol = Buf()
        bhspill = Buf()

        P.dma("sync", t_b1[:], d_b1[:], writes=[bb1])
        P.emit("vector", lambda e: e.memset(t_onesq2[:], 1.0), writes=[bonesq2])
        for r in range(KI):
            wb, bwb = t_w1m[r % 2], bw1m[r % 2]
            P.dma("scalar", wb[:], d_w1[r * 128:(r + 1) * 128, :], writes=[bwb])
            P.emit("vector", lambda e, wb=wb, r=r: e.reduce_max(
                t_sw1[:, r:r + 1], wb[:], AX.X, apply_absolute_value=True),
                reads=[bwb], writes=[bsw1])
        P.emit("vector", lambda e: e.tensor_scalar(
            t_sw1[:], t_sw1[:], 1.0 / 127.0, 1e-8, ALU.mult, ALU.max),
            writes=[bsw1])
        P.emit("vector", lambda e: e.reciprocal(t_rw1[:], t_sw1[:]),
               reads=[bsw1], writes=[brw1])
        P.emit("vector", lambda e: e.tensor_scalar(
            t_sxsw1[:], t_sw1[:], t_scal[:, 1:2], None, ALU.mult),
            reads=[bsw1, gb("scal")], writes=[bsxsw1])

        for Ip in range(KI // 4):
            f0 = Ip * 512
            wqb, bwqb = t_wq[Ip % 2], bwq[Ip % 2]
            P.dma("gpsimd", wqb[:].rearrange("p (k f) -> p k f", k=KD),
                  d_w1T[:, f0:f0 + 512].rearrange("(k p) f -> p k f", p=128),
                  writes=[bwqb])
            for c4 in range(4):
                P.emit("vector", lambda e, Ip=Ip, c4=c4: e.tensor_scalar(
                    t_qtmp[:, 0:128], t_onesq2[:, 0:128],
                    t_rw1[:, Ip * 4 + c4: Ip * 4 + c4 + 1], None, ALU.mult),
                    reads=[bonesq2, brw1], writes=[bqtmp])
                P.emit("tensor", lambda e: e.matmul(
                    pb[4][:, 0:128], t_qtmp[:, 0:128], t_eye_f[:],
                    start=True, stop=True),
                    reads=[bqtmp, gb("eye_f")], writes=[bB[4]])
                P.emit("vector", lambda e, c4=c4: e.tensor_copy(
                    out=t_rbc[:, c4 * 128:(c4 + 1) * 128], in_=pb[4][:, 0:128]),
                    reads=[bB[4]], writes=[brbc])
            for k in range(KD):
                P.emit("vector", lambda e, k=k, wqb=wqb: e.tensor_mul(
                    t_qtmp[:], wqb[:, k * 512:(k + 1) * 512], t_rbc[:]),
                    reads=[bwqb, brbc], writes=[bqtmp])
                P.emit("vector", lambda e, k=k: e.tensor_scalar(
                    t_qw[:, k * 512:(k + 1) * 512], t_qtmp[:], MAGIC, MAGIC,
                    ALU.add, ALU.subtract),
                    reads=[bqtmp], writes=[bqw])
            for tg in range(4):
                for Ic in range(4):
                    bank = (tg * 4 + Ic) % 4
                    for k in range(KD):
                        P.emit("tensor", lambda e, bank=bank, k=k, Ic=Ic, tg=tg: e.matmul(
                            pb[bank][:, 0:512],
                            t_qw[:, k * 512 + Ic * 128: k * 512 + Ic * 128 + 128],
                            t_qxf[:, k * TOK + tg * 512: k * TOK + (tg + 1) * 512],
                            start=(k == 0), stop=(k == KD - 1)),
                            reads=([bqw] + bqxf) if k == 0 else
                            ([bqw] if k == KD - 1 else ()),
                            writes=[bB[bank]] if k == KD - 1 else (),
                            deps=([bB[bank].w] + bB[bank].r) if k == 0 else (),
                            key="pe" if k == KD - 1 else None)
                    ci = Ip * 4 + Ic
                    sps = (tg * 4 + Ic) % 4
                    P.emit("scalar", lambda e, bank=bank, sps=sps, ci=ci: e.activation(
                        t_hsp[sps][:], pb[bank][:, 0:512], ACT.Gelu,
                        bias=t_b1[:, ci:ci + 1], scale=t_sxsw1[:, ci:ci + 1]),
                        reads=[bB[bank], bb1, bsxsw1], writes=[bhsp[sps]])
                    P.emit("vector", lambda e, sps=sps: e.reduce_max(
                        t_hcol[:], t_hsp[sps][:], AX.X, apply_absolute_value=True),
                        reads=[bhsp[sps]], writes=[bhcol])
                    P.emit("vector", lambda e: e.tensor_max(
                        t_amax[:, 1:2], t_amax[:, 1:2], t_hcol[:]),
                        reads=[bhcol], writes=[gb("amax")])
                    P.dma("sync",
                          d_h.rearrange("(ci p) t -> p ci t", p=128)[
                              :, ci, tg * 512:(tg + 1) * 512],
                          t_hsp[sps][:], reads=[bhsp[sps]], writes=[bhspill])

        P.dma("gpsimd", d_mx2i[:], t_amax[:, 1:2], reads=[gb("amax")],
              writes=[gb("dmx2i")])
        if local_cc:
            P.dma("gpsimd", d_mx2o[:], d_mx2i[:], reads=[gb("dmx2i")],
                  writes=[gb("dmx2o")])
        else:
            P.emit("gpsimd", lambda e: e.collective_compute(
                "AllReduce", ALU.max, replica_groups=[list(range(NCORES))],
                ins=[d_mx2i], outs=[d_mx2o]),
                reads=[gb("dmx2i")], writes=[gb("dmx2o")])
        P.dma("sync", t_r1row[:], d_mx2o[:], reads=[gb("dmx2o")], writes=[gb("r1row")])
        P.emit("vector", lambda e: e.reduce_max(
            t_misc[0:1, 8:9], t_r1row[:], AX.X), reads=[gb("r1row")], writes=[gb("m8")])
        P.emit("vector", lambda e: e.tensor_scalar(
            t_misc[0:1, 9:10], t_misc[0:1, 8:9], 1.0 / 127.0, 1e-8, ALU.mult, ALU.max),
            reads=[gb("m8")], writes=[gb("m9")])
        P.emit("vector", lambda e: e.reciprocal(
            t_misc[0:1, 10:11], t_misc[0:1, 9:10]), reads=[gb("m9")], writes=[gb("m10")])
        P.emit("tensor", lambda e: e.matmul(
            pb[5][:, 8:10], t_ones_r[:, :], t_misc[0:1, 9:11], start=True, stop=True),
            reads=[gb("m9"), gb("m10"), gb("ones_r")], writes=[bB[5]])
        P.emit("vector", lambda e: e.tensor_copy(out=t_scal[:, 3:4], in_=pb[5][:, 8:9]),
               reads=[bB[5]], writes=[gb("scal")])
        P.emit("vector", lambda e: e.tensor_copy(out=t_scal[:, 2:3], in_=pb[5][:, 9:10]),
               reads=[bB[5]], writes=[gb("scal")])

        if stop_after == "ffn1":
            _final_wait(P)
            P.finalize(ctx.enter_context(nc.Block()))
            return nc, d_dbg

        s5.close()
        P.set_fence(P.fence())
        s6 = ctx.enter_context(ExitStack())

        def sbuf6(name, shape, dt=F32):
            return s6.enter_context(nc.sbuf_tensor(name, shape, dt))

        t_qw2 = sbuf6("f2_qw2", [128, 16 * DM], BF16)
        t_w2s = [sbuf6(f"f2_w2s{i}", [128, DM]) for i in range(2)]
        t_hq = [sbuf6(f"f2_hq{i}", [128, DM]) for i in range(2)]
        t_qh = [sbuf6(f"f2_qh{i}", [128, DM], BF16) for i in range(2)]
        t_gp = [sbuf6(f"f2_gp{i}", [128, DM]) for i in range(2)]
        t_xf = [sbuf6(f"f2_x{i}", [128, DM]) for i in range(2)]
        t_sf = [sbuf6(f"f2_s{i}", [128, DM], BF16) for i in range(2)]
        t_of = [sbuf6(f"f2_o{i}", [128, DM]) for i in range(2)]
        t_gtmp = sbuf6("f2_gt", [128, DM])
        t_rb2 = sbuf6("f2_rb2", [128, DM])
        t_sc2 = sbuf6("f2_sc2", [128, DM])
        t_bb2 = sbuf6("f2_bb2", [128, DM])
        t_b2c = sbuf6("f2_b2c", [128, KD])
        t_sw2c = sbuf6("sw2c", [128, KD])
        t_rw2c = sbuf6("f2_rw2c", [128, KD])
        t_sc2c = sbuf6("f2_sc2c", [128, KD])
        t_wcol = sbuf6("f2_wcol", [128, 1])
        t_onesq3 = sbuf6("f2_ones", [128, 128])
        bqw2 = Buf()
        bw2s = [Buf(), Buf()]
        bhq = [Buf(), Buf()]
        bqh = [Buf(), Buf()]
        bgp = [Buf(), Buf()]
        bxf = [Buf(), Buf()]
        bsf = [Buf(), Buf()]
        bof = [Buf(), Buf()]
        bgtmp, brb2, bsc2, bbb2 = Buf(), Buf(), Buf(), Buf()
        bgpsp_tc = [Buf() for _ in range(NCH)]
        bout = Buf()
        bsw2c, bwcol = Buf(), Buf()
        bb2c, brw2c, bsc2c, bonesq3 = Buf(), Buf(), Buf(), Buf()

        P.emit("vector", lambda e: e.memset(t_sw2c[:], 0.0), writes=[bsw2c])
        for r in range(KD):
            for part in range(4):
                wb, bwb = t_hq[part % 2], bhq[part % 2]
                P.dma("scalar", wb[:],
                      d_w2[r * 128:(r + 1) * 128, part * 2048:(part + 1) * 2048],
                      writes=[bwb])
                P.emit("vector", lambda e, wb=wb: e.reduce_max(
                    t_wcol[:], wb[:], AX.X, apply_absolute_value=True),
                    reads=[bwb], writes=[bwcol])
                P.emit("vector", lambda e, r=r: e.tensor_max(
                    t_sw2c[:, r:r + 1], t_sw2c[:, r:r + 1], t_wcol[:]),
                    reads=[bwcol], writes=[bsw2c])
        P.emit("vector", lambda e: e.tensor_scalar(
            t_sw2c[:], t_sw2c[:], 1.0 / 127.0, 1e-8, ALU.mult, ALU.max),
            writes=[bsw2c])
        P.emit("vector", lambda e: e.memset(t_onesq3[:], 1.0), writes=[bonesq3])
        P.dma("sync", t_b2c[:], d_b2[:], writes=[bb2c])
        P.emit("vector", lambda e: e.reciprocal(t_rw2c[:], t_sw2c[:]),
               reads=[bsw2c], writes=[brw2c])
        P.emit("vector", lambda e: e.tensor_scalar(
            t_sc2c[:], t_sw2c[:], t_scal[:, 3:4], None, ALU.mult),
            reads=[bsw2c, gb("scal")], writes=[bsc2c])
        for (colap, bcol, tile, btile) in (
                (t_rw2c, brw2c, t_rb2, brb2),
                (t_sc2c, bsc2c, t_sc2, bsc2),
                (t_b2c, bb2c, t_bb2, bbb2)):
            for ci in range(KD):
                bank = 4 + ci % 2
                P.emit("vector", lambda e, colap=colap, ci=ci: e.tensor_scalar(
                    t_gtmp[:, 0:128], t_onesq3[:], colap[:, ci:ci + 1], None,
                    ALU.mult),
                    reads=[bcol, bonesq3], writes=[bgtmp])
                P.emit("tensor", lambda e, bank=bank: e.matmul(
                    pb[bank][:, 0:128], t_gtmp[:, 0:128], t_eye_f[:],
                    start=True, stop=True),
                    reads=[bgtmp, gb("eye_f")],
                    deps=[bB[bank].w] + bB[bank].r, writes=[bB[bank]])
                P.emit("vector", lambda e, tile=tile, ci=ci, bank=bank: e.tensor_copy(
                    out=tile[:, ci * 128:(ci + 1) * 128], in_=pb[bank][:, 0:128]),
                    reads=[bB[bank]], writes=[btile])

        if "dbg_sc2" in d_dbg:
            P.dma("sync", d_dbg["dbg_sc2"][:], t_sc2[:], reads=[bsc2])
            P.dma("sync", d_dbg["dbg_bb2"][:], t_bb2[:], reads=[bbb2])
            P.dma("sync", d_dbg["dbg_sw2c"][:], t_sw2c[:], reads=[bsw2c])

        for Iq in range(4):
            for k in range(16):
                wb, bwb = t_w2s[k % 2], bw2s[k % 2]
                P.dma("gpsimd", wb[:],
                      d_w2T[Iq * 2048 + k * 128: Iq * 2048 + (k + 1) * 128, :],
                      writes=[bwb])
                P.emit("vector", lambda e, wb=wb: e.tensor_mul(
                    t_gtmp[:], wb[:], t_rb2[:]),
                    reads=[bwb, brb2], writes=[bgtmp])
                P.emit("vector", lambda e, k=k: e.tensor_scalar(
                    t_qw2[:, k * DM:(k + 1) * DM], t_gtmp[:], MAGIC, MAGIC,
                    ALU.add, ALU.subtract),
                    reads=[bgtmp], writes=[bqw2])
            for tc in range(NCH):
                par = tc % 2
                hq, bhq_ = t_hq[par], bhq[par]
                qh, bqh_ = t_qh[par], bqh[par]
                P.dma("sync", hq[:].rearrange("p (c t) -> p c t", t=128),
                      d_h.rearrange("(c p) t -> p c t", p=128)[
                          :, Iq * 16:(Iq + 1) * 16, tc * 128:(tc + 1) * 128],
                      reads=[bhspill], writes=[bhq_])
                P.emit("vector", lambda e, hq=hq: e.tensor_scalar(
                    t_gtmp[:], hq[:], t_scal[:, 2:3], MAGIC, ALU.mult, ALU.add),
                    reads=[bhq_, gb("scal")], writes=[bgtmp])
                P.emit("vector", lambda e, qh=qh: e.tensor_scalar(
                    qh[:], t_gtmp[:], MAGIC, None, ALU.subtract),
                    reads=[bgtmp], writes=[bqh_])
                for k in range(16):
                    for q in range(4):
                        P.emit("tensor", lambda e, qh=qh, k=k, q=q: e.matmul(
                            pb[q][:, 0:512],
                            qh[:, k * 128:(k + 1) * 128],
                            t_qw2[:, k * DM + q * 512: k * DM + (q + 1) * 512],
                            start=(k == 0), stop=(k == 15)),
                            reads=[bqh_, bqw2] if k in (0, 15) else (),
                            writes=[bB[q]] if k == 15 else (),
                            deps=([bB[q].w] + bB[q].r) if k == 0 else (),
                            key="pe" if k == 15 else None)
                gp, bgp_ = t_gp[par], bgp[par]
                if Iq < 3:
                    for q in range(4):
                        if q % 2 == 0:
                            P.emit("vector", lambda e, gp=gp, q=q: e.tensor_copy(
                                out=gp[:, q * 512:(q + 1) * 512], in_=pb[q][:, 0:512]),
                                reads=[bB[q]], writes=[bgp_])
                        else:
                            P.emit("scalar", lambda e, gp=gp, q=q: e.copy(
                                out=gp[:, q * 512:(q + 1) * 512], in_=pb[q][:, 0:512]),
                                reads=[bB[q]], writes=[bgp_])
                    if Iq == 0 or not use_accum_dma:
                        if Iq > 0:
                            P.dma("sync", t_gtmp[:], d_gp[tc * 128:(tc + 1) * 128, :],
                                  reads=[bgpsp_tc[tc]], writes=[bgtmp])
                            P.emit("vector", lambda e, gp=gp: e.tensor_add(
                                gp[:], gp[:], t_gtmp[:]),
                                reads=[bgtmp], writes=[bgp_])
                        P.dma("sync", d_gp[tc * 128:(tc + 1) * 128, :], gp[:],
                              reads=[bgp_], writes=[bgpsp_tc[tc]])
                    else:
                        P.dma("gpsimd", d_gp[tc * 128:(tc + 1) * 128, :], gp[:],
                              reads=[bgp_], writes=[bgpsp_tc[tc]], accum=ALU.add)
                else:
                    xf, bxf_ = t_xf[par], bxf[par]
                    sf, bsf_ = t_sf[par], bsf[par]
                    of, bof_ = t_of[par], bof[par]
                    P.dma("sync", gp[:], d_gp[tc * 128:(tc + 1) * 128, :],
                          reads=[bgpsp_tc[tc]], writes=[bgp_])
                    P.dma("scalar", xf[:], d_x[PFX + tc * 128: PFX + (tc + 1) * 128, :],
                          writes=[bxf_])
                    P.dma("scalar", sf[:], d_s[tc * 128:(tc + 1) * 128, :],
                          reads=[bssp], writes=[bsf_])
                    for q in range(4):
                        P.emit("vector", lambda e, gp=gp, q=q: e.tensor_tensor(
                            gp[:, q * 512:(q + 1) * 512],
                            gp[:, q * 512:(q + 1) * 512], pb[q][:, 0:512], ALU.add),
                            reads=[bB[q]], writes=[bgp_])
                    P.emit("vector", lambda e, gp=gp: e.tensor_mul(
                        t_gtmp[:], gp[:], t_sc2[:]),
                        reads=[bgp_, bsc2], writes=[bgtmp])
                    P.emit("vector", lambda e: e.tensor_add(
                        t_gtmp[:], t_gtmp[:], t_bb2[:]),
                        reads=[bbb2], writes=[bgtmp])
                    if "dbg_acc" in d_dbg:
                        P.dma("gpsimd", d_dbg["dbg_acc"][tc * 128:(tc + 1) * 128, :],
                              gp[:], reads=[bgp_])
                    P.emit("scalar", lambda e: e.activation(
                        t_gtmp[:], t_gtmp[:], ACT.Silu),
                        writes=[bgtmp])
                    if "dbg_g" in d_dbg:
                        P.dma("gpsimd", d_dbg["dbg_g"][tc * 128:(tc + 1) * 128, :],
                              t_gtmp[:], reads=[bgtmp])
                    P.emit("vector", lambda e, sf=sf: e.tensor_mul(
                        t_gtmp[:], t_gtmp[:], sf[:]),
                        reads=[bsf_], writes=[bgtmp])
                    P.emit("vector", lambda e, xf=xf, of=of: e.tensor_add(
                        of[:], t_gtmp[:], xf[:]),
                        reads=[bgtmp, bxf_], writes=[bof_])
                    P.dma("sync", d_out[tc * 128:(tc + 1) * 128, :], of[:],
                          reads=[bof_], writes=[bout])

        s6.close()
        _final_wait(P)
        P.finalize(ctx.enter_context(nc.Block()))

    return nc, d_dbg


def _final_wait(P):
    deps = [P.tok(k) for k in ("pe", "dve", "act", "pool")]
    for qk in ("qsp", "qact", "qpool"):
        for i in range(P.NLANES):
            deps.append(P.tok(f"{qk}{i}"))
    P.emit("sync", lambda e: e.nop(), deps=deps, key=None)


def _stage(inputs):
    x = np.ascontiguousarray(np.asarray(inputs["L0_attn_x"], dtype=np.float32)
                             .reshape(B * L, DM))
    W_in = np.asarray(inputs["L0_attn_W_in"], dtype=np.float32)
    conv_w = np.ascontiguousarray(
        np.asarray(inputs["L0_attn_conv_w"], dtype=np.float32)
        .reshape(32, 128, 4).transpose(1, 0, 2).reshape(128, 32 * 4))
    conv_b = np.ascontiguousarray(
        np.asarray(inputs["L0_attn_conv_b"], dtype=np.float32)
        .reshape(32, 128).T)
    A_log = np.ascontiguousarray(
        np.asarray(inputs["L0_attn_A_log"], dtype=np.float32).reshape(NH, 1))
    Dp = np.ascontiguousarray(
        np.asarray(inputs["L0_attn_D_param"], dtype=np.float32).reshape(32, 128).T)
    dtb = np.ascontiguousarray(
        np.asarray(inputs["L0_attn_dt_bias"], dtype=np.float32).reshape(NH, 1))
    W_out = np.asarray(inputs["L0_attn_W_out"], dtype=np.float32)
    w1 = np.ascontiguousarray(np.asarray(inputs["L0_ffn_w1"], dtype=np.float32))
    b1 = np.ascontiguousarray(
        np.asarray(inputs["L0_ffn_b1"], dtype=np.float32).reshape(KI, 128).T)
    w2 = np.ascontiguousarray(np.asarray(inputs["L0_ffn_w2"], dtype=np.float32))
    b2 = np.ascontiguousarray(
        np.asarray(inputs["L0_ffn_b2"], dtype=np.float32).reshape(KD, 128).T)

    WinT = np.ascontiguousarray(W_in.T)
    WoutT = np.ascontiguousarray(W_out.T)
    w1T = np.ascontiguousarray(w1.T)
    w2T = np.ascontiguousarray(w2.T)

    in_maps = []
    for k in range(NCORES):
        t0 = k * TOK
        xs = np.zeros((TOKA, DM), dtype=np.float32)
        if k % 2 == 1:
            xs[0:PFX] = x[t0 - PFX:t0]
        xs[PFX:] = x[t0:t0 + TOK]
        in_maps.append({
            "x": np.ascontiguousarray(xs),
            "WinT": WinT, "convw": conv_w, "convb": conv_b, "Alog": A_log,
            "Dparam": Dp, "dtb": dtb, "WoutT": WoutT,
            "w1T": w1T, "w1n": w1, "w2T": w2T, "w2n": w2,
            "b1": b1, "b2": b2,
            "isodd": np.array([[1.0 if k % 2 == 1 else 0.0]], dtype=np.float32),
        })
    return in_maps


def kernel(**inputs):
    nc, _ = build()
    in_maps = _stage(inputs)
    res = run_bass_kernel_spmd(nc, in_maps, core_ids=list(range(NCORES)))
    out = np.concatenate([res.results[k]["out"] for k in range(NCORES)], axis=0)
    return out.reshape(B, L, DM)


# version 06 (final)
# speedup vs baseline: 1.1061x; 1.0755x over previous
import numpy as np
from contextlib import ExitStack

import concourse.bass as bass
import concourse.mybir as mybir
from concourse.bass_utils import run_bass_kernel_spmd

F32 = mybir.dt.float32
BF16 = mybir.dt.bfloat16
AX = mybir.AxisListType
ALU = mybir.AluOpType
ACT = mybir.ActivationFunctionType

B, L, DM = 4, 4096, 2048
DI = 4096
NH, HD, NST = 32, 128, 128
DFF = 8192
PROJ = 2 * DI + 2 * NH * NST + NH
NCORES = 8
TOK = 2048
PFX = 128
TOKA = TOK + PFX
CS = 128
NCH = TOK // CS
KD = DM // 128
KI = DFF // 128
EPSLN = 1e-5
MAGIC = float(np.float32(3 * (2 ** 22)))
NEG_BIG = -1.0e30

ENG = ("tensor", "vector", "scalar", "gpsimd", "sync")
EK = {"tensor": "pe", "vector": "dve", "scalar": "act", "gpsimd": "pool", "sync": "sp"}
QK = {"sync": "qsp", "scalar": "qact", "gpsimd": "qpool"}


class Buf:
    __slots__ = ("w", "r", "excl")

    def __init__(self, excl=False):
        self.w = None
        self.r = []
        self.excl = excl


class Prog:

    NLANES = 8

    def __init__(self, nc, ctx):
        self.nc = nc
        self.q = {e: [] for e in ENG}
        self.sem = {}
        self.cnt = {}
        self.waited = {e: {} for e in ENG}
        keys = ["pe", "dve", "act", "pool", "sp"]
        for qk in ("qsp", "qact", "qpool"):
            for i in range(self.NLANES):
                keys.append(f"{qk}{i}")
        for k in keys:
            self.sem[k] = ctx.enter_context(nc.semaphore("sb_" + k))
            self.cnt[k] = 0
        self.dma_i = {"sync": 0, "scalar": 0, "gpsimd": 0}

    def fence(self):
        return [self.tok(k) for k in self.cnt]

    def set_fence(self, toks):
        self._fence = list(toks)

    def emit(self, eng, fn, reads=(), writes=(), deps=(), key="auto", n=1):
        if key == "auto":
            key = EK[eng]
        toks = list(deps)
        fence = getattr(self, "_fence", None)
        if fence is not None:
            for b in writes:
                if b.w is None and not b.r:
                    toks.extend(fence)
                    break
        ek = EK[eng]
        for b in reads:
            toks.append(b.w)
            if b.excl:
                for t in b.r:
                    if t is not None and t[0] != ek:
                        toks.append(t)
        for b in writes:
            toks.append(b.w)
            toks.extend(b.r)
        w = self.waited[eng]
        need = []
        for t in toks:
            if t is None:
                continue
            k, v = t
            if v > 0 and w.get(k, 0) < v:
                need.append((self.sem[k], v))
                w[k] = v
        tok = None
        if key is not None:
            self.cnt[key] += n
            tok = (key, self.cnt[key])
            sem, val = self.sem[key], n
        else:
            sem, val = None, 0
        self.q[eng].append((need, fn, sem, val))
        for b in reads:
            b.r.append(tok)
        for b in writes:
            b.w = tok
            b.r = []
        return tok

    def dma(self, eng, out, in_, reads=(), writes=(), deps=(), slow=False,
            accum=None):
        lane = self.dma_i[eng] % self.NLANES
        self.dma_i[eng] += 1
        key = QK[eng] + str(lane)
        deps = list(deps) + [(key, self.cnt[key])]
        if accum is not None:
            fn = lambda e, o=out, i=in_, a=accum: e.dma_start(out=o, in_=i, accum_op=a)
        else:
            fn = lambda e, o=out, i=in_, s=slow: e.dma_start(
                out=o, in_=i, allow_slow_non_contiguous=s)
        return self.emit(eng, fn,
                         reads=reads, writes=writes, deps=deps, key=key, n=16)

    def tok(self, key):
        return (key, self.cnt[key])

    def finalize(self, block):
        for eng in ENG:
            queue = self.q[eng]
            if not queue:
                continue

            def body(e, queue=queue):
                for need, fn, sem, val in queue:
                    for s, v in need:
                        e.wait_ge(s, v)
                    ins = fn(e)
                    if sem is not None:
                        ins.then_inc(sem, val)
            getattr(block, eng)(body)


def build(debug_taps=(), local_cc=False, stop_after=None, use_accum_dma=False):
    import ml_dtypes
    nc = bass.Bass("TRN2", target_bir_lowering=False)

    d_x = nc.dram_tensor("x", [TOKA, DM], F32, kind="ExternalInput").ap()
    d_WinT = nc.dram_tensor("WinT", [DM, PROJ], F32, kind="ExternalInput").ap()
    d_convw = nc.dram_tensor("convw", [128, 32 * 4], F32, kind="ExternalInput").ap()
    d_convb = nc.dram_tensor("convb", [128, 32], F32, kind="ExternalInput").ap()
    d_Alog = nc.dram_tensor("Alog", [NH, 1], F32, kind="ExternalInput").ap()
    d_D = nc.dram_tensor("Dparam", [128, 32], F32, kind="ExternalInput").ap()
    d_dtb = nc.dram_tensor("dtb", [NH, 1], F32, kind="ExternalInput").ap()
    d_WoutT = nc.dram_tensor("WoutT", [DI, DM], F32, kind="ExternalInput").ap()
    d_w1T = nc.dram_tensor("w1T", [DM, DFF], F32, kind="ExternalInput").ap()
    d_w1 = nc.dram_tensor("w1n", [DFF, DM], F32, kind="ExternalInput").ap()
    d_w2T = nc.dram_tensor("w2T", [DFF, DM], F32, kind="ExternalInput").ap()
    d_w2 = nc.dram_tensor("w2n", [DM, DFF], F32, kind="ExternalInput").ap()
    d_b1 = nc.dram_tensor("b1", [128, KI], F32, kind="ExternalInput").ap()
    d_b2 = nc.dram_tensor("b2", [128, KD], F32, kind="ExternalInput").ap()
    d_isodd = nc.dram_tensor("isodd", [1, 1], F32, kind="ExternalInput").ap()

    d_out = nc.dram_tensor("out", [TOK, DM], F32, kind="ExternalOutput").ap()

    d_val = nc.dram_tensor("sp_val", [DI, TOK], BF16, kind="Internal").ap()
    d_B = nc.dram_tensor("sp_B", [DI, TOK], BF16, kind="Internal").ap()
    d_C = nc.dram_tensor("sp_C", [DI, TOK], BF16, kind="Internal").ap()
    d_gate = nc.dram_tensor("sp_gate", [DI, TOK], BF16, kind="Internal").ap()
    d_yg = nc.dram_tensor("sp_yg", [DI, TOK], BF16, kind="Internal").ap()
    d_h = nc.dram_tensor("sp_h", [DFF, TOK], F32, kind="Internal").ap()
    d_s = nc.dram_tensor("sp_s", [TOK, DM], BF16, kind="Internal").ap()
    d_gp = nc.dram_tensor("sp_gp", [TOK, DM], F32, kind="Internal").ap()
    d_row = nc.dram_tensor("sp_row", [1, DM], F32, kind="Internal").ap()
    d_mx1i = nc.dram_tensor("cc_mx1i", [1, 128], F32, kind="Internal").ap()
    d_mx1o = nc.dram_tensor("cc_mx1o", [1, 128], F32, kind="Internal").ap()
    d_mx2i = nc.dram_tensor("cc_mx2i", [1, 128], F32, kind="Internal").ap()
    d_mx2o = nc.dram_tensor("cc_mx2o", [1, 128], F32, kind="Internal").ap()
    d_agi = nc.dram_tensor("cc_agi", [128, DI], F32, kind="Internal").ap()
    d_ago = nc.dram_tensor("cc_ago", [256, DI], F32, kind="Internal").ap()

    eye_f32 = nc.inline_tensor(np.eye(128, dtype=np.float32), "c_eye_f").ap()
    eye_bf = nc.inline_tensor(
        np.eye(128, dtype=np.float32).astype(ml_dtypes.bfloat16), "c_eye_b").ap()
    mk = np.where(np.arange(128)[None, :] >= np.arange(128)[:, None],
                  np.float32(0.0), np.float32(NEG_BIG)).astype(np.float32)
    d_mask = nc.inline_tensor(mk, "c_mask").ap()
    mk01 = np.tile((np.arange(128)[None, :] >= np.arange(128)[:, None]).astype(np.float32), (1, 4))
    d_mask01 = nc.inline_tensor(np.ascontiguousarray(mk01), "c_mask01").ap()

    DBG = {
        "dbg_xnf": ([128, KD * TOKA], BF16),
        "dbg_stats": ([128, 2 * (TOKA // 128)], F32),
        "dbg_scal": ([128, 8], F32),
        "dbg_ld": ([32, TOK], F32),
        "dbg_hend": ([128, DI], F32),
        "dbg_sw": ([128, 80], F32),
        "dbg_g": ([TOK, DM], F32),
        "dbg_acc": ([TOK, DM], F32),
        "dbg_sc2": ([128, DM], F32),
        "dbg_bb2": ([128, DM], F32),
        "dbg_sw2c": ([128, KD], F32),
    }
    d_dbg = {}
    for nm in debug_taps:
        shape, dt = DBG[nm]
        d_dbg[nm] = nc.dram_tensor(nm, shape, dt, kind="ExternalOutput").ap()

    with ExitStack() as ctx:
        P = Prog(nc, ctx)

        def sbuf(name, shape, dt=F32):
            return ctx.enter_context(nc.sbuf_tensor(name, shape, dt))

        t_eye_f = sbuf("eye_f", [128, 128])
        t_eye_b = sbuf("eye_b", [128, 128], BF16)
        t_mask = sbuf("mask", [128, 128])
        t_mask01 = sbuf("mask01", [128, 512])
        t_ones_r = sbuf("ones_r", [1, 128])
        t_cst = sbuf("cst", [128, 8])
        t_stats = sbuf("stats", [128, 2 * (TOKA // 128)])
        t_amax = sbuf("amax", [128, 2])
        t_scal = sbuf("scal", [128, 8])
        t_r1row = sbuf("r1row", [1, 128])
        t_misc = sbuf("misc", [128, 16])

        sA = ctx.enter_context(ExitStack())

        def sbufA(name, shape, dt=F32):
            return sA.enter_context(nc.sbuf_tensor(name, shape, dt))

        t_ones_q = sbufA("ones_q", [32, 128])
        t_ld = sbufA("ld", [32, TOK])
        t_cumf = sbufA("cumf", [32, TOK])
        t_lrow = sbufA("lrow", [1, NH * NCH])
        t_lpre = sbufA("lpre", [1, NH * NCH])
        t_lrun = sbufA("lrun", [1, NH])
        t_h = sbufA("h_state", [128, DI])
        t_Dp = sbufA("Dp", [128, 32])
        t_cflat = sbufA("cflat", [1, DI])

        pb = [ctx.enter_context(nc.psum_tensor(f"pb{i}", [128, 512], F32))
              for i in range(8)]
        bB = [Buf(excl=True) for _ in range(8)]

        bufs = {}
        def gb(name):
            if name not in bufs:
                bufs[name] = Buf()
            return bufs[name]

        P.dma("sync", t_eye_f[:], eye_f32[:], writes=[gb("eye_f")])
        P.dma("sync", t_eye_b[:], eye_bf[:], writes=[gb("eye_b")])
        P.dma("sync", t_mask[:], d_mask[:], writes=[gb("mask")])
        P.dma("sync", t_mask01[:], d_mask01[:], writes=[gb("mask01")])
        P.dma("sync", t_Dp[:], d_D[:], writes=[gb("Dp")])
        P.emit("vector", lambda e: e.memset(t_scal[:], 0.0), writes=[gb("scal")])
        P.dma("sync", t_scal[0:1, 4:5], d_isodd[:], writes=[gb("scal")])

        P.emit("vector", lambda e: e.memset(t_ones_r[:], 1.0), writes=[gb("ones_r")])
        P.emit("vector", lambda e: e.memset(t_ones_q[:], 1.0), writes=[gb("ones_q")])
        P.emit("vector", lambda e: e.memset(t_cst[:], 0.0), writes=[gb("cst")])
        P.emit("vector", lambda e: e.memset(t_cst[:, 0:1], -1.0),
               writes=[gb("cst")])
        P.emit("vector", lambda e: e.memset(t_cst[:, 1:2], 1.0), writes=[gb("cst")])
        P.emit("vector", lambda e: e.memset(t_cst[:, 2:3], EPSLN), writes=[gb("cst")])
        P.emit("vector", lambda e: e.memset(t_amax[:], 0.0), writes=[gb("amax")])
        P.emit("vector", lambda e: e.memset(t_misc[:], 0.0), writes=[gb("m0"), gb("m1"), gb("m2"), gb("m3"), gb("m4")])
        P.emit("vector", lambda e: e.memset(t_lrun[:], 0.0), writes=[gb("lrun")])
        P.emit("vector", lambda e: e.memset(t_h[:], 0.0), writes=[gb("h")])

        s1 = ctx.enter_context(ExitStack())

        def sbuf1(name, shape, dt=F32):
            return s1.enter_context(nc.sbuf_tensor(name, shape, dt))

        t_cw = sbuf1("t_convw", [128, 32 * 4])
        t_cb = sbuf1("t_convb", [128, 32])
        t_negA = sbuf1("negA", [32, 1])
        t_dtb = sbuf1("dtbc", [32, 1])
        t_cvh = sbuf1("cvh", [128, 32 * 4])
        P.dma("sync", t_cw[:], d_convw[:], writes=[gb("cw")])
        P.dma("sync", t_cb[:], d_convb[:], writes=[gb("cb")])
        P.dma("sync", t_negA[:], d_Alog[:], writes=[gb("negA")])
        P.dma("sync", t_dtb[:], d_dtb[:], writes=[gb("dtb")])
        P.emit("scalar", lambda e: e.activation(t_negA[:], t_negA[:], ACT.Exp),
               reads=[gb("negA")], writes=[gb("negA")])
        P.emit("vector", lambda e: e.tensor_scalar_mul(t_negA[:], t_negA[:], -1.0),
               writes=[gb("negA")])

        NLC = TOKA // 128
        t_xc = [sbuf1(f"ln_x{i}", [128, DM]) for i in range(2)]
        t_bn6 = sbuf1("ln_bn6", [128, 24])
        t_mv = sbuf1("ln_mv", [128, 2])
        t_xn = sbuf1("ln_xn", [128, DM])
        t_xnb = sbuf1("ln_xnb", [128, DM], BF16)
        t_cmax = sbuf1("ln_cmax", [128, 1])
        t_xnf = sbuf1("xnf", [128, KD * TOKA], BF16)

        bxc = [Buf(), Buf()]
        bbn6, bmv, bxn, bxnb, bcmax = Buf(), Buf(), Buf(), Buf(), Buf()
        bstats, bamax = gb("stats"), gb("amax")
        bxnf = [Buf() for _ in range(NLC)]

        for i in range(NLC):
            xb, bx = t_xc[i % 2], bxc[i % 2]
            P.dma("sync", xb[:], d_x[i * 128:(i + 1) * 128, :], writes=[bx])
            for s4 in range(4):
                P.emit("vector", lambda e, s4=s4, xb=xb: e.bn_stats(
                    t_bn6[:, s4 * 6:(s4 + 1) * 6], xb[:, s4 * 512:(s4 + 1) * 512]),
                    reads=[bx], writes=[bbn6])
            P.emit("vector", lambda e: e.bn_aggr(t_mv[:], t_bn6[:]),
                   reads=[bbn6], writes=[bmv])
            P.emit("vector", lambda e: e.tensor_scalar_add(
                t_misc[:, 0:1], t_mv[:, 1:2], t_cst[:, 2:3]),
                reads=[bmv, gb("cst")], writes=[gb("m0")])
            P.emit("scalar", lambda e: e.sqrt(t_misc[:, 1:2], t_misc[:, 0:1]),
                   reads=[gb("m0")], writes=[gb("m1")])
            P.emit("vector", lambda e, i=i: e.reciprocal(
                t_stats[:, 2 * i + 1:2 * i + 2], t_misc[:, 1:2]),
                reads=[gb("m1")], writes=[bstats])
            P.emit("vector", lambda e, i=i: e.tensor_scalar(
                t_stats[:, 2 * i:2 * i + 1], t_mv[:, 0:1],
                t_stats[:, 2 * i + 1:2 * i + 2], -1.0, ALU.mult, ALU.mult),
                reads=[bmv, bstats], writes=[bstats])
            P.emit("scalar", lambda e, i=i, xb=xb: e.activation(
                t_xn[:], xb[:], ACT.Identity,
                bias=t_stats[:, 2 * i:2 * i + 1],
                scale=t_stats[:, 2 * i + 1:2 * i + 2]),
                reads=[bx, bstats], writes=[bxn])
            P.emit("vector", lambda e: e.reduce_max(
                t_cmax[:], t_xn[:], AX.X, apply_absolute_value=True),
                reads=[bxn], writes=[bcmax])
            P.emit("vector", lambda e: e.tensor_max(
                t_amax[:, 0:1], t_amax[:, 0:1], t_cmax[:]),
                reads=[bcmax], writes=[bamax])
            P.emit("scalar", lambda e: e.copy(out=t_xnb[:], in_=t_xn[:]),
                   reads=[bxn], writes=[bxnb])
            for kc in range(KD):
                bank = 6 + kc % 2
                P.emit("tensor", lambda e, kc=kc, bank=bank: e.matmul(
                    pb[bank][:, 0:128], t_xnb[:, kc * 128:(kc + 1) * 128], t_eye_b[:],
                    start=True, stop=True),
                    reads=[bxnb, gb("eye_b")], writes=[bB[bank]])
                if kc % 2 == 0:
                    P.emit("vector", lambda e, kc=kc, i=i, bank=bank: e.tensor_copy(
                        out=t_xnf[:, kc * TOKA + i * 128: kc * TOKA + (i + 1) * 128],
                        in_=pb[bank][:, 0:128]),
                        reads=[bB[bank]], writes=[bxnf[i]])
                else:
                    P.emit("scalar", lambda e, kc=kc, i=i, bank=bank: e.copy(
                        out=t_xnf[:, kc * TOKA + i * 128: kc * TOKA + (i + 1) * 128],
                        in_=pb[bank][:, 0:128]),
                        reads=[bB[bank]], writes=[bxnf[i]])

        P.dma("gpsimd", d_mx1i[:], t_amax[:, 0:1], reads=[bamax], writes=[gb("dmx1i")])
        if local_cc:
            P.dma("gpsimd", d_mx1o[:], d_mx1i[:], reads=[gb("dmx1i")],
                  writes=[gb("dmx1o")])
        else:
            P.emit("gpsimd", lambda e: e.collective_compute(
                "AllReduce", ALU.max, replica_groups=[list(range(NCORES))],
                ins=[d_mx1i], outs=[d_mx1o]),
                reads=[gb("dmx1i")], writes=[gb("dmx1o")])
        P.dma("sync", t_r1row[:], d_mx1o[:], reads=[gb("dmx1o")], writes=[gb("r1row")])
        P.emit("vector", lambda e: e.reduce_max(
            t_misc[0:1, 2:3], t_r1row[:], AX.X), reads=[gb("r1row")], writes=[gb("m2")])
        P.emit("vector", lambda e: e.tensor_scalar(
            t_misc[0:1, 3:4], t_misc[0:1, 2:3], 1.0 / 127.0, 1e-8, ALU.mult, ALU.max),
            reads=[gb("m2")], writes=[gb("m3")])
        P.emit("vector", lambda e: e.reciprocal(
            t_misc[0:1, 4:5], t_misc[0:1, 3:4]), reads=[gb("m3")], writes=[gb("m4")])
        P.emit("tensor", lambda e: e.matmul(
            pb[5][:, 0:2], t_ones_r[:, :], t_misc[0:1, 3:5], start=True, stop=True),
            reads=[gb("m3"), gb("m4"), gb("ones_r")], writes=[bB[5]])
        bscal = gb("scal")
        P.emit("vector", lambda e: e.tensor_copy(out=t_scal[:, 1:2], in_=pb[5][:, 0:1]),
               reads=[bB[5]], writes=[bscal])
        P.emit("vector", lambda e: e.tensor_copy(out=t_scal[:, 0:1], in_=pb[5][:, 1:2]),
               reads=[bB[5]], writes=[bscal])
        P.emit("tensor", lambda e: e.matmul(
            pb[5][:, 4:5], t_ones_r[:, :], t_scal[0:1, 4:5], start=True, stop=True),
            reads=[gb("scal"), gb("ones_r")], writes=[bB[5]])
        P.emit("vector", lambda e: e.tensor_copy(out=t_scal[:, 4:5], in_=pb[5][:, 4:5]),
               reads=[bB[5]], writes=[bscal])

        if "dbg_stats" in d_dbg:
            P.dma("sync", d_dbg["dbg_stats"][:], t_stats[:], reads=[bstats])
        if "dbg_xnf" in d_dbg:
            P.dma("sync", d_dbg["dbg_xnf"][:], t_xnf[:], reads=bxnf)
        if "dbg_scal" in d_dbg:
            P.dma("sync", d_dbg["dbg_scal"][:], t_scal[:], reads=[bscal])
        if stop_after == "ln":
            _final_wait(P)
            P.finalize(ctx.enter_context(nc.Block()))
            return nc, d_dbg

        t_wp = [sbuf1(f"ip_wp{i}", [128, KD * 512], BF16) for i in range(2)]
        t_cv = sbuf1("ip_cv", [128, 512])
        t_sp = [sbuf1(f"ip_sp{i}", [128, 512], BF16) for i in range(4)]
        t_dt1 = sbuf1("ip_dt1", [32, 512])
        t_dt2 = sbuf1("ip_dt2", [32, 512])

        bwp = [Buf(), Buf()]
        bcv, bdt1, bdt2 = Buf(), Buf(), Buf()
        bsp = [Buf() for _ in range(4)]
        bcvh = [Buf() for _ in range(32)]
        bld = gb("ld")
        bspill = {"val": Buf(), "B": Buf(), "C": Buf(), "gate": Buf()}

        NPAN = PROJ // 512 + 1
        for fp in range(NPAN):
            f0 = fp * 512
            W = 512 if fp < 32 else 32
            pbuf, bwpc = t_wp[fp % 2], bwp[fp % 2]
            P.dma("gpsimd", pbuf[:, 0:KD * W].rearrange("p (k f) -> p k f", k=KD),
                  d_WinT[:, f0:f0 + W].rearrange("(k p) f -> p k f", p=128),
                  writes=[bwpc])
            is_val = f0 < DI
            is_gate = DI <= f0 < 2 * DI
            is_dt = fp == 32
            groups = ([(-1, 0, 128)] if is_val else []) + \
                     [(g, PFX + g * 512, 512) for g in range(4)]
            for (g, c0, Wt) in groups:
                nfc = 4 if not is_dt else 1
                for fc in range(nfc):
                    fg = f0 + fc * 128
                    bank = 2 + ((fp * 24 + (g + 1) * 4 + fc) % 2)
                    Mt = 128 if not is_dt else 32
                    lnreads = [bxnf[ci] for ci in
                               range(c0 // 128, (c0 + Wt + 127) // 128)]
                    for k in range(KD):
                        P.emit("tensor",
                               lambda e, bank=bank, Mt=Mt, Wt=Wt, k=k, fc=fc, W=W,
                               c0=c0, pbuf=pbuf: e.matmul(
                                   pb[bank][0:Mt, 0:Wt],
                                   pbuf[:, k * W + fc * 128: k * W + fc * 128 + Mt],
                                   t_xnf[:, k * TOKA + c0: k * TOKA + c0 + Wt],
                                   start=(k == 0), stop=(k == KD - 1)),
                               reads=([bwpc] + lnreads) if k == 0
                               else ([bwpc] if k == KD - 1 else ()),
                               writes=[bB[bank]] if k == KD - 1 else (),
                               deps=[bB[bank].w] + [t for t in bB[bank].r]
                               if k == 0 else (),
                               key="pe" if k == KD - 1 else None)
                    if is_dt:
                        P.emit("scalar", lambda e, bank=bank, Wt=Wt: e.activation(
                            t_dt1[:, 0:Wt], pb[bank][0:32, 0:Wt], ACT.Abs,
                            bias=t_dtb[:, 0:1]),
                            reads=[bB[bank], gb("dtb")], writes=[bdt1])
                        P.emit("scalar", lambda e, Wt=Wt: e.activation(
                            t_dt1[:, 0:Wt], t_dt1[:, 0:Wt], ACT.Exp,
                            scale=t_cst[0:32, 0:1]), reads=[gb("cst")], writes=[bdt1])
                        P.emit("scalar", lambda e, Wt=Wt: e.activation(
                            t_dt1[:, 0:Wt], t_dt1[:, 0:Wt], ACT.Ln,
                            bias=t_cst[0:32, 1:2]), reads=[gb("cst")], writes=[bdt1])
                        P.emit("scalar", lambda e, bank=bank, Wt=Wt: e.activation(
                            t_dt2[:, 0:Wt], pb[bank][0:32, 0:Wt], ACT.Relu,
                            bias=t_dtb[:, 0:1]),
                            reads=[bB[bank], gb("dtb")], writes=[bdt2])
                        P.emit("vector", lambda e, Wt=Wt: e.tensor_add(
                            t_dt1[:, 0:Wt], t_dt1[:, 0:Wt], t_dt2[:, 0:Wt]),
                            reads=[bdt2, bB[bank]], writes=[bdt1])
                        P.emit("vector", lambda e, c0=c0, Wt=Wt: e.tensor_scalar(
                            t_ld[:, c0 - PFX: c0 - PFX + Wt], t_dt1[:, 0:Wt],
                            t_negA[:, 0:1], None, ALU.mult),
                            reads=[bdt1, gb("negA")], writes=[bld])
                        continue
                    if is_val:
                        cic = fg // 128
                        if g == -1:
                            P.emit("vector", lambda e, bank=bank, cic=cic: e.tensor_copy(
                                out=t_cvh[:, cic * 4: cic * 4 + 3],
                                in_=pb[bank][:, 125:128]),
                                reads=[bB[bank]], writes=[bcvh[cic]])
                            continue
                        w4 = [t_cw[:, cic * 4 + j: cic * 4 + j + 1] for j in range(4)]
                        hist = t_cvh[:, cic * 4: cic * 4 + 3]
                        P.emit("vector", lambda e, bank=bank, w=w4[0]: e.tensor_scalar(
                            t_cv[:, 3:512], pb[bank][:, 0:509], w, None, ALU.mult),
                            reads=[bB[bank], gb("cw")], writes=[bcv])
                        for j, sl in ((1, (1, 510)), (2, (2, 511)), (3, (3, 512))):
                            P.emit("vector",
                                   lambda e, bank=bank, w=w4[j], sl=sl:
                                   e.scalar_tensor_tensor(
                                       t_cv[:, 3:512], pb[bank][:, sl[0]:sl[1]], w,
                                       t_cv[:, 3:512], ALU.mult, ALU.add),
                                   reads=[bB[bank]], writes=[bcv])
                        P.emit("vector", lambda e, hist=hist, w=w4[0]: e.tensor_scalar(
                            t_cv[:, 0:3], hist, w, None, ALU.mult),
                            reads=[bcvh[cic]], writes=[bcv])
                        P.emit("vector", lambda e, hist=hist, w=w4[1]:
                               e.scalar_tensor_tensor(
                                   t_cv[:, 0:2], hist[:, 1:3], w, t_cv[:, 0:2],
                                   ALU.mult, ALU.add), writes=[bcv])
                        P.emit("vector", lambda e, bank=bank, w=w4[1]:
                               e.scalar_tensor_tensor(
                                   t_cv[:, 2:3], pb[bank][:, 0:1], w, t_cv[:, 2:3],
                                   ALU.mult, ALU.add), reads=[bB[bank]], writes=[bcv])
                        P.emit("vector", lambda e, hist=hist, w=w4[2]:
                               e.scalar_tensor_tensor(
                                   t_cv[:, 0:1], hist[:, 2:3], w, t_cv[:, 0:1],
                                   ALU.mult, ALU.add), writes=[bcv])
                        P.emit("vector", lambda e, bank=bank, w=w4[2]:
                               e.scalar_tensor_tensor(
                                   t_cv[:, 1:3], pb[bank][:, 0:2], w, t_cv[:, 1:3],
                                   ALU.mult, ALU.add), reads=[bB[bank]], writes=[bcv])
                        P.emit("vector", lambda e, bank=bank, w=w4[3]:
                               e.scalar_tensor_tensor(
                                   t_cv[:, 0:3], pb[bank][:, 0:3], w, t_cv[:, 0:3],
                                   ALU.mult, ALU.add), reads=[bB[bank]], writes=[bcv])
                        P.emit("vector", lambda e, bank=bank, cic=cic: e.tensor_copy(
                            out=t_cvh[:, cic * 4: cic * 4 + 3],
                            in_=pb[bank][:, 509:512]),
                            reads=[bB[bank]], writes=[bcvh[cic]])
                        sps = (fp * 20 + g * 4 + fc) % 4
                        P.emit("scalar", lambda e, sps=sps, cic=cic: e.activation(
                            t_sp[sps][:], t_cv[:], ACT.Silu,
                            bias=t_cb[:, cic: cic + 1]),
                            reads=[bcv, gb("cb")], writes=[bsp[sps]])
                        P.dma("sync",
                              d_val.rearrange("(c p) t -> p c t", p=128)[
                                  :, cic, c0 - PFX: c0 - PFX + 512],
                              t_sp[sps][:], reads=[bsp[sps]], writes=[bspill["val"]])
                    else:
                        sps = (fp * 20 + g * 4 + fc) % 4
                        if is_gate:
                            dst, cic, nmk = d_gate, (fg - DI) // 128, "gate"
                            P.emit("scalar", lambda e, sps=sps, bank=bank: e.activation(
                                t_sp[sps][:], pb[bank][:], ACT.Silu),
                                reads=[bB[bank]], writes=[bsp[sps]])
                        else:
                            isB = fg < 3 * DI
                            dst = d_B if isB else d_C
                            cic = (fg - 2 * DI) // 128 if isB else (fg - 3 * DI) // 128
                            nmk = "B" if isB else "C"
                            P.emit("scalar", lambda e, sps=sps, bank=bank: e.copy(
                                out=t_sp[sps][:], in_=pb[bank][:]),
                                reads=[bB[bank]], writes=[bsp[sps]])
                        P.dma("sync",
                              dst.rearrange("(c p) t -> p c t", p=128)[
                                  :, cic, c0 - PFX: c0 - PFX + 512],
                              t_sp[sps][:], reads=[bsp[sps]], writes=[bspill[nmk]])

        if "dbg_ld" in d_dbg:
            P.dma("sync", d_dbg["dbg_ld"][:], t_ld[:], reads=[bld])
        if stop_after == "inproj":
            _final_wait(P)
            P.finalize(ctx.enter_context(nc.Block()))
            return nc, d_dbg

        s1.close()
        P.set_fence(P.fence())
        s2 = ctx.enter_context(ExitStack())

        def sbuf2(name, shape, dt=F32):
            return s2.enter_context(nc.sbuf_tensor(name, shape, dt))

        t_sv = [sbuf2(f"sc_v{i}", [128, 32 * 256], BF16) for i in range(2)]
        t_sB = [sbuf2(f"sc_B{i}", [128, 32 * 256], BF16) for i in range(2)]
        t_sC = [sbuf2(f"sc_C{i}", [128, 32 * 256], BF16) for i in range(2)]
        t_sG = [sbuf2(f"sc_G{i}", [128, 32 * 256], BF16) for i in range(2)]
        bsv = [Buf(), Buf()]
        bsB = [Buf(), Buf()]
        bsC = [Buf(), Buf()]
        bsG = [Buf(), Buf()]
        t_yg = sbuf2("sc_yg", [128, DI], BF16)
        byg = Buf()
        t_cumt = sbuf2("sc_cumt", [128, 32])
        t_wtm = sbuf2("sc_wtm", [128, 32])
        t_wfm = sbuf2("sc_wfm", [32, 128])
        t_Ebc = sbuf2("sc_Ebc", [128, 32])
        t_lr0 = sbuf2("sc_lr0", [1, 32])
        t_er0 = sbuf2("sc_er0", [1, 32])
        bcumt, bwtm, bwfm, bEbc = Buf(), Buf(), Buf(), Buf()
        bcflat, blr0, ber0 = Buf(), Buf(), Buf()
        t_d2 = [sbuf2(f"sc_d2{i}", [128, 256]) for i in range(2)]
        t_ed2 = [sbuf2(f"sc_ed2{i}", [128, 256]) for i in range(2)]
        t_eec = [sbuf2(f"sc_eec{i}", [128, 256]) for i in range(2)]
        t_mt = [sbuf2(f"sc_mt{i}", [128, 256], BF16) for i in range(2)]
        t_ce = [sbuf2(f"sc_ce{i}", [128, 256], BF16) for i in range(2)]
        t_vtm = [sbuf2(f"sc_vtm{i}", [128, 256], BF16) for i in range(2)]
        t_btm = [sbuf2(f"sc_btm{i}", [128, 256], BF16) for i in range(2)]
        t_tmp = [sbuf2(f"sc_tmp{i}", [128, 256]) for i in range(2)]
        t_hw = [sbuf2(f"sc_hw{i}", [128, 256], BF16) for i in range(2)]
        bd2 = [Buf(), Buf()]
        bed2 = [Buf(), Buf()]
        beec = [Buf(), Buf()]
        bmt = [Buf(), Buf()]
        bce = [Buf(), Buf()]
        bvtm = [Buf(), Buf()]
        bbtm = [Buf(), Buf()]
        btmp = [Buf(), Buf()]
        bhw = [Buf(), Buf()]
        bcumf, blrow, blpre, blrun = gb("cumf"), gb("lrow"), gb("lpre"), gb("lrun")
        bh = gb("h")
        bygsp = Buf()


        def scan_waveA(c, w, bsvp, bsBp, bsCp):
            p = w % 2
            co = (c % 2) * 128
            h0 = 2 * w
            P.emit("tensor", lambda e, h0=h0, p=p: e.matmul(
                pb[4 + p][:, 0:256], t_ones_r[:, :],
                t_cflat[0:1, h0 * 128:(h0 + 2) * 128], start=True, stop=False,
                skip_group_check=True),
                reads=[bcflat, gb("ones_r")],
                writes=(),
                deps=[bB[4 + p].w] + bB[4 + p].r,
                key=None)
            for j in range(2):
                hh = 2 * w + j
                a_B = t_sB[(c // 2) % 2][:, hh * 256 + co: hh * 256 + co + 128]
                a_C = t_sC[(c // 2) % 2][:, hh * 256 + co: hh * 256 + co + 128]
                P.emit("tensor", lambda e, a_B=a_B, a_C=a_C, j=j, p=p: e.matmul(
                    pb[4 + p][:, 256 + j * 128: 256 + (j + 1) * 128], a_B, a_C,
                    start=False, stop=(j == 1), skip_group_check=True),
                    reads=[bsBp, bsCp],
                    writes=[bB[4 + p]] if j == 1 else (),
                    key="pe" if j == 1 else None)
            for j in range(2):
                hh = 2 * w + j
                a_v = t_sv[(c // 2) % 2][:, hh * 256 + co: hh * 256 + co + 128]
                P.emit("tensor", lambda e, a_v=a_v, j=j, p=p: e.matmul(
                    pb[6 + p][:, j * 128:(j + 1) * 128], a_v, t_eye_b[:],
                    start=(j == 0), stop=False, skip_group_check=True),
                    reads=[bsvp, gb("eye_b")],
                    writes=(),
                    deps=([bB[6 + p].w] + bB[6 + p].r) if j == 0 else (),
                    key=None)
            for j in range(2):
                hh = 2 * w + j
                a_B = t_sB[(c // 2) % 2][:, hh * 256 + co: hh * 256 + co + 128]
                P.emit("tensor", lambda e, a_B=a_B, j=j, p=p: e.matmul(
                    pb[6 + p][:, 256 + j * 128: 256 + (j + 1) * 128], a_B, t_eye_b[:],
                    start=False, stop=(j == 1), skip_group_check=True),
                    reads=[bsBp, gb("eye_b")],
                    writes=[bB[6 + p]] if j == 1 else (),
                    key="pe" if j == 1 else None)

        def scan_waveB(c, w):
            p = w % 2
            for j in range(2):
                hh = 2 * w + j
                P.emit("vector", lambda e, hh=hh, j=j, p=p: e.tensor_scalar(
                    t_d2[p][:, j * 128:(j + 1) * 128],
                    pb[4 + p][:, j * 128:(j + 1) * 128],
                    t_cumt[:, hh:hh + 1], 0.0, ALU.subtract, ALU.min),
                    reads=[bB[4 + p], bcumt], writes=[bd2[p]] if j == 1 else [],
                    deps=([bd2[p].w] + bd2[p].r) if j == 0 else ())
            P.emit("scalar", lambda e, p=p: e.activation(
                t_ed2[p][:], t_d2[p][:], ACT.Exp),
                reads=[bd2[p]], writes=[bed2[p]])
            P.emit("scalar", lambda e, p=p: e.activation(
                t_eec[p][:], pb[4 + p][:, 0:256], ACT.Exp),
                reads=[bB[4 + p]], writes=[beec[p]])
            P.emit("vector", lambda e, p=p: e.tensor_mul(
                t_ed2[p][:], t_ed2[p][:], t_mask01[:, 0:256]),
                reads=[gb("mask01")], writes=[bed2[p]])
            P.emit("vector", lambda e, p=p: e.tensor_mul(
                t_mt[p][:], pb[4 + p][:, 256:512], t_ed2[p][:]),
                reads=[bB[4 + p], bed2[p]], writes=[bmt[p]])
            co = (c % 2) * 128
            w0 = 2 * w
            a_C2 = t_sC[(c // 2) % 2][:, w0 * 256 + co: w0 * 256 + co + 128]
            a_C2b = t_sC[(c // 2) % 2][:, (w0 + 1) * 256 + co: (w0 + 1) * 256 + co + 128]
            P.emit("gpsimd", lambda e, p=p, a_C2=a_C2: e.tensor_mul(
                t_ce[p][:, 0:128], a_C2, t_eec[p][:, 0:128]),
                reads=[bsC[(c // 2) % 2], beec[p]], writes=[bce[p]])
            P.emit("gpsimd", lambda e, p=p, a_C2b=a_C2b: e.tensor_mul(
                t_ce[p][:, 128:256], a_C2b, t_eec[p][:, 128:256]),
                reads=[bsC[(c // 2) % 2], beec[p]], writes=[bce[p]])
            P.emit("scalar", lambda e, p=p: e.copy(
                out=t_vtm[p][:], in_=pb[6 + p][:, 0:256]),
                reads=[bB[6 + p]], writes=[bvtm[p]])
            P.emit("scalar", lambda e, p=p, w=w: e.copy(
                out=t_hw[p][:], in_=t_h[:, w * 256:(w + 1) * 256]),
                reads=[bh], writes=[bhw[p]])
            for j in range(2):
                hh = 2 * w + j
                P.emit("vector", lambda e, hh=hh, j=j, p=p: e.tensor_scalar(
                    t_btm[p][:, j * 128:(j + 1) * 128],
                    pb[6 + p][:, 256 + j * 128: 256 + (j + 1) * 128],
                    t_wtm[:, hh:hh + 1], None, ALU.mult),
                    reads=[bB[6 + p], bwtm], writes=[bbtm[p]] if j == 1 else [],
                    deps=([bbtm[p].w] + bbtm[p].r) if j == 0 else ())

        def scan_waveC(c, w):
            p = w % 2
            for j in range(2):
                hh = 2 * w + j
                P.emit("tensor", lambda e, hh=hh, j=j, p=p: e.matmul(
                    pb[0 + p][:, j * 128:(j + 1) * 128],
                    t_hw[p][:, j * 128:(j + 1) * 128],
                    t_ce[p][:, j * 128:(j + 1) * 128],
                    start=(j == 0), stop=False, skip_group_check=True),
                    reads=[bhw[p], bce[p]], writes=[],
                    deps=([bB[0 + p].w] + bB[0 + p].r) if j == 0 else (),
                    key=None)
                P.emit("tensor", lambda e, j=j, p=p: e.matmul(
                    pb[0 + p][:, j * 128:(j + 1) * 128],
                    t_vtm[p][:, j * 128:(j + 1) * 128],
                    t_mt[p][:, j * 128:(j + 1) * 128],
                    start=False, stop=(j == 1), skip_group_check=True),
                    reads=[bvtm[p], bmt[p]],
                    writes=[bB[0 + p]] if j == 1 else (),
                    key="pe" if j == 1 else None)
            for j in range(2):
                P.emit("tensor", lambda e, j=j, p=p: e.matmul(
                    pb[2 + p][:, j * 128:(j + 1) * 128],
                    t_btm[p][:, j * 128:(j + 1) * 128],
                    t_vtm[p][:, j * 128:(j + 1) * 128],
                    start=(j == 0), stop=(j == 1), skip_group_check=True),
                    reads=[bbtm[p], bvtm[p]],
                    writes=[bB[2 + p]] if j == 1 else (),
                    deps=([bB[2 + p].w] + bB[2 + p].r) if j == 0 else (),
                    key="pe" if j == 1 else None)

        def scan_waveD(c, w, bsvp, bsGp):
            p = w % 2
            co = (c % 2) * 128
            for j in range(2):
                hh = 2 * w + j
                a_v = t_sv[(c // 2) % 2][:, hh * 256 + co: hh * 256 + co + 128]
                P.emit("vector", lambda e, hh=hh, j=j, p=p, a_v=a_v: e.scalar_tensor_tensor(
                    t_tmp[p][:, j * 128:(j + 1) * 128], a_v, t_Dp[:, hh:hh + 1],
                    pb[0 + p][:, j * 128:(j + 1) * 128], ALU.mult, ALU.add),
                    reads=[bsvp, gb("Dp"), bB[0 + p]],
                    writes=[btmp[p]] if j == 1 else [],
                    deps=([btmp[p].w] + btmp[p].r) if j == 0 else ())
            w0 = 2 * w
            a_G2 = t_sG[(c // 2) % 2][:, w0 * 256 + co: w0 * 256 + co + 128]
            a_G2b = t_sG[(c // 2) % 2][:, (w0 + 1) * 256 + co: (w0 + 1) * 256 + co + 128]
            P.emit("gpsimd", lambda e, p=p, w0=w0, a_G2=a_G2: e.tensor_mul(
                t_yg[:, w0 * 128:(w0 + 1) * 128], t_tmp[p][:, 0:128], a_G2),
                reads=[btmp[p], bsGp], writes=[byg])
            P.emit("gpsimd", lambda e, p=p, w0=w0, a_G2b=a_G2b: e.tensor_mul(
                t_yg[:, (w0 + 1) * 128:(w0 + 2) * 128], t_tmp[p][:, 128:256], a_G2b),
                reads=[btmp[p], bsGp], writes=[byg])
            for j in range(2):
                hh = 2 * w + j
                P.emit("vector", lambda e, hh=hh, j=j, p=p: e.scalar_tensor_tensor(
                    t_h[:, hh * 128:(hh + 1) * 128],
                    t_h[:, hh * 128:(hh + 1) * 128], t_Ebc[:, hh:hh + 1],
                    pb[2 + p][:, j * 128:(j + 1) * 128], ALU.mult, ALU.add),
                    reads=[bEbc, bB[2 + p]], writes=[bh])

        NW = NH // 2
        for c in range(NCH):
            pp = (c // 2) % 2
            if c % 2 == 0:
                cp0 = c * CS
                for (dstl, bufl, srcd, nm) in (
                        (t_sv, bsv, d_val, "val"), (t_sB, bsB, d_B, "B"),
                        (t_sC, bsC, d_C, "C"), (t_sG, bsG, d_gate, "gate")):
                    P.dma("sync", dstl[pp][:],
                          srcd.rearrange("(ci p) t -> p ci t", p=128)[
                              :, :, cp0:cp0 + 256],
                          reads=[bspill[nm]], writes=[bufl[pp]])
            bsvp, bsBp, bsCp, bsGp = bsv[pp], bsB[pp], bsC[pp], bsG[pp]

            P.emit("vector", lambda e, c=c: e.tensor_tensor_scan(
                t_cumf[:, c * CS:(c + 1) * CS], t_ones_q[:],
                t_ld[:, c * CS:(c + 1) * CS], 0.0, ALU.mult, ALU.add),
                reads=[bld, gb("ones_q")], writes=[bcumf])
            P.dma("sync", t_cflat[:], t_cumf[:, c * CS:(c + 1) * CS],
                  reads=[bcumf], writes=[bcflat])
            P.emit("tensor", lambda e, c=c: e.matmul(
                pb[2][:, 256:288], t_cumf[:, c * CS:(c + 1) * CS],
                t_eye_f[0:32, 0:32], start=True, stop=True,
                skip_group_check=True), reads=[bcumf, gb("eye_f")],
                deps=[bB[2].w] + bB[2].r, writes=[bB[2]])
            P.emit("vector", lambda e: e.tensor_copy(out=t_cumt[:], in_=pb[2][:, 256:288]),
                   reads=[bB[2]], writes=[bcumt])
            P.emit("vector", lambda e: e.tensor_copy(
                out=t_lr0[:].rearrange("one (a u) -> one a u", u=1),
                in_=t_cflat[:].rearrange("one (a b) -> one a b", b=128)[:, :, 127:128]),
                reads=[bcflat], writes=[blr0])
            P.emit("vector", lambda e, c=c: e.tensor_copy(
                out=t_lrow[:, c * NH:(c + 1) * NH], in_=t_lr0[:]),
                reads=[blr0], writes=[blrow])
            P.emit("vector", lambda e, c=c: e.tensor_copy(
                out=t_lpre[:, c * NH:(c + 1) * NH], in_=t_lrun[:]),
                reads=[blrun], writes=[blpre])
            P.emit("vector", lambda e: e.tensor_add(t_lrun[:], t_lrun[:], t_lr0[:]),
                   reads=[blr0], writes=[blrun])
            P.emit("scalar", lambda e, c=c: e.activation(
                t_wfm[:], t_cumf[:, c * CS:(c + 1) * CS], ACT.Exp,
                bias=t_cumf[:, c * CS + CS - 1: c * CS + CS],
                scale=t_cst[0:32, 0:1]),
                reads=[bcumf, gb("cst")], writes=[bwfm])
            P.emit("tensor", lambda e: e.matmul(
                pb[3][:, 256:288], t_wfm[:], t_eye_f[0:32, 0:32],
                start=True, stop=False, skip_group_check=True),
                reads=[bwfm, gb("eye_f")], deps=[bB[3].w] + bB[3].r, writes=[bB[3]])
            P.emit("vector", lambda e: e.tensor_copy(out=t_wtm[:], in_=pb[3][:, 256:288]),
                   reads=[bB[3]], writes=[bwtm])
            P.emit("scalar", lambda e: e.activation(t_er0[:], t_lr0[:], ACT.Exp),
                   reads=[blr0], writes=[ber0])
            P.emit("tensor", lambda e: e.matmul(
                pb[3][:, 288:320], t_ones_r[:, :], t_er0[:], start=False, stop=True,
                skip_group_check=True),
                reads=[ber0, gb("ones_r")], writes=[bB[3]])
            P.emit("vector", lambda e: e.tensor_copy(out=t_Ebc[:], in_=pb[3][:, 288:320]),
                   reads=[bB[3]], writes=[bEbc])

            scan_waveA(c, 0, bsvp, bsBp, bsCp)
            for w in range(NW):
                if w + 1 < NW:
                    scan_waveA(c, w + 1, bsvp, bsBp, bsCp)
                scan_waveB(c, w)
                scan_waveC(c, w)
                scan_waveD(c, w, bsvp, bsGp)
            P.dma("sync", d_yg.rearrange("(ci p) t -> p ci t", p=128)[
                :, :, c * CS:(c + 1) * CS], t_yg[:], reads=[byg], writes=[bygsp])

        P.dma("gpsimd", d_agi[:], t_h[:], reads=[bh], writes=[gb("dagi")])
        if local_cc:
            P.dma("gpsimd", d_ago[0:128, :], d_agi[:], reads=[gb("dagi")],
                  writes=[gb("dago")])
        else:
            P.emit("gpsimd", lambda e: e.collective_compute(
                "AllGather", ALU.bypass,
                replica_groups=[[0, 1], [2, 3], [4, 5], [6, 7]],
                ins=[d_agi], outs=[d_ago]),
                reads=[gb("dagi")], writes=[gb("dago")])

        if "dbg_hend" in d_dbg:
            P.dma("sync", d_dbg["dbg_hend"][:], t_h[:], reads=[bh])
        if stop_after == "scan":
            _final_wait(P)
            P.finalize(ctx.enter_context(nc.Block()))
            return nc, d_dbg

        s2.close()
        P.set_fence(P.fence())
        s35 = ctx.enter_context(ExitStack())
        t_wo_a = s35.enter_context(nc.sbuf_tensor("op_wo_a", [128, 16 * 2048], BF16))
        bwo = Buf()
        for dc in range(4):
            P.dma("gpsimd",
                  t_wo_a[:].rearrange("p (c m) -> p c m", m=2048)[:, dc * 4:(dc + 1) * 4, :],
                  d_WoutT.rearrange("(c p) m -> p c m", p=128)[:, dc * 4:(dc + 1) * 4, :],
                  writes=[bwo])
        s3 = ctx.enter_context(ExitStack())

        def sbuf3(name, shape, dt=F32):
            return s3.enter_context(nc.sbuf_tensor(name, shape, dt))

        t_hinit = sbuf3("h_init", [128, DI])
        t_kC = sbuf3("co_C", [128, DI], BF16)
        t_kG = sbuf3("co_G", [128, DI], BF16)
        t_kY = sbuf3("co_Y", [128, DI], BF16)
        t_pr0 = sbuf3("co_pr0", [1, 32])
        t_Pb = sbuf3("co_Pb", [128, 32])
        t_eec2 = [sbuf3(f"co_eec{i}", [128, 512]) for i in range(2)]
        t_ce2 = [sbuf3(f"co_ce{i}", [128, 512], BF16) for i in range(2)]
        t_hP = [sbuf3(f"co_hP{i}", [128, 512], BF16) for i in range(2)]
        t_u = sbuf3("co_u", [128, 512], BF16)
        bkC, bkG, bkY2 = Buf(), Buf(), Buf()
        bpr0, bPb = Buf(), Buf()
        beec2 = [Buf(), Buf()]
        bce2 = [Buf(), Buf()]
        bhP = [Buf(), Buf()]
        bu = Buf()
        bhinit = gb("hinit")
        bygsp2 = Buf()

        P.dma("sync", t_hinit[:], d_ago[0:128, :], reads=[gb("dago")],
              writes=[bhinit])
        P.emit("vector", lambda e: e.tensor_scalar(
            t_hinit[:], t_hinit[:], t_scal[:, 4:5], None, ALU.mult),
            reads=[gb("scal")], writes=[bhinit])

        def corr_waveA(w):
            p = w % 2
            h0 = 4 * w
            P.emit("tensor", lambda e, h0=h0, p=p: e.matmul(
                pb[4 + p][:, 0:512], t_ones_r[:, :],
                t_cflat[0:1, h0 * 128:(h0 + 4) * 128], start=True, stop=True),
                reads=[bcflat, gb("ones_r")], writes=[bB[4 + p]],
                deps=[bB[4 + p].w] + bB[4 + p].r)

        def corr_waveB(w):
            p = w % 2
            P.emit("scalar", lambda e, p=p: e.activation(
                t_eec2[p][:], pb[4 + p][:, 0:512], ACT.Exp),
                reads=[bB[4 + p]], writes=[beec2[p]])
            P.emit("gpsimd", lambda e, p=p, w=w: e.tensor_mul(
                t_ce2[p][:], t_kC[:, w * 512:(w + 1) * 512], t_eec2[p][:]),
                reads=[bkC, beec2[p]], writes=[bce2[p]])
            for j in range(4):
                hh = 4 * w + j
                P.emit("vector", lambda e, hh=hh, j=j, p=p: e.tensor_scalar(
                    t_hP[p][:, j * 128:(j + 1) * 128],
                    t_hinit[:, hh * 128:(hh + 1) * 128],
                    t_Pb[:, hh:hh + 1], None, ALU.mult),
                    reads=[bhinit, bPb],
                    writes=[bhP[p]] if j == 3 else [],
                    deps=([bhP[p].w] + bhP[p].r) if j == 0 else ())

        def corr_waveC(w):
            p = w % 2
            for j in range(4):
                P.emit("tensor", lambda e, j=j, p=p: e.matmul(
                    pb[0 + p][:, j * 128:(j + 1) * 128],
                    t_hP[p][:, j * 128:(j + 1) * 128],
                    t_ce2[p][:, j * 128:(j + 1) * 128],
                    start=(j == 0), stop=(j == 3), skip_group_check=True),
                    reads=[bhP[p], bce2[p]],
                    writes=[bB[0 + p]] if j == 3 else (),
                    deps=([bB[0 + p].w] + bB[0 + p].r) if j == 0 else (),
                    key="pe" if j == 3 else None)

        def corr_waveD(w):
            p = w % 2
            P.emit("vector", lambda e, p=p, w=w: e.tensor_mul(
                t_u[:], pb[0 + p][:, 0:512],
                t_kG[:, w * 512:(w + 1) * 512]),
                reads=[bB[0 + p], bkG], writes=[bu])
            P.emit("vector", lambda e, w=w: e.tensor_add(
                t_kY[:, w * 512:(w + 1) * 512],
                t_kY[:, w * 512:(w + 1) * 512], t_u[:]),
                reads=[bu], writes=[bkY2])

        for c in range(NCH):
            ccols = (c * CS, (c + 1) * CS)
            P.dma("sync", t_kC[:],
                  d_C.rearrange("(ci p) t -> p ci t", p=128)[:, :, ccols[0]:ccols[1]],
                  reads=[bspill["C"]], writes=[bkC])
            P.dma("scalar", t_kG[:],
                  d_gate.rearrange("(ci p) t -> p ci t", p=128)[:, :, ccols[0]:ccols[1]],
                  reads=[bspill["gate"]], writes=[bkG])
            P.dma("sync", t_kY[:],
                  d_yg.rearrange("(ci p) t -> p ci t", p=128)[:, :, ccols[0]:ccols[1]],
                  reads=[bygsp], writes=[bkY2])
            P.dma("scalar", t_cflat[:], t_cumf[:, ccols[0]:ccols[1]],
                  reads=[bcumf], writes=[bcflat])
            P.emit("scalar", lambda e, c=c: e.activation(
                t_pr0[:], t_lpre[:, c * NH:(c + 1) * NH], ACT.Exp),
                reads=[blpre], writes=[bpr0])
            P.emit("tensor", lambda e: e.matmul(
                pb[2][:, 256:288], t_ones_r[:, :], t_pr0[:], start=True, stop=True,
                skip_group_check=True),
                reads=[bpr0, gb("ones_r")], deps=[bB[2].w] + bB[2].r, writes=[bB[2]])
            P.emit("vector", lambda e: e.tensor_copy(out=t_Pb[:], in_=pb[2][:, 256:288]),
                   reads=[bB[2]], writes=[bPb])
            corr_waveA(0)
            for w in range(NH // 4):
                if w + 1 < NH // 4:
                    corr_waveA(w + 1)
                corr_waveB(w)
                corr_waveC(w)
                corr_waveD(w)
            P.dma("sync", d_yg.rearrange("(ci p) t -> p ci t", p=128)[
                :, :, ccols[0]:ccols[1]], t_kY[:], reads=[bkY2], writes=[bygsp2])

        if stop_after == "corr":
            _final_wait(P)
            P.finalize(ctx.enter_context(nc.Block()))
            return nc, d_dbg

        s3.close()
        P.set_fence(P.fence())
        s4 = ctx.enter_context(ExitStack())

        def sbuf4(name, shape, dt=F32):
            return s4.enter_context(nc.sbuf_tensor(name, shape, dt))

        t_wo_b = sbuf4("op_wo_b", [128, 16 * 2048], BF16)
        bwo_b = Buf()
        for dc in range(4, 8):
            P.dma("gpsimd",
                  t_wo_b[:].rearrange("p (c m) -> p c m", m=2048)[
                      :, dc * 4 - 16:(dc + 1) * 4 - 16, :],
                  d_WoutT.rearrange("(c p) m -> p c m", p=128)[:, dc * 4:(dc + 1) * 4, :],
                  writes=[bwo_b])
        t_ygi = [sbuf4(f"op_yg{i}", [128, DI], BF16) for i in range(2)]
        _so = sbuf4("op_so0", [128, DM], BF16)
        t_so = [_so, _so]
        bygi = [Buf(), Buf()]
        _bso = Buf()
        bso = [_bso, _bso]
        bssp = Buf()

        for tc in range(NCH):
            gi, bgi = t_ygi[tc % 2], bygi[tc % 2]
            P.dma("sync", gi[:],
                  d_yg.rearrange("(ci p) t -> p ci t", p=128)[
                      :, :, tc * 128:(tc + 1) * 128],
                  reads=[bygsp2], writes=[bgi])
            for di in range(32):
                wo_t = t_wo_a if di < 16 else t_wo_b
                dio = di if di < 16 else di - 16
                for q in range(4):
                    P.emit("tensor", lambda e, gi=gi, wo_t=wo_t, di=di, dio=dio, q=q: e.matmul(
                        pb[q][:, 0:512],
                        gi[:, di * 128:(di + 1) * 128],
                        wo_t[:, dio * 2048 + q * 512: dio * 2048 + (q + 1) * 512],
                        start=(di == 0), stop=(di == 31)),
                        reads=[bgi, bwo, bwo_b] if di in (0, 31) else (),
                        writes=[bB[q]] if di == 31 else (),
                        deps=([bB[q].w] + bB[q].r) if di == 0 else (),
                        key="pe" if di == 31 else None)
            so, bso_ = t_so[tc % 2], bso[tc % 2]
            for q in range(4):
                eng = "vector" if q % 2 == 0 else "scalar"
                if eng == "vector":
                    P.emit("vector", lambda e, so=so, q=q: e.tensor_copy(
                        out=so[:, q * 512:(q + 1) * 512], in_=pb[q][:, 0:512]),
                        reads=[bB[q]], writes=[bso_])
                else:
                    P.emit("scalar", lambda e, so=so, q=q: e.copy(
                        out=so[:, q * 512:(q + 1) * 512], in_=pb[q][:, 0:512]),
                        reads=[bB[q]], writes=[bso_])
            P.dma("sync", d_s[tc * 128:(tc + 1) * 128, :], so[:],
                  reads=[bso_], writes=[bssp])

        if stop_after == "outproj":
            _final_wait(P)
            P.finalize(ctx.enter_context(nc.Block()))
            return nc, d_dbg

        s4.close()
        s35.close()
        sA.close()
        P.set_fence(P.fence())
        s5 = ctx.enter_context(ExitStack())

        def sbuf5(name, shape, dt=F32):
            return s5.enter_context(nc.sbuf_tensor(name, shape, dt))

        t_qxf = sbuf5("qxf", [128, KD * TOK], BF16)
        t_xq = [sbuf5(f"qx_x{i}", [128, DM]) for i in range(2)]
        t_xnq = sbuf5("qx_xn", [128, DM])
        t_qxb = sbuf5("qx_qb", [128, DM], BF16)
        bqxf = [Buf() for _ in range(NCH)]
        bxq = [Buf(), Buf()]
        bxnq, bqxb = Buf(), Buf()

        for i in range(NCH):
            xb, bx = t_xq[i % 2], bxq[i % 2]
            P.dma("sync", xb[:], d_x[PFX + i * 128: PFX + (i + 1) * 128, :],
                  writes=[bx])
            st2 = 2 * (i + 1)
            P.emit("scalar", lambda e, xb=xb, st2=st2: e.activation(
                t_xnq[:], xb[:], ACT.Identity,
                bias=t_stats[:, st2:st2 + 1], scale=t_stats[:, st2 + 1:st2 + 2]),
                reads=[bx, gb("stats")], writes=[bxnq])
            P.emit("vector", lambda e: e.tensor_scalar(
                t_xnq[:], t_xnq[:], t_scal[:, 0:1], MAGIC, ALU.mult, ALU.add),
                reads=[gb("scal")], writes=[bxnq])
            P.emit("vector", lambda e: e.tensor_scalar(
                t_qxb[:], t_xnq[:], MAGIC, None, ALU.subtract),
                reads=[bxnq], writes=[bqxb])
            for kc in range(KD):
                bank = 6 + kc % 2
                P.emit("tensor", lambda e, kc=kc, bank=bank: e.matmul(
                    pb[bank][:, 0:128], t_qxb[:, kc * 128:(kc + 1) * 128], t_eye_b[:],
                    start=True, stop=True),
                    reads=[bqxb, gb("eye_b")], writes=[bB[bank]])
                if kc % 2 == 0:
                    P.emit("vector", lambda e, kc=kc, i=i, bank=bank: e.tensor_copy(
                        out=t_qxf[:, kc * TOK + i * 128: kc * TOK + (i + 1) * 128],
                        in_=pb[bank][:, 0:128]),
                        reads=[bB[bank]], writes=[bqxf[i]])
                else:
                    P.emit("scalar", lambda e, kc=kc, i=i, bank=bank: e.copy(
                        out=t_qxf[:, kc * TOK + i * 128: kc * TOK + (i + 1) * 128],
                        in_=pb[bank][:, 0:128]),
                        reads=[bB[bank]], writes=[bqxf[i]])

        t_b1 = sbuf5("t_b1c", [128, KI])
        t_sw1 = sbuf5("sw1", [128, KI])
        t_rw1 = sbuf5("rw1", [128, KI])
        t_sxsw1 = sbuf5("sxsw1", [128, KI])
        t_onesq2 = sbuf5("onesq2", [128, 128])
        t_w1m = [sbuf5(f"w1m{i}", [128, DM]) for i in range(2)]
        t_wq = [sbuf5(f"f1_wq{i}", [128, KD * 512]) for i in range(2)]
        t_qw = sbuf5("f1_qw", [128, KD * 512], BF16)
        t_rbc = sbuf5("f1_rbc", [128, 512])
        t_qtmp = sbuf5("f1_qtmp", [128, 512])
        t_hsp = [sbuf5(f"f1_h{i}", [128, 512]) for i in range(4)]
        t_hcol = sbuf5("f1_hc", [128, 1])
        bb1, bsw1, brw1, bsxsw1 = Buf(), Buf(), Buf(), Buf()
        bonesq2 = Buf()
        bw1m = [Buf(), Buf()]
        bwq = [Buf(), Buf()]
        bqw, brbc, bqtmp = Buf(), Buf(), Buf()
        bhsp = [Buf() for _ in range(4)]
        bhcol = Buf()
        bhspill = Buf()

        P.dma("sync", t_b1[:], d_b1[:], writes=[bb1])
        P.emit("vector", lambda e: e.memset(t_onesq2[:], 1.0), writes=[bonesq2])
        for r in range(KI):
            wb, bwb = t_w1m[r % 2], bw1m[r % 2]
            P.dma("scalar", wb[:], d_w1[r * 128:(r + 1) * 128, :], writes=[bwb])
            P.emit("vector", lambda e, wb=wb, r=r: e.reduce_max(
                t_sw1[:, r:r + 1], wb[:], AX.X, apply_absolute_value=True),
                reads=[bwb], writes=[bsw1])
        P.emit("vector", lambda e: e.tensor_scalar(
            t_sw1[:], t_sw1[:], 1.0 / 127.0, 1e-8, ALU.mult, ALU.max),
            writes=[bsw1])
        P.emit("vector", lambda e: e.reciprocal(t_rw1[:], t_sw1[:]),
               reads=[bsw1], writes=[brw1])
        P.emit("vector", lambda e: e.tensor_scalar(
            t_sxsw1[:], t_sw1[:], t_scal[:, 1:2], None, ALU.mult),
            reads=[bsw1, gb("scal")], writes=[bsxsw1])

        for Ip in range(KI // 4):
            f0 = Ip * 512
            wqb, bwqb = t_wq[Ip % 2], bwq[Ip % 2]
            P.dma("gpsimd", wqb[:].rearrange("p (k f) -> p k f", k=KD),
                  d_w1T[:, f0:f0 + 512].rearrange("(k p) f -> p k f", p=128),
                  writes=[bwqb])
            for c4 in range(4):
                P.emit("vector", lambda e, Ip=Ip, c4=c4: e.tensor_scalar(
                    t_qtmp[:, 0:128], t_onesq2[:, 0:128],
                    t_rw1[:, Ip * 4 + c4: Ip * 4 + c4 + 1], None, ALU.mult),
                    reads=[bonesq2, brw1], writes=[bqtmp])
                P.emit("tensor", lambda e: e.matmul(
                    pb[4][:, 0:128], t_qtmp[:, 0:128], t_eye_f[:],
                    start=True, stop=True),
                    reads=[bqtmp, gb("eye_f")], writes=[bB[4]])
                P.emit("vector", lambda e, c4=c4: e.tensor_copy(
                    out=t_rbc[:, c4 * 128:(c4 + 1) * 128], in_=pb[4][:, 0:128]),
                    reads=[bB[4]], writes=[brbc])
            for k in range(KD):
                P.emit("vector", lambda e, k=k, wqb=wqb: e.tensor_mul(
                    t_qtmp[:], wqb[:, k * 512:(k + 1) * 512], t_rbc[:]),
                    reads=[bwqb, brbc], writes=[bqtmp])
                P.emit("vector", lambda e, k=k: e.tensor_scalar(
                    t_qw[:, k * 512:(k + 1) * 512], t_qtmp[:], MAGIC, MAGIC,
                    ALU.add, ALU.subtract),
                    reads=[bqtmp], writes=[bqw])
            for tg in range(4):
                for Ic in range(4):
                    bank = (tg * 4 + Ic) % 4
                    for k in range(KD):
                        P.emit("tensor", lambda e, bank=bank, k=k, Ic=Ic, tg=tg: e.matmul(
                            pb[bank][:, 0:512],
                            t_qw[:, k * 512 + Ic * 128: k * 512 + Ic * 128 + 128],
                            t_qxf[:, k * TOK + tg * 512: k * TOK + (tg + 1) * 512],
                            start=(k == 0), stop=(k == KD - 1)),
                            reads=([bqw] + bqxf) if k == 0 else
                            ([bqw] if k == KD - 1 else ()),
                            writes=[bB[bank]] if k == KD - 1 else (),
                            deps=([bB[bank].w] + bB[bank].r) if k == 0 else (),
                            key="pe" if k == KD - 1 else None)
                    ci = Ip * 4 + Ic
                    sps = (tg * 4 + Ic) % 4
                    P.emit("scalar", lambda e, bank=bank, sps=sps, ci=ci: e.activation(
                        t_hsp[sps][:], pb[bank][:, 0:512], ACT.Gelu,
                        bias=t_b1[:, ci:ci + 1], scale=t_sxsw1[:, ci:ci + 1]),
                        reads=[bB[bank], bb1, bsxsw1], writes=[bhsp[sps]])
                    P.emit("vector", lambda e, sps=sps: e.reduce_max(
                        t_hcol[:], t_hsp[sps][:], AX.X, apply_absolute_value=True),
                        reads=[bhsp[sps]], writes=[bhcol])
                    P.emit("vector", lambda e: e.tensor_max(
                        t_amax[:, 1:2], t_amax[:, 1:2], t_hcol[:]),
                        reads=[bhcol], writes=[gb("amax")])
                    P.dma("sync",
                          d_h.rearrange("(ci p) t -> p ci t", p=128)[
                              :, ci, tg * 512:(tg + 1) * 512],
                          t_hsp[sps][:], reads=[bhsp[sps]], writes=[bhspill])

        P.dma("gpsimd", d_mx2i[:], t_amax[:, 1:2], reads=[gb("amax")],
              writes=[gb("dmx2i")])
        if local_cc:
            P.dma("gpsimd", d_mx2o[:], d_mx2i[:], reads=[gb("dmx2i")],
                  writes=[gb("dmx2o")])
        else:
            P.emit("gpsimd", lambda e: e.collective_compute(
                "AllReduce", ALU.max, replica_groups=[list(range(NCORES))],
                ins=[d_mx2i], outs=[d_mx2o]),
                reads=[gb("dmx2i")], writes=[gb("dmx2o")])
        P.dma("sync", t_r1row[:], d_mx2o[:], reads=[gb("dmx2o")], writes=[gb("r1row")])
        P.emit("vector", lambda e: e.reduce_max(
            t_misc[0:1, 8:9], t_r1row[:], AX.X), reads=[gb("r1row")], writes=[gb("m8")])
        P.emit("vector", lambda e: e.tensor_scalar(
            t_misc[0:1, 9:10], t_misc[0:1, 8:9], 1.0 / 127.0, 1e-8, ALU.mult, ALU.max),
            reads=[gb("m8")], writes=[gb("m9")])
        P.emit("vector", lambda e: e.reciprocal(
            t_misc[0:1, 10:11], t_misc[0:1, 9:10]), reads=[gb("m9")], writes=[gb("m10")])
        P.emit("tensor", lambda e: e.matmul(
            pb[5][:, 8:10], t_ones_r[:, :], t_misc[0:1, 9:11], start=True, stop=True),
            reads=[gb("m9"), gb("m10"), gb("ones_r")], writes=[bB[5]])
        P.emit("vector", lambda e: e.tensor_copy(out=t_scal[:, 3:4], in_=pb[5][:, 8:9]),
               reads=[bB[5]], writes=[gb("scal")])
        P.emit("vector", lambda e: e.tensor_copy(out=t_scal[:, 2:3], in_=pb[5][:, 9:10]),
               reads=[bB[5]], writes=[gb("scal")])

        if stop_after == "ffn1":
            _final_wait(P)
            P.finalize(ctx.enter_context(nc.Block()))
            return nc, d_dbg

        s5.close()
        P.set_fence(P.fence())
        s6 = ctx.enter_context(ExitStack())

        def sbuf6(name, shape, dt=F32):
            return s6.enter_context(nc.sbuf_tensor(name, shape, dt))

        t_qw2 = sbuf6("f2_qw2", [128, 16 * DM], BF16)
        t_w2s = [sbuf6(f"f2_w2s{i}", [128, DM]) for i in range(2)]
        t_hq = [sbuf6(f"f2_hq{i}", [128, DM]) for i in range(2)]
        t_qh = [sbuf6(f"f2_qh{i}", [128, DM], BF16) for i in range(2)]
        t_gp = [sbuf6(f"f2_gp{i}", [128, DM]) for i in range(2)]
        t_xf = [sbuf6(f"f2_x{i}", [128, DM]) for i in range(2)]
        t_sf = [sbuf6(f"f2_s{i}", [128, DM], BF16) for i in range(2)]
        t_of = [sbuf6(f"f2_o{i}", [128, DM]) for i in range(2)]
        t_gtmp = sbuf6("f2_gt", [128, DM])
        t_rb2 = sbuf6("f2_rb2", [128, DM])
        t_sc2 = sbuf6("f2_sc2", [128, DM])
        t_bb2 = sbuf6("f2_bb2", [128, DM])
        t_b2c = sbuf6("f2_b2c", [128, KD])
        t_sw2c = sbuf6("sw2c", [128, KD])
        t_rw2c = sbuf6("f2_rw2c", [128, KD])
        t_sc2c = sbuf6("f2_sc2c", [128, KD])
        t_wcol = sbuf6("f2_wcol", [128, 1])
        t_onesq3 = sbuf6("f2_ones", [128, 128])
        bqw2 = Buf()
        bw2s = [Buf(), Buf()]
        bhq = [Buf(), Buf()]
        bqh = [Buf(), Buf()]
        bgp = [Buf(), Buf()]
        bxf = [Buf(), Buf()]
        bsf = [Buf(), Buf()]
        bof = [Buf(), Buf()]
        bgtmp, brb2, bsc2, bbb2 = Buf(), Buf(), Buf(), Buf()
        bgpsp_tc = [Buf() for _ in range(NCH)]
        bout = Buf()
        bsw2c, bwcol = Buf(), Buf()
        bb2c, brw2c, bsc2c, bonesq3 = Buf(), Buf(), Buf(), Buf()

        P.emit("vector", lambda e: e.memset(t_sw2c[:], 0.0), writes=[bsw2c])
        for r in range(KD):
            for part in range(4):
                wb, bwb = t_hq[part % 2], bhq[part % 2]
                P.dma("scalar", wb[:],
                      d_w2[r * 128:(r + 1) * 128, part * 2048:(part + 1) * 2048],
                      writes=[bwb])
                P.emit("vector", lambda e, wb=wb: e.reduce_max(
                    t_wcol[:], wb[:], AX.X, apply_absolute_value=True),
                    reads=[bwb], writes=[bwcol])
                P.emit("vector", lambda e, r=r: e.tensor_max(
                    t_sw2c[:, r:r + 1], t_sw2c[:, r:r + 1], t_wcol[:]),
                    reads=[bwcol], writes=[bsw2c])
        P.emit("vector", lambda e: e.tensor_scalar(
            t_sw2c[:], t_sw2c[:], 1.0 / 127.0, 1e-8, ALU.mult, ALU.max),
            writes=[bsw2c])
        P.emit("vector", lambda e: e.memset(t_onesq3[:], 1.0), writes=[bonesq3])
        P.dma("sync", t_b2c[:], d_b2[:], writes=[bb2c])
        P.emit("vector", lambda e: e.reciprocal(t_rw2c[:], t_sw2c[:]),
               reads=[bsw2c], writes=[brw2c])
        P.emit("vector", lambda e: e.tensor_scalar(
            t_sc2c[:], t_sw2c[:], t_scal[:, 3:4], None, ALU.mult),
            reads=[bsw2c, gb("scal")], writes=[bsc2c])
        for (colap, bcol, tile, btile) in (
                (t_rw2c, brw2c, t_rb2, brb2),
                (t_sc2c, bsc2c, t_sc2, bsc2),
                (t_b2c, bb2c, t_bb2, bbb2)):
            for ci in range(KD):
                bank = 4 + ci % 2
                P.emit("vector", lambda e, colap=colap, ci=ci: e.tensor_scalar(
                    t_gtmp[:, 0:128], t_onesq3[:], colap[:, ci:ci + 1], None,
                    ALU.mult),
                    reads=[bcol, bonesq3], writes=[bgtmp])
                P.emit("tensor", lambda e, bank=bank: e.matmul(
                    pb[bank][:, 0:128], t_gtmp[:, 0:128], t_eye_f[:],
                    start=True, stop=True),
                    reads=[bgtmp, gb("eye_f")],
                    deps=[bB[bank].w] + bB[bank].r, writes=[bB[bank]])
                P.emit("vector", lambda e, tile=tile, ci=ci, bank=bank: e.tensor_copy(
                    out=tile[:, ci * 128:(ci + 1) * 128], in_=pb[bank][:, 0:128]),
                    reads=[bB[bank]], writes=[btile])

        if "dbg_sc2" in d_dbg:
            P.dma("sync", d_dbg["dbg_sc2"][:], t_sc2[:], reads=[bsc2])
            P.dma("sync", d_dbg["dbg_bb2"][:], t_bb2[:], reads=[bbb2])
            P.dma("sync", d_dbg["dbg_sw2c"][:], t_sw2c[:], reads=[bsw2c])

        for Iq in range(4):
            for k in range(16):
                wb, bwb = t_w2s[k % 2], bw2s[k % 2]
                P.dma("gpsimd", wb[:],
                      d_w2T[Iq * 2048 + k * 128: Iq * 2048 + (k + 1) * 128, :],
                      writes=[bwb])
                P.emit("vector", lambda e, wb=wb: e.tensor_mul(
                    t_gtmp[:], wb[:], t_rb2[:]),
                    reads=[bwb, brb2], writes=[bgtmp])
                P.emit("vector", lambda e, k=k: e.tensor_scalar(
                    t_qw2[:, k * DM:(k + 1) * DM], t_gtmp[:], MAGIC, MAGIC,
                    ALU.add, ALU.subtract),
                    reads=[bgtmp], writes=[bqw2])
            for tc in range(NCH):
                par = tc % 2
                hq, bhq_ = t_hq[par], bhq[par]
                qh, bqh_ = t_qh[par], bqh[par]
                P.dma("sync", hq[:].rearrange("p (c t) -> p c t", t=128),
                      d_h.rearrange("(c p) t -> p c t", p=128)[
                          :, Iq * 16:(Iq + 1) * 16, tc * 128:(tc + 1) * 128],
                      reads=[bhspill], writes=[bhq_])
                P.emit("vector", lambda e, hq=hq: e.tensor_scalar(
                    t_gtmp[:], hq[:], t_scal[:, 2:3], MAGIC, ALU.mult, ALU.add),
                    reads=[bhq_, gb("scal")], writes=[bgtmp])
                P.emit("vector", lambda e, qh=qh: e.tensor_scalar(
                    qh[:], t_gtmp[:], MAGIC, None, ALU.subtract),
                    reads=[bgtmp], writes=[bqh_])
                for k in range(16):
                    for q in range(4):
                        P.emit("tensor", lambda e, qh=qh, k=k, q=q: e.matmul(
                            pb[q][:, 0:512],
                            qh[:, k * 128:(k + 1) * 128],
                            t_qw2[:, k * DM + q * 512: k * DM + (q + 1) * 512],
                            start=(k == 0), stop=(k == 15)),
                            reads=[bqh_, bqw2] if k in (0, 15) else (),
                            writes=[bB[q]] if k == 15 else (),
                            deps=([bB[q].w] + bB[q].r) if k == 0 else (),
                            key="pe" if k == 15 else None)
                gp, bgp_ = t_gp[par], bgp[par]
                if Iq < 3:
                    for q in range(4):
                        if q % 2 == 0:
                            P.emit("vector", lambda e, gp=gp, q=q: e.tensor_copy(
                                out=gp[:, q * 512:(q + 1) * 512], in_=pb[q][:, 0:512]),
                                reads=[bB[q]], writes=[bgp_])
                        else:
                            P.emit("scalar", lambda e, gp=gp, q=q: e.copy(
                                out=gp[:, q * 512:(q + 1) * 512], in_=pb[q][:, 0:512]),
                                reads=[bB[q]], writes=[bgp_])
                    if Iq == 0 or not use_accum_dma:
                        if Iq > 0:
                            P.dma("sync", t_gtmp[:], d_gp[tc * 128:(tc + 1) * 128, :],
                                  reads=[bgpsp_tc[tc]], writes=[bgtmp])
                            P.emit("vector", lambda e, gp=gp: e.tensor_add(
                                gp[:], gp[:], t_gtmp[:]),
                                reads=[bgtmp], writes=[bgp_])
                        P.dma("sync", d_gp[tc * 128:(tc + 1) * 128, :], gp[:],
                              reads=[bgp_], writes=[bgpsp_tc[tc]])
                    else:
                        P.dma("gpsimd", d_gp[tc * 128:(tc + 1) * 128, :], gp[:],
                              reads=[bgp_], writes=[bgpsp_tc[tc]], accum=ALU.add)
                else:
                    xf, bxf_ = t_xf[par], bxf[par]
                    sf, bsf_ = t_sf[par], bsf[par]
                    of, bof_ = t_of[par], bof[par]
                    P.dma("sync", gp[:], d_gp[tc * 128:(tc + 1) * 128, :],
                          reads=[bgpsp_tc[tc]], writes=[bgp_])
                    P.dma("scalar", xf[:], d_x[PFX + tc * 128: PFX + (tc + 1) * 128, :],
                          writes=[bxf_])
                    P.dma("scalar", sf[:], d_s[tc * 128:(tc + 1) * 128, :],
                          reads=[bssp], writes=[bsf_])
                    for q in range(4):
                        P.emit("vector", lambda e, gp=gp, q=q: e.tensor_tensor(
                            gp[:, q * 512:(q + 1) * 512],
                            gp[:, q * 512:(q + 1) * 512], pb[q][:, 0:512], ALU.add),
                            reads=[bB[q]], writes=[bgp_])
                    P.emit("vector", lambda e, gp=gp: e.tensor_mul(
                        t_gtmp[:], gp[:], t_sc2[:]),
                        reads=[bgp_, bsc2], writes=[bgtmp])
                    P.emit("vector", lambda e: e.tensor_add(
                        t_gtmp[:], t_gtmp[:], t_bb2[:]),
                        reads=[bbb2], writes=[bgtmp])
                    if "dbg_acc" in d_dbg:
                        P.dma("gpsimd", d_dbg["dbg_acc"][tc * 128:(tc + 1) * 128, :],
                              gp[:], reads=[bgp_])
                    P.emit("scalar", lambda e: e.activation(
                        t_gtmp[:], t_gtmp[:], ACT.Silu),
                        writes=[bgtmp])
                    if "dbg_g" in d_dbg:
                        P.dma("gpsimd", d_dbg["dbg_g"][tc * 128:(tc + 1) * 128, :],
                              t_gtmp[:], reads=[bgtmp])
                    P.emit("vector", lambda e, sf=sf: e.tensor_mul(
                        t_gtmp[:], t_gtmp[:], sf[:]),
                        reads=[bsf_], writes=[bgtmp])
                    P.emit("vector", lambda e, xf=xf, of=of: e.tensor_add(
                        of[:], t_gtmp[:], xf[:]),
                        reads=[bgtmp, bxf_], writes=[bof_])
                    P.dma("sync", d_out[tc * 128:(tc + 1) * 128, :], of[:],
                          reads=[bof_], writes=[bout])

        s6.close()
        _final_wait(P)
        P.finalize(ctx.enter_context(nc.Block()))

    return nc, d_dbg


def _final_wait(P):
    deps = [P.tok(k) for k in ("pe", "dve", "act", "pool")]
    for qk in ("qsp", "qact", "qpool"):
        for i in range(P.NLANES):
            deps.append(P.tok(f"{qk}{i}"))
    P.emit("sync", lambda e: e.nop(), deps=deps, key=None)


def _stage(inputs):
    x = np.ascontiguousarray(np.asarray(inputs["L0_attn_x"], dtype=np.float32)
                             .reshape(B * L, DM))
    W_in = np.asarray(inputs["L0_attn_W_in"], dtype=np.float32)
    conv_w = np.ascontiguousarray(
        np.asarray(inputs["L0_attn_conv_w"], dtype=np.float32)
        .reshape(32, 128, 4).transpose(1, 0, 2).reshape(128, 32 * 4))
    conv_b = np.ascontiguousarray(
        np.asarray(inputs["L0_attn_conv_b"], dtype=np.float32)
        .reshape(32, 128).T)
    A_log = np.ascontiguousarray(
        np.asarray(inputs["L0_attn_A_log"], dtype=np.float32).reshape(NH, 1))
    Dp = np.ascontiguousarray(
        np.asarray(inputs["L0_attn_D_param"], dtype=np.float32).reshape(32, 128).T)
    dtb = np.ascontiguousarray(
        np.asarray(inputs["L0_attn_dt_bias"], dtype=np.float32).reshape(NH, 1))
    W_out = np.asarray(inputs["L0_attn_W_out"], dtype=np.float32)
    w1 = np.ascontiguousarray(np.asarray(inputs["L0_ffn_w1"], dtype=np.float32))
    b1 = np.ascontiguousarray(
        np.asarray(inputs["L0_ffn_b1"], dtype=np.float32).reshape(KI, 128).T)
    w2 = np.ascontiguousarray(np.asarray(inputs["L0_ffn_w2"], dtype=np.float32))
    b2 = np.ascontiguousarray(
        np.asarray(inputs["L0_ffn_b2"], dtype=np.float32).reshape(KD, 128).T)

    WinT = np.ascontiguousarray(W_in.T)
    WoutT = np.ascontiguousarray(W_out.T)
    w1T = np.ascontiguousarray(w1.T)
    w2T = np.ascontiguousarray(w2.T)

    in_maps = []
    for k in range(NCORES):
        t0 = k * TOK
        xs = np.zeros((TOKA, DM), dtype=np.float32)
        if k % 2 == 1:
            xs[0:PFX] = x[t0 - PFX:t0]
        xs[PFX:] = x[t0:t0 + TOK]
        in_maps.append({
            "x": np.ascontiguousarray(xs),
            "WinT": WinT, "convw": conv_w, "convb": conv_b, "Alog": A_log,
            "Dparam": Dp, "dtb": dtb, "WoutT": WoutT,
            "w1T": w1T, "w1n": w1, "w2T": w2T, "w2n": w2,
            "b1": b1, "b2": b2,
            "isodd": np.array([[1.0 if k % 2 == 1 else 0.0]], dtype=np.float32),
        })
    return in_maps


def kernel(**inputs):
    nc, _ = build()
    in_maps = _stage(inputs)
    res = run_bass_kernel_spmd(nc, in_maps, core_ids=list(range(NCORES)))
    out = np.concatenate([res.results[k]["out"] for k in range(NCORES)], axis=0)
    return out.reshape(B, L, DM)
